# version 10
# baseline (speedup 1.0000x reference)
import numpy as np
from contextlib import ExitStack

import concourse.bass as bass
import concourse.mybir as mybir
from concourse.bass_utils import run_bass_kernel_spmd

F32 = mybir.dt.float32
F16 = mybir.dt.float16
AF = mybir.ActivationFunctionType
ALU = mybir.AluOpType
AX = mybir.AxisListType

H = 4096
L = 8192
NCORE = 8
T = L // NCORE
TH = T + 3
PROJ = 8512
DS = 4096
CONV = 4352
NH = 64
HD = 64
NST = 128
Q = 128
NCH = 8
HID = 10928
H8 = 1366
D8 = 512
EPS = 1e-5

_TRIPLES = [(1, 2, 3), (1, 4, 5), (1, 7, 6), (2, 4, 6), (2, 5, 7), (3, 4, 7), (3, 6, 5)]

def _oct_tables():
    S = np.zeros((8, 8, 8), np.float32)
    S[0, 0, 0] = 1.0
    for i in range(1, 8):
        S[0, i, i] = 1.0
        S[i, 0, i] = 1.0
        S[i, i, 0] = -1.0
    for a, b, c in _TRIPLES:
        for x, y, z in ((a, b, c), (b, c, a), (c, a, b)):
            S[x, y, z] = 1.0
            S[y, x, z] = -1.0
    jtab = np.zeros((8, 8), np.int64)
    stab = np.zeros((8, 8), np.float32)
    for i in range(8):
        for k in range(8):
            js = np.nonzero(S[:, i, k])[0]
            assert len(js) == 1
            jtab[i, k] = int(js[0])
            stab[i, k] = S[js[0], i, k]
    return jtab, stab

JTAB, STAB = _oct_tables()
OCH = [(oi * 128, min(128, H8 - oi * 128)) for oi in range(11)]
GU_POS = [[kc for kc in range(32) if STAB[kc // 4, kg] > 0] for kg in range(8)]
GU_NEG = [[kc for kc in range(32) if STAB[kc // 4, kg] < 0] for kg in range(8)]
D_POS = [[i for i in range(8) if STAB[i, kg] > 0] for kg in range(8)]
D_NEG = [[i for i in range(8) if STAB[i, kg] < 0] for kg in range(8)]
for _kg in range(8):
    assert GU_POS[_kg] and GU_NEG[_kg] and D_POS[_kg] and D_NEG[_kg]
NOI = len(OCH)
N_WG = 8 * H8 * D8


class Sem:
    def __init__(self, ctx, nc, name):
        self.h = ctx.enter_context(nc.semaphore(name))
        self.n = 0
        self.base = 0
    def inc(self, inst, k=1):
        inst.then_inc(self.h, k)
        self.n += k
        return self.n
    def dma(self, inst):
        return self.inc(inst, 16)


def build(debug_taps=(), cc_group=8):
    taps = set(debug_taps)
    nc = bass.Bass("TRN2", target_bir_lowering=False)

    xT = nc.dram_tensor("xT", [H, TH], F32, kind="ExternalInput").ap()
    w_in = nc.dram_tensor("w_in", [H, PROJ], F32, kind="ExternalInput").ap()
    w_out = nc.dram_tensor("w_out", [DS, H], F32, kind="ExternalInput").ap()
    wg_t = nc.dram_tensor("wg_t", [8, D8, H8], F32, kind="ExternalInput").ap()
    wu_t = nc.dram_tensor("wu_t", [8, D8, H8], F32, kind="ExternalInput").ap()
    wd_t = nc.dram_tensor("wd_t", [8, H8, D8], F32, kind="ExternalInput").ap()
    convw = nc.dram_tensor("convw", [CONV, 4], F32, kind="ExternalInput").ap()
    ln_w = nc.dram_tensor("ln_w", [H], F32, kind="ExternalInput").ap()
    norm_w = nc.dram_tensor("norm_w", [DS], F32, kind="ExternalInput").ap()
    mup = nc.dram_tensor("mup", [PROJ], F32, kind="ExternalInput").ap()
    a_log = nc.dram_tensor("a_log", [NH], F32, kind="ExternalInput").ap()
    dt_bias = nc.dram_tensor("dt_bias", [NH], F32, kind="ExternalInput").ap()
    d_vec = nc.dram_tensor("d_vec", [NH], F32, kind="ExternalInput").ap()
    maskv = nc.dram_tensor("maskv", [8], F32, kind="ExternalInput").ap()
    outT = nc.dram_tensor("outT", [H, T], F32, kind="ExternalOutput").ap()

    z_hbm = nc.dram_tensor("z_hbm", [DS, T], F16)
    xn_hbm = nc.dram_tensor("xn_hbm", [H, T], F16)
    yn_hbm = nc.dram_tensor("yn_hbm", [DS, T], F16)
    g_hbm = nc.dram_tensor("g_hbm", [H, T], F16)
    qg_hbm = nc.dram_tensor("qg_hbm", [8, D8, H8], F16)
    qu_hbm = nc.dram_tensor("qu_hbm", [8, D8, H8], F16)
    qd_hbm = nc.dram_tensor("qd_hbm", [8, H8, D8], F16)
    lcrow = nc.dram_tensor("lcrow", [NCH, NH * Q], F32)
    cq_dram = nc.dram_tensor("cq_dram", [NCH, NH], F32)
    rowbc = nc.dram_tensor("rowbc", [16, 1088], F32)
    ccb_in = nc.dram_tensor("ccb_in", [NST, NH * HD], F16)
    ccb_out = nc.dram_tensor("ccb_out", [NCORE * NST, NH * HD], F16)
    cclc_in = nc.dram_tensor("cclc_in", [1, NH], F32)
    cclc_out = nc.dram_tensor("cclc_out", [NCORE, NH], F32)

    dbg = {}
    def tap(name, shape, dtype=F32):
        if name in taps:
            dbg[name] = nc.dram_tensor("dbg_" + name, shape, dtype, kind="ExternalOutput").ap()
            return dbg[name]
        return None

    ctx = ExitStack()

    ARENA = 209000
    SBUF_BASE = nc.bump_sbuf(ARENA)[0]
    C0, A0, B0 = SBUF_BASE, SBUF_BASE + 65728, SBUF_BASE + 135584
    P0OFF = SBUF_BASE + 205216
    def sb(name, shape, dtype, off):
        return nc.alloc_sbuf_tensor_at(name, shape, dtype, offset=off)

    _bump = [P0OFF]
    def sbp(name, shape, dtype):
        elems = 1
        for s in shape[1:]:
            elems *= s
        nbytes = elems * (2 if dtype == F16 else 4)
        off = (_bump[0] + 31) // 32 * 32
        _bump[0] = off + nbytes
        assert _bump[0] <= SBUF_BASE + ARENA, (name, _bump[0])
        return sb(name, shape, dtype, off)

    ident16 = sbp("ident16", [128, 128], F16)
    ident32 = sbp("ident32", [128, 128], F32)
    ones16_col = sbp("ones16_col", [128, 1], F16)
    ones32_col = sbp("ones32_col", [128, 1], F32)
    onesQ = sbp("onesQ", [NH, Q], F32)
    mup_sb = sbp("mup_sb", [128, 67], F32)
    lnw_sb = sbp("lnw_sb", [128, 32], F32)
    nw_sb = sbp("nw_sb", [128, 32], F32)
    cw_sb = sbp("cw_sb", [128, 34 * 4], F32)
    acol = sbp("acol", [NH, 1], F32)
    dtb_col = sbp("dtb_col", [NH, 1], F32)
    dall = sbp("dall", [128, 64], F32)
    db_pair = sbp("db_pair", [128, 32], F32)
    mask_sb = sbp("mask_sb", [8, 1], F32)
    lcq_acc = sbp("lcq_acc", [NH, 1], F32)
    sc_g = sbp("sc_g", [128, 1], F32)
    sc_u = sbp("sc_u", [128, 1], F32)
    sc_d = sbp("sc_d", [128, 1], F32)
    thr_g = sbp("thr_g", [128, 2], F32)
    thr_u = sbp("thr_u", [128, 2], F32)
    thr_d = sbp("thr_d", [128, 2], F32)
    facc = sbp("facc", [128, 3], F32)
    tsums = sbp("tsums", [128, 33], F32)
    tri8 = sbp("tri8", [8, 8], F32)
    ones8 = sbp("ones8", [1, 8], F32)
    tcol = sbp("tcol", [128, 1], F32)
    scmeans = sbp("scmeans", [128, 3], F32)
    constt = sbp("constt", [1, 16], F32)
    _consts = {}
    def reg_const(v):
        if v not in _consts:
            i = len(_consts)
            assert i < 16
            ap = constt[0:1, i:i + 1]
            nc.const_aps.register(F32, float(v), ap)
            _consts[v] = (i, float(v))
        return _consts[v]
    for _v in (EPS, 1.0 / H, 1.0 / DS, 1.0 / N_WG):
        reg_const(_v)

    xr = sb("xr", [128, 32 * TH], F16, C0)
    dt_sb = sb("dt_sb", [NH, T], F32, C0 + 0)
    lt_sb = sb("lt_sb", [NH, T], F32, C0 + 4128)
    lc_sb = sb("lc_sb", [NH, T], F32, C0 + 8256)
    h_all = sb("h_all", [128, NH * HD], F16, C0 + 12384)
    _cb = [C0 + 20576]
    _coff = {}
    def sbc(name, shape, dtype):
        elems = 1
        for s in shape[1:]:
            elems *= s
        nbytes = elems * (2 if dtype == F16 else 4)
        off = (_cb[0] + 31) // 32 * 32
        _cb[0] = off + nbytes
        _coff[name] = off
        assert _cb[0] <= C0 + 65728, (name, _cb[0])
        return sb(name, shape, dtype, off)

    dtT = sbc("dtT", [128, NH], F32)
    lcTn = sbc("lcTn", [128, NH], F32)
    wT = sbc("wT", [128, NH], F32)
    cumq = sbc("cumq", [128, NH], F32)
    btok = sbc("btok", [128, 128], F16)
    sbct = sbc("sbct", [128, 128], F16)
    wsc_t = sbc("wsc_t", [NH, 128], F32)
    combT = sbc("combT", [128, NH], F32)
    xsw = sbc("xsw", [128, 4 * 64], F16)
    xst = sbc("xst", [128, 4096], F16)
    lcb = sbc("lcb", [128, 4096], F32)
    mbig = sbc("mbig", [128, 4096], F16)
    cwt = sbc("cwt", [128, 4096], F16)
    csc = [sb(f"csc{i}", [128, T], F32, _coff["xst"] + i * 4128) for i in range(8)]
    hacc32 = sb("hacc32", [128, 4096], F32, _coff["lcb"])
    mfin = sb("mfin", [128, 4096], F16, _coff["lcb"])
    wful = sb("wful", [128, 8 * 64], F32, _coff["xst"])
    lcall = sb("lcall", [8, 64], F32, _coff["xst"] + 2048)
    lcq8 = sb("lcq8", [8, 64], F32, _coff["xst"] + 2336)
    lcp = sb("lcp", [1, 64], F32, _coff["xst"] + 2624)
    wcore = sb("wcore", [8, 64], F32, _coff["xst"] + 2912)
    bslice = sb("bslice", [128, 2 * 4096], F16, _coff["mbig"])
    zst = sb("zst", [128, 2 * T], F16, C0 + 0)
    rstd2_bc = sb("rstd2_bc", [128, T], F32, C0 + 4128)
    ynstg = sb("ynstg", [128, 2 * T], F16, C0 + 8352)
    row_d = sb("row_d", [1, 1088], F32, C0 + 12576)
    row_e = sb("row_e", [1, 1088], F32, C0 + 16928)
    sgl = sb("sgl", [128, 2 * T], F32, C0 + 21312)
    sqb = sb("sqb", [128, 2 * T], F16, C0 + 29536)
    qwin_g = [sb("qwin_g0", [128, 4096], F16, C0 + 0),
              sb("qwin_g1", [128, 4096], F16, C0 + 16384)]
    qwin_u = [sb("qwin_u0", [128, 4096], F16, C0 + 8192),
              sb("qwin_u1", [128, 4096], F16, C0 + 24576)]
    qwin_d = [sb("qwin_d0", [128, 11264], F16, C0 + 32768),
              sb("qwin_d1", [128, 11264], F16, A0 + 45056)]
    gstg = sb("gstg", [128, 2 * 256], F16, C0 + 55296)
    fevac = sb("fevac", [128, 6 * 256], F32, C0 + 56352)
    yn = sb("yn", [128, 32 * T], F16, C0 + 0)
    ph32 = sb("ph32", [128, T], F32, B0 + 0)

    xbc = sb("xbc", [128, 34 * TH], F16, A0)
    wstream = sb("wstream", [128, 4 * 4096], F32, A0)
    yfeat = sb("yfeat", [128, 32 * T], F16, A0)
    h_q = sb("h_q", [128, 88 * 256], F16, A0)
    wslab6 = sb("wslab6", [128, 3 * 4096], F16, A0)
    xin6 = sb("xin6", [128, 2 * T], F32, A0 + 24576)
    gbuf = sb("gbuf", [128, 2 * T], F16, A0 + 32768)
    stg32 = sb("stg32", [128, 2 * T], F32, A0 + 36864)

    xin = sb("xin", [128, 2 * TH], F32, B0)
    f16a = sb("f16a", [128, 2 * TH], F16, B0 + 16448)
    f16b = sb("f16b", [128, 2 * TH], F16, B0 + 20576)
    f32a = sb("f32a", [128, TH], F32, B0 + 24704)
    mean_bc = sb("mean_bc", [128, TH], F32, B0 + 28832)
    rstd_bc = sb("rstd_bc", [128, TH], F32, B0 + 32960)
    stg = sb("stg", [128, 2 * T], F16, B0 + 37088)
    row_a = sb("row_a", [1, 1088], F32, B0 + 41184)
    row_b = sb("row_b", [1, 1088], F32, B0 + 45536)
    row_c = sb("row_c", [1, 1088], F32, B0 + 49888)
    qin = sb("qin", [128, 2 * 2048], F32, B0)
    qpos = sb("qpos", [128, 2048], F16, B0 + 16384)
    qneg = sb("qneg", [128, 2048], F16, B0 + 20480)
    qout = sb("qout", [128, 2 * 2048], F16, B0 + 24576)
    wslab = sb("wslab", [128, 3 * 4096], F16, B0 + 32768)
    zstg1 = sb("zstg1", [128, 2 * T], F16, B0 + 57344)
    dt_tmp = sb("dt_tmp", [NH, T], F32, B0 + 61440)
    convo = sb("convo", [128, 34 * T], F16, B0)
    yg = sb("yg", [128, 32 * T], F16, B0)
    xn_sb = sb("xn_sb", [128, 32 * T], F16, B0)

    sems = {}
    free_pool = []
    build._alloc_log = []
    def S(name, fresh=False):
        if name not in sems:
            if free_pool and not fresh:
                old = free_pool.pop()
                s = Sem.__new__(Sem)
                s.h = old.h
                s.n = old.n
                s.base = old.n
                sems[name] = s
                build._alloc_log.append((name, getattr(old.h, "name", "?"), old.n))
            else:
                sems[name] = Sem(ctx, nc, name)
                build._alloc_log.append((name, "FRESH", 0))
        return sems[name]

    def sfree(*names):
        for nm in names:
            if nm in sems:
                free_pool.append(sems.pop(nm))

    s_bar = Sem(ctx, nc, "s_bar")

    class GatedBlock:
        def __getattr__(self, eng):
            dec = getattr(block, eng)
            def wrap(f):
                def g(e):
                    if s_bar.n:
                        e.wait_ge(s_bar.h, s_bar.n)
                    return f(e)
                return dec(g)
            return wrap
    B = GatedBlock()

    def quiesce_and_free(*names):
        @block.sync
        def _(e):
            if s_bar.n:
                e.wait_ge(s_bar.h, s_bar.n)
            for nm in names:
                if nm in sems:
                    e.wait_ge(sems[nm].h, sems[nm].n)
            s_bar.inc(e.dma_start(out=rowbc.ap()[15:16, 0:4], in_=xT[0:1, 0:4]), 16)
        sfree(*names)

    def fence(e, ename, inst):
        s = S("self_" + ename)
        s.inc(inst)
        e.wait_ge(s.h, s.n)

    class Stream:
        def __init__(self, name, nbufs, step=16, fresh=False):
            self.sems = [S(f"{name}_{i}", fresh=fresh) for i in range(nbufs)]
            self.nb = nbufs
            self.step = step
        def put(self, i, inst):
            self.sems[i % self.nb].inc(inst, self.step)
        def wait(self, e, i):
            s = self.sems[i % self.nb]
            e.wait_ge(s.h, s.base + self.step * (i // self.nb + 1))

    block = ctx.enter_context(nc.Block())

    s_const = S("s_const")
    s_cmem = S("s_cmem")
    s_cder = S("s_cder")

    @B.scalar
    def _(e):
        for c in range(66):
            s_const.dma(e.dma_start(out=mup_sb[:, c:c + 1], in_=mup[c * 128:(c + 1) * 128][:, None]))
        s_const.dma(e.dma_start(out=mup_sb[0:64, 66:67], in_=mup[8448:8512][:, None]))
        for c in range(32):
            s_const.dma(e.dma_start(out=lnw_sb[:, c:c + 1], in_=ln_w[c * 128:(c + 1) * 128][:, None]))
            s_const.dma(e.dma_start(out=nw_sb[:, c:c + 1], in_=norm_w[c * 128:(c + 1) * 128][:, None]))
        for c in range(34):
            s_const.dma(e.dma_start(out=cw_sb[:, c * 4:(c + 1) * 4], in_=convw[c * 128:(c + 1) * 128, :]))
        s_const.dma(e.dma_start(out=acol[:, 0:1], in_=a_log[:, None]))
        s_const.dma(e.dma_start(out=dtb_col[:, 0:1], in_=dt_bias[:, None]))
        s_const.dma(e.dma_start(out=mask_sb[:, 0:1], in_=maskv[:, None]))
        s_const.dma(e.dma_start(out=rowbc.ap()[0:1, 0:64], in_=d_vec[None, :]))
        e.wait_ge(s_const.h, s_const.n)
        s_const.dma(e.dma_start(out=dall[:, :], in_=rowbc.ap()[0, 0:64][None, :].partition_broadcast(128)))

    @B.gpsimd
    def _(e):
        fence(e, "pl", e.memset(ident16[:], 1.0))
        e.affine_select(out=ident16[:], in_=ident16[:], pattern=[[1, 128]],
                        compare_op=ALU.is_equal, fill=0.0, base=0, channel_multiplier=-1)
        fence(e, "pl", e.memset(ident32[:], 1.0))
        e.affine_select(out=ident32[:], in_=ident32[:], pattern=[[1, 128]],
                        compare_op=ALU.is_equal, fill=0.0, base=0, channel_multiplier=-1)
        e.memset(ones16_col[:], 1.0)
        e.memset(ones32_col[:], 1.0)
        e.memset(ones8[:], 1.0)
        fence(e, "pl", e.memset(tri8[:], 1.0))
        e.affine_select(out=tri8[:], in_=tri8[:], pattern=[[1, 8]],
                        compare_op=ALU.is_ge, fill=0.0, base=0, channel_multiplier=-1)
        e.memset(onesQ[:], 1.0)
        e.memset(facc[:], 0.0)
        e.memset(lcq_acc[:], 0.0)
        inst = None
        for _v, (_i, _fv) in sorted(_consts.items(), key=lambda kv: kv[1][0]):
            inst = e.memset(constt[0:1, _i:_i + 1], _fv)
        s_cmem.inc(inst)

    @B.scalar
    def _(e):
        fence(e, "act", e.activation(out=acol[:, 0:1], in_=acol[:, 0:1], func=AF.Exp))
        inst = e.mul(out=acol[:, 0:1], in_=acol[:, 0:1], mul=-1.0)
        s_cder.inc(inst)

    s_cder2 = S("s_cder2")

    @B.vector
    def _(e):
        e.wait_ge(s_const.h, s_const.n)
        e.tensor_copy(out=db_pair[0:64, :], in_=dall[0:64, 0:64:2])
        inst = e.tensor_copy(out=db_pair[64:128, :], in_=dall[64:128, 1:64:2])
        s_cder2.inc(inst)

    xin_st = Stream("xin_st", 2)
    s_xpre = S("s_xpre")
    s_mmp = S("s_mmp")
    s_rows = S("s_rows")
    s_bc = S("s_bc")
    xin2_st = Stream("xin2_st", 2)
    s_xnv = S("s_xnv")
    s_xns = S("s_xns")
    xnh_st = Stream("xnh_st", 2)
    s_rmsp = S("s_rmsp")
    s_rms = S("s_rms")
    s_xr = S("s_xr")

    with nc.psum_tensor("ps_sum", [1, TH], F32) as ps_sum, \
         nc.psum_tensor("ps_sq", [1, TH], F32) as ps_sq:

        @B.sync
        def _(e):
            for c in range(32):
                if c >= 2:
                    e.wait_ge(s_xpre.h, s_xpre.base + (2 * (c - 1)))
                xin_st.put(c, e.dma_start(out=xin[:, (c % 2) * TH:(c % 2) * TH + TH],
                                          in_=xT[c * 128:(c + 1) * 128, :]))

        @B.scalar
        def _(e):
            for c in range(32):
                xin_st.wait(e, c)
                if c >= 2:
                    e.wait_ge(s_mmp.h, s_mmp.base + (c - 1))
                xc = xin[:, (c % 2) * TH:(c % 2) * TH + TH]
                fa = f16a[:, (c % 2) * TH:(c % 2) * TH + TH]
                fb = f16b[:, (c % 2) * TH:(c % 2) * TH + TH]
                s_xpre.inc(e.activation(out=fa, in_=xc, func=AF.Copy))
                s_xpre.inc(e.activation(out=fb, in_=xc, func=AF.Square))

        @B.tensor
        def _(e):
            e.wait_ge(s_cmem.h, s_cmem.base + (1))
            for c in range(32):
                e.wait_ge(s_xpre.h, s_xpre.base + (2 * (c + 1)))
                fa = f16a[:, (c % 2) * TH:(c % 2) * TH + TH]
                fb = f16b[:, (c % 2) * TH:(c % 2) * TH + TH]
                for (lo, n) in ((0, 512), (512, 512), (1024, 3)):
                    e.matmul(ps_sum[:, lo:lo + n], ones16_col[:], fa[:, lo:lo + n],
                             start=(c == 0), stop=(c == 31))
                inst = None
                for (lo, n) in ((0, 512), (512, 512), (1024, 3)):
                    inst = e.matmul(ps_sq[:, lo:lo + n], ones16_col[:], fb[:, lo:lo + n],
                                    start=(c == 0), stop=(c == 31))
                s_mmp.inc(inst)

        @B.vector
        def _(e):
            e.wait_ge(s_mmp.h, s_mmp.base + (32))
            e.tensor_scalar(out=row_a[:, 0:TH], in0=ps_sum[:, :], scalar1=1.0 / H,
                            scalar2=None, op0=ALU.mult)
            fence(e, "dve", e.tensor_scalar(out=row_b[:, 0:TH], in0=ps_sq[:, :], scalar1=1.0 / H,
                                            scalar2=None, op0=ALU.mult))
            fence(e, "dve", e.tensor_mul(out=row_c[:, 0:TH], in0=row_a[:, 0:TH], in1=row_a[:, 0:TH]))
            inst = e.tensor_sub(out=row_b[:, 0:TH], in0=row_b[:, 0:TH], in1=row_c[:, 0:TH])
            s_rows.inc(inst)

        @B.scalar
        def _(e):
            e.wait_ge(s_rows.h, s_rows.base + (1))
            e.wait_ge(s_cmem.h, s_cmem.base + (1))
            inst = e.activation(out=row_c[:, 0:TH], in_=row_b[:, 0:TH], func=AF.Sqrt, bias=EPS)
            s_rows.inc(inst)

        @B.vector
        def _(e):
            e.wait_ge(s_rows.h, s_rows.base + (2))
            inst = e.reciprocal(out=row_c[:, 0:TH], in_=row_c[:, 0:TH])
            s_rows.inc(inst)

        @B.sync
        def _(e):
            e.wait_ge(s_rows.h, s_rows.base + (3))
            s_bc.dma(e.dma_start(out=rowbc.ap()[1:2, 0:TH], in_=row_a[:, 0:TH]))
            s_bc.dma(e.dma_start(out=rowbc.ap()[2:3, 0:TH], in_=row_c[:, 0:TH]))
            e.wait_ge(s_bc.h, s_bc.base + (32))
            s_bc.dma(e.dma_start(out=mean_bc[:, :],
                                 in_=rowbc.ap()[1, 0:TH][None, :].partition_broadcast(128)))
            s_bc.dma(e.dma_start(out=rstd_bc[:, :],
                                 in_=rowbc.ap()[2, 0:TH][None, :].partition_broadcast(128)))

    with nc.psum_tensor("ps_rms", [1, TH], F32) as ps_rms:

        @B.sync
        def _(e):
            for c in range(32):
                if c >= 2:
                    e.wait_ge(s_xnv.h, s_xnv.base + (c - 1))
                xin2_st.put(c, e.dma_start(out=xin[:, (c % 2) * TH:(c % 2) * TH + TH],
                                           in_=xT[c * 128:(c + 1) * 128, :]))

        @B.vector
        def _(e):
            e.wait_ge(s_bc.h, s_bc.base + (64))
            for c in range(32):
                xin2_st.wait(e, c)
                if c >= 1:
                    e.wait_ge(s_xns.h, s_xns.base + (2 * c))
                xc = xin[:, (c % 2) * TH:(c % 2) * TH + TH]
                fa = f16a[:, (c % 2) * TH:(c % 2) * TH + TH]
                fence(e, "dve", e.tensor_sub(out=f32a[:], in0=xc, in1=mean_bc[:]))
                inst = e.tensor_mul(out=fa, in0=f32a[:], in1=rstd_bc[:])
                s_xnv.inc(inst)

        @B.scalar
        def _(e):
            for c in range(32):
                e.wait_ge(s_xnv.h, s_xnv.base + (c + 1))
                if c >= 2:
                    e.wait_ge(s_rmsp.h, s_rmsp.base + (c - 1))
                    xnh_st.wait(e, c - 2)
                fa = f16a[:, (c % 2) * TH:(c % 2) * TH + TH]
                fb = f16b[:, (c % 2) * TH:(c % 2) * TH + TH]
                s_xns.inc(e.activation(out=fb, in_=fa, func=AF.Square))
                fence(e, "act", e.activation(out=stg[:, (c % 2) * T:(c % 2) * T + T],
                                             in_=fa[:, 3:TH], func=AF.Copy))
                xnh_st.put(c, e.dma_start(out=xn_hbm.ap()[c * 128:(c + 1) * 128, :],
                                          in_=stg[:, (c % 2) * T:(c % 2) * T + T]))
                s_xns.inc(e.activation(out=xr[:, c * TH:(c + 1) * TH], in_=fa, func=AF.Copy))

        @B.tensor
        def _(e):
            e.wait_ge(s_cmem.h, s_cmem.base + (1))
            inst = None
            for c in range(32):
                e.wait_ge(s_xns.h, s_xns.base + (2 * c + 1))
                fb = f16b[:, (c % 2) * TH:(c % 2) * TH + TH]
                for (lo, n) in ((0, 512), (512, 512), (1024, 3)):
                    inst = e.matmul(ps_rms[:, lo:lo + n], ones16_col[:], fb[:, lo:lo + n],
                                    start=(c == 0), stop=(c == 31))
                s_rmsp.inc(inst)

        @B.scalar
        def _(e):
            e.wait_ge(s_rmsp.h, s_rmsp.base + (32))
            e.wait_ge(s_cmem.h, s_cmem.base + (1))
            inst = e.activation(out=row_a[:, 0:TH], in_=ps_rms[:, :], func=AF.Sqrt,
                                scale=1.0 / DS, bias=EPS)
            s_rms.inc(inst)

        @B.vector
        def _(e):
            e.wait_ge(s_rms.h, s_rms.base + (1))
            inst = e.reciprocal(out=row_a[:, 0:TH], in_=row_a[:, 0:TH])
            s_rms.inc(inst)

        @B.sync
        def _(e):
            e.wait_ge(s_rms.h, s_rms.base + (2))
            s_bc.dma(e.dma_start(out=rowbc.ap()[3:4, 0:TH], in_=row_a[:, 0:TH]))
            e.wait_ge(s_bc.h, s_bc.n)
            s_bc.dma(e.dma_start(out=rstd_bc[:, :],
                                 in_=rowbc.ap()[3, 0:TH][None, :].partition_broadcast(128)))

        @B.vector
        def _(e):
            e.wait_ge(s_bc.h, s_bc.n)
            e.wait_ge(s_xns.h, s_xns.base + (64))
            for c in range(32):
                if c >= 1:
                    e.wait_ge(s_xr.h, s_xr.base + (c))
                fence(e, "dve", e.tensor_mul(out=f32a[:], in0=xr[:, c * TH:(c + 1) * TH], in1=rstd_bc[:]))
                inst = e.tensor_scalar(out=xr[:, c * TH:(c + 1) * TH], in0=f32a[:],
                                       scalar1=lnw_sb[:, c:c + 1], scalar2=None, op0=ALU.mult)
                s_xr.inc(inst)


    wsc_st = Stream("wsc_st", 2)
    s_wred = S("s_wred")
    s_thr = S("s_thr")

    WTILES = []
    for pj in range(3):
        for t0 in range(11):
            lo = t0 * 4096
            wdt = min(4096, 43712 - lo)
            WTILES.append((pj, lo, wdt))

    s_scbc = S("s_scbc")

    @B.gpsimd
    def _(e):
        e.wait_ge(s_cmem.h, s_cmem.base + (1))
        srcs = [wg_t, wu_t, wd_t]
        for i, (pj, lo, wdt) in enumerate(WTILES):
            if i >= 4:
                e.wait_ge(s_wred.h, s_wred.base + (i - 3))
            flat = srcs[pj].rearrange("j a b -> (j a b)").rearrange("(p f) -> p f", p=128)
            wsc_st.put(i, e.dma_start(out=wstream[:, (i % 4) * 4096:(i % 4) * 4096 + wdt],
                                      in_=flat[:, lo:lo + wdt]))

    @B.scalar
    def _(e):
        e.wait_ge(s_cmem.h, s_cmem.base + (1))
        for i, (pj, lo, wdt) in enumerate(WTILES):
            wsc_st.wait(e, i)
            ws = wstream[:, (i % 4) * 4096:(i % 4) * 4096 + wdt]
            inst = e.activation(out=ws, in_=ws, func=AF.Abs, accum_out=tsums[:, i:i + 1])
            s_wred.inc(inst)
            e.wait_ge(s_wred.h, s_wred.n)

    @B.gpsimd
    def _(e):
        e.wait_ge(s_scbc.h, s_scbc.base + (32))
        inst = None
        for pj, (sc, th) in enumerate(((sc_g, thr_g), (sc_u, thr_u), (sc_d, thr_d))):
            e.tensor_copy(out=sc[:, 0:1], in_=scmeans[:, pj:pj + 1])
            e.tensor_scalar(out=th[:, 0:1], in0=scmeans[:, pj:pj + 1], scalar1=0.5,
                            scalar2=None, op0=ALU.mult)
            inst = e.tensor_scalar(out=th[:, 1:2], in0=scmeans[:, pj:pj + 1], scalar1=-0.5,
                                   scalar2=None, op0=ALU.mult)
        s_thr.inc(inst)

    with nc.psum_tensor("ps_wtot", [1, 33], F32) as ps_wtot:
        @B.tensor
        def _(e):
            e.wait_ge(s_wred.h, s_wred.base + (33))
            inst = None
            for pj in range(3):
                inst = e.matmul(ps_wtot[0:1, pj * 11:(pj + 1) * 11], ones32_col[:],
                                tsums[:, pj * 11:(pj + 1) * 11], start=True, stop=True)
            s_thr.inc(inst)

        @B.scalar
        def _(e):
            e.wait_ge(s_thr.h, s_thr.base + (1))
            inst = None
            for pj in range(3):
                inst = e.activation(out=row_c[0:1, pj * 16:pj * 16 + 11],
                                    in_=ps_wtot[0:1, pj * 11:(pj + 1) * 11],
                                    func=AF.Copy, scale=1.0 / N_WG,
                                    accum_out=row_b[0:1, pj:pj + 1])
            s_thr.inc(inst)

    @B.sync
    def _(e):
        e.wait_ge(s_thr.h, s_thr.base + (2))
        s_scbc.dma(e.dma_start(out=rowbc.ap()[5:6, 0:3], in_=row_b[0:1, 0:3]))
        e.wait_ge(s_scbc.h, s_scbc.base + (16))
        s_scbc.dma(e.dma_start(out=scmeans[:, :],
                               in_=rowbc.ap()[5, 0:3][None, :].partition_broadcast(128)))

    wst = Stream("wst", 3)
    zhst = Stream("zhst", 2)
    qld = Stream("qld", 2)
    qst = Stream("qst", 2)
    s_mmoc = S("s_mmoc")
    s_evoc = S("s_evoc")
    s_tail = S("s_tail")
    s_qrd = S("s_qrd")
    s_qv = S("s_qv")

    @B.gpsimd
    def _(e):
        e.wait_ge(s_xr.h, s_xr.base + (32))
        e.wait_ge(s_scbc.h, s_scbc.base + (16))
        for _s in xnh_st.sems:
            e.wait_ge(_s.h, _s.n)
        for oc in range(67):
            if oc >= 3:
                e.wait_ge(s_mmoc.h, s_mmoc.base + (oc - 2))
            w = 64 if oc == 66 else 128
            dst = wslab[:, (oc % 3) * 4096:(oc % 3) * 4096 + 32 * w].rearrange(
                "p (k c) -> p k c", c=w)
            srcv = w_in[:, oc * 128: oc * 128 + w].rearrange("(k p) c -> p k c", p=128)
            wst.put(oc, e.dma_start(out=dst, in_=srcv))

    with nc.psum_tensor("ps_ip0", [128, 1024], F32) as ps_ip0, \
         nc.psum_tensor("ps_ip1", [128, 1024], F32) as ps_ip1, \
         nc.psum_tensor("ps_ip2", [128, 1024], F32) as ps_ip2, \
         nc.psum_tensor("ps_tail", [128, 102], F32) as ps_tail:
        ps_ip = [ps_ip0, ps_ip1, ps_ip2]

        @B.tensor
        def _(e):
            e.wait_ge(s_xr.h, s_xr.base + (32))
            for oc in range(67):
                wst.wait(e, oc)
                if oc >= 3:
                    e.wait_ge(s_evoc.h, s_evoc.base + (oc - 2))
                w = 64 if oc == 66 else 128
                buf = (oc % 3) * 4096
                pst = ps_ip[oc % 3]
                is_xbc = 32 <= oc <= 65
                inst = None
                for k in range(32):
                    lhsT = wslab[:, buf + k * w: buf + (k + 1) * w]
                    e.matmul(pst[0:w, 0:512], lhsT, xr[:, k * TH + 3: k * TH + 515],
                             start=(k == 0), stop=(k == 31))
                    inst = e.matmul(pst[0:w, 512:1024], lhsT, xr[:, k * TH + 515: k * TH + 1027],
                                    start=(k == 0), stop=(k == 31))
                    if is_xbc:
                        inst = e.matmul(ps_tail[:, (oc - 32) * 3:(oc - 32) * 3 + 3], lhsT,
                                        xr[:, k * TH: k * TH + 3],
                                        start=(k == 0), stop=(k == 31))
                s_mmoc.inc(inst)

        @B.scalar
        def _(e):
            e.wait_ge(s_thr.h, s_thr.base + (3))
            for oc in range(67):
                e.wait_ge(s_mmoc.h, s_mmoc.base + (oc + 1))
                pst = ps_ip[oc % 3]
                if oc < 32:
                    if oc >= 2:
                        zhst.wait(e, oc - 2)
                    ist = e.activation(out=zstg1[:, (oc % 2) * T:(oc % 2) * T + T],
                                       in_=pst[:, 0:1024], func=AF.Copy,
                                       scale=mup_sb[:, oc:oc + 1])
                    s_evoc.inc(ist)
                    e.wait_ge(s_evoc.h, s_evoc.n)
                    zhst.put(oc, e.dma_start(out=z_hbm.ap()[oc * 128:(oc + 1) * 128, :],
                                             in_=zstg1[:, (oc % 2) * T:(oc % 2) * T + T]))
                elif oc < 66:
                    cb = oc - 32
                    s_evoc.inc(e.activation(out=xbc[:, cb * TH + 3: cb * TH + TH],
                                            in_=pst[:, 0:1024], func=AF.Copy,
                                            scale=mup_sb[:, oc:oc + 1]))
                else:
                    s_evoc.inc(e.activation(out=dt_tmp[:, :], in_=pst[0:64, 0:1024],
                                            func=AF.Copy, scale=mup_sb[0:64, 66:67]))
            for cb in range(34):
                s_tail.inc(e.activation(out=xbc[:, cb * TH: cb * TH + 3],
                                        in_=ps_tail[:, cb * 3: cb * 3 + 3], func=AF.Copy,
                                        scale=mup_sb[:, 32 + cb:33 + cb]))

    QT = []
    for pj in range(3):
        for t0 in range(22):
            lo = t0 * 2048
            wdt = min(2048, 43712 - lo)
            QT.append((pj, lo, wdt))
    qsrc = [wg_t, wu_t, wd_t]
    qdst = [qg_hbm, qu_hbm, qd_hbm]

    def _qflat(t):
        a = t.ap() if not isinstance(t, bass.AP) else t
        return a.rearrange("j a b -> (j a b)").rearrange("(p f) -> p f", p=128)

    @B.sync
    def _(e):
        e.wait_ge(s_thr.h, s_thr.base + (3))
        e.wait_ge(s_xr.h, s_xr.base + (32))
        for i, (pj, lo, wdt) in enumerate(QT):
            if i >= 2:
                e.wait_ge(s_qrd.h, s_qrd.base + (i - 1))
            qld.put(i, e.dma_start(out=qin[:, (i % 2) * 2048:(i % 2) * 2048 + wdt],
                                   in_=_qflat(qsrc[pj])[:, lo:lo + wdt]))
            if i >= 1:
                pjp, lop, wdtp = QT[i - 1]
                e.wait_ge(s_qv.h, s_qv.base + (i))
                qst.put(i - 1, e.dma_start(out=_qflat(qdst[pjp])[:, lop:lop + wdtp],
                                           in_=qout[:, ((i - 1) % 2) * 2048:((i - 1) % 2) * 2048 + wdtp]))
        pjp, lop, wdtp = QT[-1]
        e.wait_ge(s_qv.h, s_qv.base + (len(QT)))
        qst.put(len(QT) - 1, e.dma_start(out=_qflat(qdst[pjp])[:, lop:lop + wdtp],
                                         in_=qout[:, ((len(QT) - 1) % 2) * 2048:((len(QT) - 1) % 2) * 2048 + wdtp]))

    @B.vector
    def _(e):
        thr = [thr_g, thr_u, thr_d]
        for i, (pj, lo, wdt) in enumerate(QT):
            qld.wait(e, i)
            if i >= 1:
                e.wait_ge(s_qv.h, s_qv.base + (i))
            if i >= 2:
                qst.wait(e, i - 2)
            src = qin[:, (i % 2) * 2048:(i % 2) * 2048 + wdt]
            e.tensor_scalar(out=qpos[:, 0:wdt], in0=src, scalar1=thr[pj][:, 0:1],
                            scalar2=None, op0=ALU.is_gt)
            inst = e.tensor_scalar(out=qneg[:, 0:wdt], in0=src, scalar1=thr[pj][:, 1:2],
                                   scalar2=None, op0=ALU.is_lt)
            s_qrd.inc(inst)
            e.wait_ge(s_qrd.h, s_qrd.n)
            inst = e.tensor_sub(out=qout[:, (i % 2) * 2048:(i % 2) * 2048 + wdt],
                                in0=qpos[:, 0:wdt], in1=qneg[:, 0:wdt])
            s_qv.inc(inst)


    quiesce_and_free("xin_st_0", "xin_st_1", "s_xpre", "s_mmp", "s_rows", "s_bc",
                     "xin2_st_0", "xin2_st_1", "s_xnv", "s_xns", "xnh_st_0", "xnh_st_1",
                     "s_rmsp", "s_rms", "s_xr", "s_wred", "s_scbc")
    s_dtc = S("s_dtc")
    s_cnvd = S("s_cnvd")
    s_cnvp = S("s_cnvp")
    s_cnva = S("s_cnva")
    s_dec = S("s_dec")
    s_lcch = S("s_lcch")
    s_lcqa = S("s_lcqa")
    lcrow_st = Stream("lcrow_st", 8)
    cq_st = Stream("cq_st", 8)

    def tapwait(e):
        for nm, sm in list(sems.items()):
            if nm.startswith("s_dbg_"):
                e.wait_ge(sm.h, sm.n)

    def conv_body(e, ename, chunks, base, smine):
        e.wait_ge(s_evoc.h, s_evoc.base + (67))
        e.wait_ge(s_tail.h, s_tail.base + (34))
        e.wait_ge(s_const.h, s_const.n)
        for idx, cb in enumerate(chunks):
            if idx >= 2:
                e.wait_ge(s_cnva.h, s_cnva.base + (chunks[idx - 2] + 1))
            pair = idx % 2
            a = csc[base + 2 * pair]
            b = csc[base + 2 * pair + 1]
            fence(e, ename, e.tensor_scalar(out=a[:, 0:T], in0=xbc[:, cb * TH: cb * TH + T],
                                            scalar1=cw_sb[:, cb * 4: cb * 4 + 1], scalar2=None,
                                            op0=ALU.mult))
            fence(e, ename, e.scalar_tensor_tensor(out=b[:, 0:T], in0=xbc[:, cb * TH + 1: cb * TH + 1 + T],
                                                   scalar=cw_sb[:, cb * 4 + 1: cb * 4 + 2], in1=a[:, 0:T],
                                                   op0=ALU.mult, op1=ALU.add))
            fence(e, ename, e.scalar_tensor_tensor(out=a[:, 0:T], in0=xbc[:, cb * TH + 2: cb * TH + 2 + T],
                                                   scalar=cw_sb[:, cb * 4 + 2: cb * 4 + 3], in1=b[:, 0:T],
                                                   op0=ALU.mult, op1=ALU.add))
            inst = e.scalar_tensor_tensor(out=b[:, 0:T], in0=xbc[:, cb * TH + 3: cb * TH + 3 + T],
                                          scalar=cw_sb[:, cb * 4 + 3: cb * 4 + 4], in1=a[:, 0:T],
                                          op0=ALU.mult, op1=ALU.add)
            smine.inc(inst)

    CONV_D = list(range(34))
    CONV_P = []

    @B.vector
    def _(e):
        e.wait_ge(s_mmoc.h, s_mmoc.base + (67))
        e.wait_ge(s_evoc.h, s_evoc.base + (67))
        tapwait(e)
        inst = e.tensor_copy(out=dt_sb[:, :], in_=dt_tmp[:, :])
        s_dtc.inc(inst)
        conv_body(e, "dve", CONV_D, 0, s_cnvd)

    s_hini = S("s_hini")

    @B.gpsimd
    def _(e):
        e.wait_ge(s_mmoc.h, s_mmoc.base + (67))
        tapwait(e)
        inst = e.memset(h_all[:], 0.0)
        s_hini.inc(inst)

    @B.scalar
    def _(e):
        e.wait_ge(s_qv.h, s_qv.base + 66)
        for _s in qst.sems:
            e.wait_ge(_s.h, _s.n)
        for _s in zhst.sems:
            e.wait_ge(_s.h, _s.n)
        e.wait_ge(s_dtc.h, s_dtc.base + 1)
        e.wait_ge(s_mmoc.h, s_mmoc.base + 67)
        for cb in range(34):
            e.wait_ge(s_cnvd.h, s_cnvd.base + (cb + 1))
            b = csc[2 * (cb % 2) + 1]
            inst = e.activation(out=convo[:, cb * T:(cb + 1) * T], in_=b[:, 0:T], func=AF.Silu)
            s_cnva.inc(inst)

    @B.vector
    def _(e):
        e.wait_ge(s_dtc.h, s_dtc.base + (1))
        inst = e.tensor_scalar(out=dt_sb[:, :], in0=dt_sb[:, :], scalar1=dtb_col[:, 0:1],
                               scalar2=None, op0=ALU.add)
        s_dec.inc(inst)
        e.wait_ge(s_dec.h, s_dec.n)
        inst = e.tensor_scalar(out=lc_sb[:, :], in0=dt_sb[:, :], scalar1=0.0, scalar2=None,
                               op0=ALU.max)
        s_dec.inc(inst)

    @B.scalar
    def _(e):
        e.wait_ge(s_dec.h, s_dec.base + (2))
        fence(e, "act", e.activation(out=lt_sb[:, :], in_=dt_sb[:, :], func=AF.Abs))
        fence(e, "act", e.activation(out=lt_sb[:, :], in_=lt_sb[:, :], func=AF.Exp, scale=-1.0))
        inst = e.activation(out=lt_sb[:, :], in_=lt_sb[:, :], func=AF.Ln, bias=1.0)
        s_dec.inc(inst)

    @B.vector
    def _(e):
        e.wait_ge(s_dec.h, s_dec.base + (3))
        inst = e.tensor_add(out=dt_sb[:, :], in0=lt_sb[:, :], in1=lc_sb[:, :])
        s_dec.inc(inst)
        e.wait_ge(s_dec.h, s_dec.n)
        e.wait_ge(s_cder.h, s_cder.base + (1))
        inst = e.tensor_scalar(out=lt_sb[:, :], in0=dt_sb[:, :], scalar1=acol[:, 0:1],
                               scalar2=None, op0=ALU.mult)
        s_dec.inc(inst)
        e.wait_ge(s_dec.h, s_dec.n)
        for ch in range(NCH):
            inst = e.tensor_tensor_scan(out=lc_sb[:, ch * Q:(ch + 1) * Q], data0=onesQ[:, :],
                                        data1=lt_sb[:, ch * Q:(ch + 1) * Q], initial=0.0,
                                        op0=ALU.mult, op1=ALU.add)
            s_lcch.inc(inst)
            e.wait_ge(s_lcch.h, s_lcch.n)
            inst = e.tensor_add(out=lcq_acc[:, 0:1], in0=lcq_acc[:, 0:1],
                                in1=lc_sb[:, ch * Q + Q - 1: ch * Q + Q])
            s_lcqa.inc(inst)
            e.wait_ge(s_lcqa.h, s_lcqa.n)

    @B.sync
    def _(e):
        for ch in range(NCH):
            e.wait_ge(s_lcch.h, s_lcch.base + (ch + 1))
            lcrow_st.put(ch, e.dma_start(out=lcrow.ap()[ch:ch + 1, :],
                                         in_=lc_sb[:, ch * Q:(ch + 1) * Q]))
            lcrow_st.wait(e, ch)
            cq_st.put(ch, e.dma_start(out=cq_dram.ap()[ch:ch + 1, :],
                                      in_=lcrow.ap()[ch, Q - 1::Q][None, :],
                                      allow_slow_non_contiguous=True))

    with nc.psum_tensor("tp16a", [128, 128], F16) as tp16a, \
         nc.psum_tensor("tp16b", [128, 128], F16) as tp16b, \
         nc.psum_tensor("tp32", [128, 64], F32) as tp32, \
         nc.psum_tensor("sbct_ps", [128, 128], F32) as sbct_ps, \
         nc.psum_tensor("psY0", [128, 128], F32) as psY0, \
         nc.psum_tensor("psY1", [128, 128], F32) as psY1, \
         nc.psum_tensor("psH0", [128, 64], F32) as psH0, \
         nc.psum_tensor("psH1", [128, 64], F32) as psH1:
        tp16 = [tp16a, tp16b]
        psY = [psY0, psY1]
        psH = [psH0, psH1]

        def emit_scan_pass(pfx, full):
            tr16 = S(pfx + "tr16")
            t16 = S(pfx + "t16")
            tr32 = S(pfx + "tr32")
            t32 = S(pfx + "t32")
            wsc0 = S(pfx + "wsc0")
            wsc1 = S(pfx + "wsc1")
            wscv = S(pfx + "wscv")
            cq = S(pfx + "cq")
            xsw_s = S(pfx + "xsw")
            psh = S(pfx + "psh")
            hstt = S(pfx + "hstt")
            n32 = 3 if full else 1
            psHp = [psH0, psH1] if full else [psH0, psH1, psY0, psY1]
            nHd = len(psHp)
            if full:
                sbctmm = S(pfx + "sbctmm")
                sbcte = S(pfx + "sbcte")
                cmbL = S(pfx + "cmbL")
                cmb = S(pfx + "cmb")
                cwx = S(pfx + "cwx")
                cwm = S(pfx + "cwm")
                bsub = S(pfx + "bsub")
                mbx = S(pfx + "mbx")
                msk = S(pfx + "msk")
                m_s = S(pfx + "m")
                y_s = S(pfx + "y")
                yev = S(pfx + "yev")
                lcb_st = Stream(pfx + "lcb", 2)
            else:
                cqb_st = Stream(pfx + "cqb", 2)

            @B.sync
            def _(e):
                if full:
                    e.wait_ge(s_gath.h, s_gath.base + (2))
                    for ch in range(NCH):
                        for hl in range(2):
                            g2 = 2 * ch + hl
                            if g2 >= 1:
                                e.wait_ge(y_s.h, y_s.base + (32 * g2))
                            lcb_st.put(g2, e.dma_start(
                                out=lcb[:, :],
                                in_=lcrow.ap()[ch, hl * 4096:(hl + 1) * 4096][None, :].partition_broadcast(128)))
                else:
                    for ch in range(NCH):
                        cq_st.wait(e, ch)
                        if ch >= 1:
                            e.wait_ge(hstt.h, hstt.base + (ch * 64))
                        cqb_st.put(ch, e.dma_start(
                            out=cumq[:, 0:64],
                            in_=cq_dram.ap()[ch, :][None, :].partition_broadcast(128)))

            @B.tensor
            def _(e):
                e.wait_ge(s_cnva.h, s_cnva.base + (34))
                if full:
                    e.wait_ge(s_gath.h, s_gath.base + (2))
                for ch in range(NCH):
                    for j in range(33):
                        gt = ch * 33 + j
                        if gt >= 2:
                            e.wait_ge(t16.h, t16.base + (gt - 1))
                        src_off = j * T + ch * 128
                        inst = e.transpose(tp16[gt % 2][:, :], convo[:, src_off:src_off + 128],
                                           ident16[:])
                        tr16.inc(inst)
                    f32srcs = []
                    if full:
                        f32srcs = [dt_sb[:, ch * Q:(ch + 1) * Q], lc_sb[:, ch * Q:(ch + 1) * Q]]
                    for jj, srcv in enumerate(f32srcs):
                        g32 = ch * n32 + jj
                        if g32 >= 1:
                            e.wait_ge(t32.h, t32.base + (g32))
                        inst = e.transpose(tp32[:, 0:64], srcv, ident32[0:64, 0:64])
                        tr32.inc(inst)
                    e.wait_ge(wscv.h, wscv.base + (ch + 1))
                    g32 = ch * n32 + n32 - 1
                    if g32 >= 1:
                        e.wait_ge(t32.h, t32.base + (g32))
                    inst = e.transpose(tp32[:, 0:64], wsc_t[:, :], ident32[0:64, 0:64])
                    tr32.inc(inst)
                    if full:
                        if ch >= 1:
                            e.wait_ge(sbcte.h, sbcte.base + (ch))
                        inst = e.matmul(sbct_ps[:, :], convo[:, 32 * T + ch * 128: 32 * T + ch * 128 + 128],
                                        convo[:, 33 * T + ch * 128: 33 * T + ch * 128 + 128],
                                        start=True, stop=True)
                        sbctmm.inc(inst)
                    for r in range(64):
                        g = ch * 64 + r
                        hl = r // 32
                        if full:
                            p = r // 2
                            gp = ch * 32 + p
                            e.wait_ge(m_s.h, m_s.base + (2 * ch + hl + 1))
                            e.wait_ge(t16.h, t16.base + (ch * 33 + 33))
                            if gp >= 2:
                                e.wait_ge(yev.h, yev.base + (gp - 1))
                            e.matmul(psY[gp % 2][64 * (r % 2):64 * (r % 2) + 64, :],
                                     xst[:, (r // 2) * 128 + 64 * (r % 2): (r // 2) * 128 + 64 * (r % 2) + 64],
                                     mfin[:, (r % 32) * 128:(r % 32) * 128 + 128],
                                     start=True, stop=False,
                                     tile_position=(0, 64 * (r % 2)))
                            e.wait_ge(cwm.h, cwm.base + (2 * ch + hl + 1))
                            e.wait_ge(hstt.h, hstt.base + (ch * 64))
                            inst = e.matmul(psY[gp % 2][64 * (r % 2):64 * (r % 2) + 64, :],
                                            h_all[:, r * 64:(r + 1) * 64],
                                            cwt[:, (r % 32) * 128:(r % 32) * 128 + 128],
                                            start=False, stop=True,
                                            tile_position=(0, 64 * (r % 2)))
                            y_s.inc(inst)
                        e.wait_ge(xsw_s.h, xsw_s.base + (g + 1))
                        if g >= nHd:
                            e.wait_ge(hstt.h, hstt.base + (g - (nHd - 1)))
                        inst = e.matmul(psHp[r % nHd][:, 0:64], btok[:, :], xsw[:, (r % 4) * 64:(r % 4) * 64 + 64],
                                        start=True, stop=True)
                        psh.inc(inst)

            @B.scalar
            def _(e):
                if full:
                    e.wait_ge(s_gath.h, s_gath.base + (2))
                for ch in range(NCH):
                    for j in range(33):
                        gt = ch * 33 + j
                        e.wait_ge(tr16.h, tr16.base + (gt + 1))
                        if j < 32:
                            dst = xst[:, j * 128:(j + 1) * 128]
                        else:
                            dst = btok[:, :]
                        if full and gt >= 1 and j == 0:
                            e.wait_ge(psh.h, psh.base + (ch * 64))
                            if full:
                                e.wait_ge(y_s.h, y_s.base + (ch * 64))
                        if (not full) and gt >= 1 and j == 0:
                            e.wait_ge(psh.h, psh.base + (ch * 64))
                        inst = e.activation(out=dst, in_=tp16[gt % 2][:, :], func=AF.Copy)
                        t16.inc(inst)
                    e.wait_ge(wsc0.h, wsc0.base + (ch + 1))
                    inst = e.activation(out=wsc_t[:, :], in_=wsc_t[:, :], func=AF.Exp, scale=-1.0)
                    wsc1.inc(inst)
                    dsts = ([dtT, lcTn] if full else []) + [wT]
                    if ch >= 1:
                        e.wait_ge(xsw_s.h, xsw_s.base + (ch * 64))
                        if full:
                            e.wait_ge(cmbL.h, cmbL.base + (ch))
                            e.wait_ge(cmb.h, cmb.base + (ch))
                    for jj, dstv in enumerate(dsts):
                        g32 = ch * n32 + jj
                        e.wait_ge(tr32.h, tr32.base + (g32 + 1))
                        if full and dstv is lcTn:
                            inst = e.activation(out=dstv[:, 0:64], in_=tp32[:, 0:64], func=AF.Copy,
                                                scale=-1.0)
                        else:
                            inst = e.activation(out=dstv[:, 0:64], in_=tp32[:, 0:64], func=AF.Copy)
                        t32.inc(inst)
                    if full:
                        e.wait_ge(t32.h, t32.base + (ch * 3 + 1))
                        if ch >= 1:
                            e.wait_ge(mbx.h, mbx.base + (64 * ch))
                        inst = e.activation(out=combT[:, 0:64], in_=dtT[:, 0:64], func=AF.Ln)
                        cmbL.inc(inst)
                        e.wait_ge(sbctmm.h, sbctmm.base + (ch + 1))
                        inst = e.activation(out=sbct[:, :], in_=sbct_ps[:, :], func=AF.Copy)
                        sbcte.inc(inst)
                        for hl in range(2):
                            g2 = 2 * ch + hl
                            lcb_st.wait(e, g2)
                            e.wait_ge(m_s.h, m_s.base + (g2))
                            inst = e.activation(out=mbig[:, :], in_=lcb[:, :], func=AF.Exp)
                            cwx.inc(inst)
                            e.wait_ge(cwx.h, cwx.n)
                            if ch >= 1:
                                e.wait_ge(hstt.h, hstt.base + ((ch - 1)) * 64 + hl * 32 + 32)
                            ecv = bass.AP(mbig.ap().tensor, 127,
                                          [[4096, 128], [128, 32]])
                            inst = e.activation(out=cumq[:, hl * 32:(hl + 1) * 32], in_=ecv,
                                                func=AF.Copy)
                            cq.inc(inst)
                            e.wait_ge(cq.h, cq.n)
                            e.wait_ge(cwm.h, cwm.base + (g2 + 1))
                            e.wait_ge(msk.h, msk.base + (g2 + 1))
                            e.wait_ge(cmb.h, cmb.base + (ch + 1))
                            for r in range(hl * 32, hl * 32 + 32):
                                inst = e.activation(out=mbig[:, (r % 32) * 128:(r % 32) * 128 + 128],
                                                    in_=lcb[:, (r % 32) * 128:(r % 32) * 128 + 128],
                                                    func=AF.Exp, bias=combT[:, r:r + 1])
                                mbx.inc(inst)
                    else:
                        cqb_st.wait(e, ch)
                        inst = e.activation(out=cumq[:, 0:64], in_=cumq[:, 0:64], func=AF.Exp)
                        cq.inc(inst)

            if full:
                @B.gpsimd
                def _(e):
                    e.wait_ge(s_gath.h, s_gath.base + (2))
                    for ch in range(NCH):
                        for hl in range(2):
                            g2 = 2 * ch + hl
                            e.wait_ge(cwx.h, cwx.base + (g2 + 1))
                            inst = e.affine_select(out=lcb[:, :], in_=lcb[:, :],
                                                   pattern=[[0, 32], [1, 128]],
                                                   compare_op=ALU.is_ge,
                                                   fill=-1.0e30, base=0,
                                                   channel_multiplier=-1)
                            msk.inc(inst)

            @B.vector
            def _(e):
                if full:
                    e.wait_ge(s_gath.h, s_gath.base + (2))
                    e.wait_ge(s_cder2.h, s_cder2.base + (1))
                else:
                    e.wait_ge(s_hini.h, s_hini.base + (1))
                e.wait_ge(s_lcch.h, s_lcch.base + (8))
                for ch in range(NCH):
                    if ch >= 1:
                        e.wait_ge(tr32.h, tr32.base + (ch * n32))
                    fence(e, "dve", e.tensor_scalar(out=wsc_t[:, :], in0=lc_sb[:, ch * Q:(ch + 1) * Q],
                                                    scalar1=lc_sb[:, ch * Q + Q - 1: ch * Q + Q],
                                                    scalar2=None, op0=ALU.subtract))
                    inst = e.tensor_copy(out=tcol[0:64, 0:1], in_=wsc_t[:, 0:1])
                    wsc0.inc(inst)
                    e.wait_ge(wsc1.h, wsc1.base + (ch + 1))
                    inst = e.tensor_mul(out=wsc_t[:, :], in0=wsc_t[:, :], in1=dt_sb[:, ch * Q:(ch + 1) * Q])
                    wscv.inc(inst)
                    e.wait_ge(wscv.h, wscv.n)
                    if full:
                        e.wait_ge(cmbL.h, cmbL.base + (ch + 1))
                        e.wait_ge(t32.h, t32.base + (ch * 3 + 2))
                        inst = e.tensor_add(out=combT[:, 0:64], in0=combT[:, 0:64],
                                            in1=lcTn[:, 0:64])
                        cmb.inc(inst)
                    for hl in range(2):
                        if full:
                            g2 = 2 * ch + hl
                            e.wait_ge(cwx.h, cwx.base + (g2 + 1))
                            e.wait_ge(y_s.h, y_s.base + (32 * g2))
                            cbc = convo.ap()[:, 33 * T + ch * 128: 33 * T + ch * 128 + 128]
                            cbc = cbc.unsqueeze(1).broadcast_to([128, 32, 128])
                            inst = e.tensor_tensor(
                                out=cwt.ap().rearrange("p (a b) -> p a b", b=128),
                                in0=cbc,
                                in1=mbig.ap().rearrange("p (a b) -> p a b", b=128),
                                op=ALU.mult)
                            cwm.inc(inst)
                            e.wait_ge(mbx.h, mbx.base + (32 * (g2 + 1)))
                            e.wait_ge(sbcte.h, sbcte.base + (ch + 1))
                            inst = e.tensor_tensor(
                                out=mfin.ap().rearrange("p (a b) -> p a b", b=128),
                                in0=sbct.ap().unsqueeze(1).broadcast_to([128, 32, 128]),
                                in1=mbig.ap().rearrange("p (a b) -> p a b", b=128),
                                op=ALU.mult)
                            m_s.inc(inst)
                            e.wait_ge(m_s.h, m_s.n)
                        for r in range(hl * 32, hl * 32 + 32):
                            g = ch * 64 + r
                            e.wait_ge(t16.h, t16.base + (ch * 33 + 33))
                            e.wait_ge(t32.h, t32.base + (ch * n32 + n32))
                            if g >= 4:
                                e.wait_ge(psh.h, psh.base + (g - 3))
                            inst = e.tensor_scalar(
                                out=xsw[:, (r % 4) * 64:(r % 4) * 64 + 64],
                                in0=xst[:, (r // 2) * 128 + 64 * (r % 2): (r // 2) * 128 + 64 * (r % 2) + 64],
                                scalar1=wT[:, r:r + 1], scalar2=None, op0=ALU.mult)
                            xsw_s.inc(inst)
                            e.wait_ge(psh.h, psh.base + (g + 1))
                            e.wait_ge(cq.h, cq.base + ((2 * ch + hl + 1) if full else (ch + 1)))
                            if full:
                                e.wait_ge(y_s.h, y_s.base + (g + 1))
                            inst = e.scalar_tensor_tensor(out=h_all[:, r * 64:(r + 1) * 64],
                                                          in0=h_all[:, r * 64:(r + 1) * 64],
                                                          scalar=cumq[:, r:r + 1],
                                                          in1=psHp[r % nHd][:, 0:64],
                                                          op0=ALU.mult, op1=ALU.add)
                            hstt.inc(inst)
                            e.wait_ge(hstt.h, hstt.n)
                            if full and (r % 2 == 1):
                                p = r // 2
                                gp = ch * 32 + p
                                inst = e.scalar_tensor_tensor(
                                    out=yfeat[:, p * T + ch * 128: p * T + ch * 128 + 128],
                                    in0=convo[:, p * T + ch * 128: p * T + ch * 128 + 128],
                                    scalar=db_pair[:, p:p + 1],
                                    in1=psY[gp % 2][:, :],
                                    op0=ALU.mult, op1=ALU.add)
                                yev.inc(inst)
            return locals()

        s_gath = S("s_gath")
        s_cc = S("s_cc")
        pub_st = Stream("pub_st", 2)
        bsl_st = Stream("bsl_st", 2)
        gw_st = Stream("gw_st", 4)
        s_gmm = S("s_gmm")
        s_gev = S("s_gev")
        s_hac = S("s_hac")

        quiesce_and_free("qld_0", "qld_1", "qst_0", "qst_1", "s_qrd", "s_qv", "zhst_0",
                         "zhst_1", "s_mmoc", "s_evoc", "s_tail", "s_dec", "s_dtc")
        z = emit_scan_pass("z_", False)

        @B.sync
        def _(e):
            e.wait_ge(sems["z_hstt"].h, sems["z_hstt"].base + 512)
            pub_st.put(0, e.dma_start(out=ccb_in.ap()[:, :], in_=h_all[:, :]))
            e.wait_ge(s_lcqa.h, s_lcqa.base + (8))
            pub_st.put(1, e.dma_start(out=cclc_in.ap().rearrange("a b -> (a b)")[:, None],
                                      in_=lcq_acc[:, 0:1]))

        @B.gpsimd
        def _(e):
            pub_st.wait(e, 0)
            pub_st.wait(e, 1)
            e.collective_compute("AllGather", ALU.bypass, replica_groups=[list(range(cc_group))],
                                 ins=[ccb_in.ap().opt()], outs=[ccb_out.ap()[0:cc_group * NST, :].opt()]).then_inc(s_cc.h, 1)
            s_cc.n += 1
            e.collective_compute("AllGather", ALU.bypass, replica_groups=[list(range(cc_group))],
                                 ins=[cclc_in.ap().opt()], outs=[cclc_out.ap()[0:cc_group, :].opt()]).then_inc(s_cc.h, 1)
            s_cc.n += 1

        @B.sync
        def _(e):
            e.wait_ge(s_cc.h, s_cc.base + (2))
            gw_st.put(0, e.dma_start(out=lcall[:, :], in_=cclc_out.ap()[:, :]))

        @B.tensor
        def _(e):
            gw_st.wait(e, 0)
            e.wait_ge(s_cmem.h, s_cmem.base + (1))
            inst = e.matmul(psY0[0:8, 0:64], tri8[:, :], lcall[:, :], start=True, stop=True)
            inst = e.matmul(psY1[0:1, 0:64], mask_sb[:, 0:1], lcall[:, :], start=True, stop=True)
            s_gmm.inc(inst)

        @B.scalar
        def _(e):
            e.wait_ge(s_gmm.h, s_gmm.base + (1))
            fence(e, "act", e.activation(out=lcq8[:, :], in_=psY0[0:8, 0:64], func=AF.Copy))
            inst = e.activation(out=lcp[0:1, :], in_=psY1[0:1, 0:64], func=AF.Copy)
            s_gev.inc(inst)

        @B.tensor
        def _(e):
            e.wait_ge(s_gev.h, s_gev.base + (1))
            inst = e.matmul(psH0[0:8, 0:64], ones8[:, :], lcp[0:1, :], start=True, stop=True)
            s_gmm.inc(inst)

        @B.scalar
        def _(e):
            e.wait_ge(s_gmm.h, s_gmm.base + (2))
            inst = e.activation(out=wcore[:, :], in_=psH0[0:8, 0:64], func=AF.Copy)
            s_gev.inc(inst)

        @B.vector
        def _(e):
            e.wait_ge(s_gev.h, s_gev.base + (2))
            fence(e, "dve", e.tensor_sub(out=wcore[:, :], in0=wcore[:, :], in1=lcq8[:, :]))
            inst = e.tensor_scalar(out=wcore[:, :], in0=wcore[:, :], scalar1=0.0, scalar2=None,
                                   op0=ALU.min)
            s_gev.inc(inst)

        @B.scalar
        def _(e):
            e.wait_ge(s_gev.h, s_gev.base + (3))
            inst = e.activation(out=wcore[:, :], in_=wcore[:, :], func=AF.Exp)
            s_gev.inc(inst)

        @B.vector
        def _(e):
            e.wait_ge(s_gev.h, s_gev.base + (4))
            inst = e.tensor_scalar(out=wcore[:, :], in0=wcore[:, :], scalar1=mask_sb[:, 0:1],
                                   scalar2=None, op0=ALU.mult)
            s_gev.inc(inst)

        @B.sync
        def _(e):
            e.wait_ge(s_gev.h, s_gev.base + (5))
            gw_st.put(1, e.dma_start(out=rowbc.ap()[6:7, 0:512], in_=wcore[:, :]))
            gw_st.wait(e, 1)
            gw_st.put(2, e.dma_start(out=wful[:, :],
                                     in_=rowbc.ap()[6, 0:512][None, :].partition_broadcast(128)))
            for cp in range(8):
                if cp >= 2:
                    e.wait_ge(s_hac.h, s_hac.base + (cp - 1))
                bsl_st.put(cp, e.dma_start(out=bslice[:, (cp % 2) * 4096:(cp % 2) * 4096 + 4096],
                                           in_=ccb_out.ap()[cp * 128:(cp + 1) * 128, :]))

        @B.vector
        def _(e):
            gw_st.wait(e, 2)
            for cp in range(8):
                bsl_st.wait(e, cp)
                if cp >= 1:
                    e.wait_ge(s_hac.h, s_hac.base + (cp))
                inst = None
                for r in range(64):
                    bsl = bslice[:, (cp % 2) * 4096 + r * 64:(cp % 2) * 4096 + r * 64 + 64]
                    if cp == 0:
                        inst = e.tensor_scalar(out=hacc32[:, r * 64:(r + 1) * 64], in0=bsl,
                                               scalar1=wful[:, cp * 64 + r:cp * 64 + r + 1],
                                               scalar2=None, op0=ALU.mult)
                    else:
                        inst = e.scalar_tensor_tensor(out=hacc32[:, r * 64:(r + 1) * 64], in0=bsl,
                                                      scalar=wful[:, cp * 64 + r:cp * 64 + r + 1],
                                                      in1=hacc32[:, r * 64:(r + 1) * 64],
                                                      op0=ALU.mult, op1=ALU.add)
                s_hac.inc(inst)

        @B.scalar
        def _(e):
            e.wait_ge(s_hac.h, s_hac.base + (8))
            inst = e.activation(out=h_all[:, :], in_=hacc32[:, :], func=AF.Copy)
            s_gath.inc(inst, 1)
            e.wait_ge(s_gath.h, s_gath.n)
            inst = e.activation(out=tcol[:, 0:1], in_=h_all[:, 0:1], func=AF.Copy)
            s_gath.inc(inst, 1)

        quiesce_and_free("z_tr16", "z_t16", "z_tr32", "z_t32", "z_wsc0", "z_wsc1", "z_wscv",
                         "z_cq", "z_xsw", "z_psh", "z_hstt", "z_cqb_0", "z_cqb_1",
                         "pub_st_0", "pub_st_1", "bsl_st_0", "bsl_st_1",
                         "gw_st_0", "gw_st_1", "gw_st_2", "gw_st_3", "s_gmm", "s_gev", "s_hac",
                         "s_cnvd", "s_cnvp", "s_lcqa", "lcrow_st_0", "lcrow_st_1", "lcrow_st_2",
                         "lcrow_st_3", "lcrow_st_4", "lcrow_st_5", "lcrow_st_6", "lcrow_st_7",
                         "cq_st_0", "cq_st_1", "cq_st_2", "cq_st_3", "cq_st_4", "cq_st_5",
                         "cq_st_6", "cq_st_7")
        u = emit_scan_pass("u_", True)

    tz = tap("zx", [DS, T], F16)
    if tz is not None:
        @B.gpsimd
        def _(e):
            for _s in zhst.sems:
                e.wait_ge(_s.h, _s.n)
            sd = S("s_dbg_zx")
            sd.dma(e.dma_start(out=tz[:, :], in_=z_hbm.ap()[:, :]))
            e.wait_ge(sd.h, sd.base + (16))

    txb = tap("xbc", [128, 34 * TH], F16)
    if txb is not None:
        @B.sync
        def _(e):
            e.wait_ge(s_evoc.h, s_evoc.base + (67))
            e.wait_ge(s_tail.h, s_tail.base + (34))
            sd = S("s_dbg_xbc")
            sd.dma(e.dma_start(out=txb[:, :], in_=xbc[:, :]))
            e.wait_ge(sd.h, sd.base + (16))

    tdt = tap("dtraw", [NH, T], F32)
    if tdt is not None:
        @B.sync
        def _(e):
            e.wait_ge(s_evoc.h, s_evoc.base + (67))
            sd = S("s_dbg_dtraw")
            sd.dma(e.dma_start(out=tdt[:, :], in_=dt_tmp[:, :]))
            e.wait_ge(sd.h, sd.base + (16))

    tq = tap("qg", [8, D8, H8], F16)
    if tq is not None:
        @B.gpsimd
        def _(e):
            for _s in qst.sems:
                e.wait_ge(_s.h, _s.n)
            sd = S("s_dbg_qg")
            sd.dma(e.dma_start(out=tq[:, :, :], in_=qg_hbm.ap()[:, :, :]))
            e.wait_ge(sd.h, sd.base + (16))

    t = tap("xr", [128, 32 * TH], F16)
    if t is not None:
        @B.sync
        def _(e):
            e.wait_ge(s_xr.h, s_xr.base + (32))
            sd = S("s_dbg_xr")
            sd.dma(e.dma_start(out=t[:, :], in_=xr[:, :]))
            e.wait_ge(sd.h, sd.base + (16))

    t2 = tap("xn", [H, T], F16)
    if t2 is not None:
        @B.gpsimd
        def _(e):
            for _s in xnh_st.sems:
                e.wait_ge(_s.h, _s.n)
            sd = S("s_dbg_xn")
            sd.dma(e.dma_start(out=t2[:, :], in_=xn_hbm.ap()[:, :]))
            e.wait_ge(sd.h, sd.base + (16))


    ty = tap("yfeat", [128, 32 * T], F16)
    if ty is not None:
        @B.sync
        def _(e):
            e.wait_ge(sems["u_yev"].h, sems["u_yev"].base + 256)
            sd = S("s_dbg_yfeat")
            sd.dma(e.dma_start(out=ty[:, :], in_=yfeat[:, :]))
            e.wait_ge(sd.h, sd.base + 16)


    quiesce_and_free("u_tr16", "u_t16", "u_tr32", "u_t32", "u_wsc0", "u_wsc1",
                     "u_wscv", "u_cq", "u_xsw", "u_psh", "u_hstt", "u_sbctmm",
                     "u_sbcte", "u_cmbL", "u_cmb", "u_cwx", "u_cwm", "u_bsub",
                     "u_mbx", "u_msk", "u_m",
                     "u_y", "u_yev", "u_lcb_0", "u_lcb_1", "s_gath", "s_hini",
                     "s_cnva", "s_lcch")
    zld = Stream("zld", 2)
    ynh = Stream("ynh", 2)
    s_sz = S("s_sz")
    s_ygv = S("s_ygv")
    s_sq = S("s_sq")
    s_sqm = S("s_sqm")
    s_rst = S("s_rst")
    s_ynv = S("s_ynv")

    with nc.psum_tensor("ps_ssq", [1, T], F32) as ps_ssq:

        @B.sync
        def _(e):
            for cc in range(32):
                if cc >= 2:
                    e.wait_ge(s_sz.h, s_sz.base + cc - 1)
                zld.put(cc, e.dma_start(out=zst[:, (cc % 2) * T:(cc % 2) * T + T],
                                        in_=z_hbm.ap()[cc * 128:(cc + 1) * 128, :]))

        @B.scalar
        def _(e):
            for cc in range(32):
                zld.wait(e, cc)
                if cc >= 2:
                    e.wait_ge(s_ygv.h, s_ygv.base + cc - 1)
                inst = e.activation(out=sgl[:, (cc % 2) * T:(cc % 2) * T + T],
                                    in_=zst[:, (cc % 2) * T:(cc % 2) * T + T], func=AF.Silu)
                s_sz.inc(inst)
                e.wait_ge(s_ygv.h, s_ygv.base + cc + 1)
                if cc >= 2:
                    e.wait_ge(s_sqm.h, s_sqm.base + cc - 1)
                inst = e.activation(out=sqb[:, (cc % 2) * T:(cc % 2) * T + T],
                                    in_=yg[:, cc * T:(cc + 1) * T], func=AF.Square)
                s_sq.inc(inst)

        @B.vector
        def _(e):
            for cc in range(32):
                e.wait_ge(s_sz.h, s_sz.base + cc + 1)
                inst = e.tensor_mul(out=yg[:, cc * T:(cc + 1) * T],
                                    in0=yfeat[:, cc * T:(cc + 1) * T],
                                    in1=sgl[:, (cc % 2) * T:(cc % 2) * T + T])
                s_ygv.inc(inst)
                e.wait_ge(s_ygv.h, s_ygv.n)

        @B.tensor
        def _(e):
            inst = None
            for cc in range(32):
                e.wait_ge(s_sq.h, s_sq.base + cc + 1)
                for (lo, n) in ((0, 512), (512, 512)):
                    inst = e.matmul(ps_ssq[:, lo:lo + n], ones16_col[:],
                                    sqb[:, (cc % 2) * T + lo:(cc % 2) * T + lo + n],
                                    start=(cc == 0), stop=(cc == 31))
                s_sqm.inc(inst)

        @B.scalar
        def _(e):
            e.wait_ge(s_sqm.h, s_sqm.base + 32)
            inst = e.activation(out=row_d[0:1, 0:T], in_=ps_ssq[:, :], func=AF.Sqrt,
                                scale=1.0 / DS, bias=EPS)
            s_rst.inc(inst)

        @B.vector
        def _(e):
            e.wait_ge(s_rst.h, s_rst.base + 1)
            inst = e.reciprocal(out=row_d[0:1, 0:T], in_=row_d[0:1, 0:T])
            s_rst.inc(inst)

        @B.sync
        def _(e):
            e.wait_ge(s_rst.h, s_rst.base + 2)
            s_rst.inc(e.dma_start(out=rowbc.ap()[7:8, 0:T], in_=row_d[0:1, 0:T]), 16)
            e.wait_ge(s_rst.h, s_rst.n)
            s_rst.inc(e.dma_start(out=rstd2_bc[:, :],
                                  in_=rowbc.ap()[7, 0:T][None, :].partition_broadcast(128)), 16)

        @B.vector
        def _(e):
            e.wait_ge(s_rst.h, s_rst.n)
            for cc in range(32):
                if cc >= 2:
                    ynh.wait(e, cc - 2)
                inst = e.scalar_tensor_tensor(out=ynstg[:, (cc % 2) * T:(cc % 2) * T + T],
                                              in0=yg[:, cc * T:(cc + 1) * T],
                                              scalar=nw_sb[:, cc:cc + 1],
                                              in1=rstd2_bc[:, :],
                                              op0=ALU.mult, op1=ALU.mult)
                s_ynv.inc(inst)
                e.wait_ge(s_ynv.h, s_ynv.n)

        @B.sync
        def _(e):
            for cc in range(32):
                e.wait_ge(s_ynv.h, s_ynv.base + cc + 1)
                ynh.put(cc, e.dma_start(out=yn_hbm.ap()[cc * 128:(cc + 1) * 128, :],
                                        in_=ynstg[:, (cc % 2) * T:(cc % 2) * T + T]))

    quiesce_and_free("zld_0", "zld_1", "s_sz", "s_ygv", "s_sq", "s_sqm", "s_rst",
                     "s_ynv", "ynh_0", "ynh_1")
    TQ = 256
    NMC = 4 * NOI * 8
    xnl = Stream("xnl", 1)
    s_win = [S("s_win0"), S("s_win1")]
    s_uwin = [S("s_uwin0", fresh=True), S("s_uwin1", fresh=True)]
    s_dwin = S("s_dwin")
    s_gu = S("s_gu")
    a_ev = S("a_ev")
    v_ev = S("v_ev")
    s_hq = S("s_hq")
    s_dmm = S("s_dmm")
    a_dv = S("a_dv")
    v_dv = S("v_dv")
    s_gd = S("s_gd")
    gst = Stream("gst", 2)

    def win_dma(e, st, dst, src_t, oi):
        ow, wd = OCH[oi]
        for j in range(8):
            for db in range(4):
                st.inc(e.dma_start(out=dst[:, (j * 4 + db) * wd:(j * 4 + db) * wd + wd],
                                   in_=src_t[j, db * 128:(db + 1) * 128, ow:ow + wd]), 16)

    def dwin_dma(e, st, ocw, par):
        for j in range(8):
            for dd in range(NOI):
                do, cnt = OCH[dd]
                st.inc(e.dma_start(out=qwin_d[par][0:cnt, (j * NOI + dd) * 128:(j * NOI + dd) * 128 + 128],
                                   in_=qd_hbm.ap()[j, do:do + cnt, ocw * 128:(ocw + 1) * 128]), 16)

    @B.sync
    def _(e):
        xnl.put(0, e.dma_start(
            out=xn_sb[:, :].rearrange("p (cc t) -> p cc t", t=T),
            in_=xn_hbm.ap().rearrange("(cc p) t -> p cc t", p=128)))
        for qt in range(4):
            for oi in range(NOI):
                gi = qt * NOI + oi
                if gi >= 2:
                    e.wait_ge(s_gu.h, s_gu.base + ((gi - 1) * 8))
                win_dma(e, s_win[gi % 2], qwin_g[gi % 2], qg_hbm.ap(), oi)

    @B.gpsimd
    def _(e):
        for qt in range(4):
            for oi in range(NOI):
                gi = qt * NOI + oi
                if gi >= 2:
                    e.wait_ge(s_gu.h, s_gu.base + ((gi - 1) * 8))
                win_dma(e, s_uwin[gi % 2], qwin_u[gi % 2], qu_hbm.ap(), oi)
            for ocw in range(4):
                di = qt * 4 + ocw
                if di >= 2:
                    e.wait_ge(s_dmm.h, s_dmm.base + ((di - 1) * 8))
                dwin_dma(e, s_dwin, ocw, di % 2)

    with nc.psum_tensor("ps_gp0", [128, TQ], F32) as ps_gp0, \
         nc.psum_tensor("ps_gp1", [128, TQ], F32) as ps_gp1, \
         nc.psum_tensor("ps_gn0", [128, TQ], F32) as ps_gn0, \
         nc.psum_tensor("ps_gn1", [128, TQ], F32) as ps_gn1, \
         nc.psum_tensor("ps_up0", [128, TQ], F32) as ps_up0, \
         nc.psum_tensor("ps_up1", [128, TQ], F32) as ps_up1, \
         nc.psum_tensor("ps_un0", [128, TQ], F32) as ps_un0, \
         nc.psum_tensor("ps_un1", [128, TQ], F32) as ps_un1:
        ps_gp = [ps_gp0, ps_gp1]
        ps_gn = [ps_gn0, ps_gn1]
        ps_up = [ps_up0, ps_up1]
        ps_un = [ps_un0, ps_un1]

        @B.tensor
        def _(e):
            xnl.wait(e, 0)
            for qt in range(4):
                if qt >= 1:
                    e.wait_ge(s_gd.h, s_gd.base + (qt * 32))
                for oi in range(NOI):
                    gi = qt * NOI + oi
                    ow, wd = OCH[oi]
                    e.wait_ge(s_win[gi % 2].h, s_win[gi % 2].base + 512 * (gi // 2 + 1))
                    e.wait_ge(s_uwin[gi % 2].h, s_uwin[gi % 2].base + 512 * (gi // 2 + 1))
                    for kg in range(8):
                        mi = gi * 8 + kg
                        par = mi % 2
                        if mi >= 2:
                            e.wait_ge(s_hq.h, s_hq.base + (mi - 1))
                        inst = None
                        for pj in range(2):
                            qs = (qwin_g, qwin_u)[pj]
                            pp = (ps_gp, ps_up)[pj][par]
                            pn = (ps_gn, ps_un)[pj][par]
                            for kc in range(32):
                                i = kc // 4
                                j = int(JTAB[i, kg])
                                db = kc % 4
                                pos = STAB[i, kg] > 0
                                lst = GU_POS[kg] if pos else GU_NEG[kg]
                                inst = e.matmul((pp if pos else pn)[0:wd, :],
                                                qs[gi % 2][:, (j * 4 + db) * wd:(j * 4 + db) * wd + wd],
                                                xn_sb[:, kc * T + qt * TQ: kc * T + qt * TQ + TQ],
                                                start=(kc == lst[0]), stop=(kc == lst[-1]))
                        s_gu.inc(inst)
                for ocw in range(4):
                    di = qt * 4 + ocw
                    e.wait_ge(s_dwin.h, s_dwin.base + 16 * 88 * (di + 1))
                    e.wait_ge(s_hq.h, s_hq.base + ((qt + 1) * NOI * 8))
                    for kg in range(8):
                        ni = di * 8 + kg
                        par = ni % 2
                        if ni >= 2:
                            e.wait_ge(s_gd.h, s_gd.base + (ni - 1))
                        inst = None
                        for i in range(8):
                            j = int(JTAB[i, kg])
                            pos = STAB[i, kg] > 0
                            lst = D_POS[kg] if pos else D_NEG[kg]
                            dst = (ps_gp if pos else ps_gn)[par]
                            for dd in range(NOI):
                                do, cnt = OCH[dd]
                                inst = e.matmul(dst[:, :],
                                                qwin_d[di % 2][0:cnt, (j * NOI + dd) * 128:(j * NOI + dd) * 128 + 128],
                                                h_q[0:cnt, (i * NOI + dd) * TQ:(i * NOI + dd) * TQ + TQ],
                                                start=(i == lst[0] and dd == 0),
                                                stop=(i == lst[-1] and dd == NOI - 1))
                        s_dmm.inc(inst)

        @B.scalar
        def _(e):
            e.wait_ge(s_cmem.h, s_cmem.base + 1)
            for qt in range(4):
                for oi in range(NOI):
                    ow, wd = OCH[oi]
                    for kg in range(8):
                        mi = (qt * NOI + oi) * 8 + kg
                        par = mi % 2
                        fb = par * 3 * TQ
                        e.wait_ge(s_gu.h, s_gu.base + (mi + 1))
                        if mi >= 2:
                            e.wait_ge(s_hq.h, s_hq.base + (mi - 1))
                        e.activation(out=fevac[0:wd, fb:fb + TQ], in_=ps_gn[par][0:wd, :],
                                     func=AF.Copy)
                        inst = e.activation(out=fevac[0:wd, fb + TQ:fb + 2 * TQ],
                                            in_=ps_un[par][0:wd, :], func=AF.Copy)
                        a_ev.inc(inst)
                        e.wait_ge(a_ev.h, a_ev.n)
                        e.wait_ge(v_ev.h, v_ev.base + (2 * mi + 1))
                        inst = e.activation(out=fevac[0:wd, fb + 2 * TQ:fb + 3 * TQ],
                                            in_=fevac[0:wd, fb:fb + TQ], func=AF.Silu,
                                            scale=sc_g[0:wd, 0:1])
                        a_ev.inc(inst)
                        e.wait_ge(a_ev.h, a_ev.n)
                for ocw in range(4):
                    for kg in range(8):
                        ni = (qt * 4 + ocw) * 8 + kg
                        par = ni % 2
                        fb = par * 3 * TQ
                        e.wait_ge(s_dmm.h, s_dmm.base + (ni + 1))
                        if ni >= 2:
                            e.wait_ge(s_gd.h, s_gd.base + (ni - 1))
                        inst = e.activation(out=fevac[:, fb:fb + TQ], in_=ps_gn[par][:, :],
                                            func=AF.Copy)
                        a_dv.inc(inst)
                        e.wait_ge(a_dv.h, a_dv.n)
                        e.wait_ge(v_dv.h, v_dv.base + (ni + 1))
                        if ni >= 2:
                            gst.wait(e, ni - 2)
                        inst = e.activation(out=gstg[:, par * TQ:par * TQ + TQ],
                                            in_=fevac[:, fb:fb + TQ], func=AF.Silu,
                                            scale=sc_d[:, 0:1])
                        s_gd.inc(inst)
                        e.wait_ge(s_gd.h, s_gd.n)
                        orow = kg * 512 + ocw * 128
                        gst.put(ni, e.dma_start(
                            out=g_hbm.ap()[orow:orow + 128, qt * TQ:(qt + 1) * TQ],
                            in_=gstg[:, par * TQ:par * TQ + TQ]))

        @B.vector
        def _(e):
            for qt in range(4):
                if qt >= 1:
                    e.wait_ge(s_dmm.h, s_dmm.base + (qt * 32))
                for oi in range(NOI):
                    ow, wd = OCH[oi]
                    for kg in range(8):
                        mi = (qt * NOI + oi) * 8 + kg
                        par = mi % 2
                        fb = par * 3 * TQ
                        e.wait_ge(a_ev.h, a_ev.base + (2 * mi + 1))
                        inst = e.tensor_sub(out=fevac[0:wd, fb:fb + TQ],
                                            in0=ps_gp[par][0:wd, :],
                                            in1=fevac[0:wd, fb:fb + TQ])
                        v_ev.inc(inst, 1)
                        e.wait_ge(v_ev.h, v_ev.n)
                        inst = e.tensor_sub(out=fevac[0:wd, fb + TQ:fb + 2 * TQ],
                                            in0=ps_up[par][0:wd, :],
                                            in1=fevac[0:wd, fb + TQ:fb + 2 * TQ])
                        v_ev.inc(inst, 1)
                        e.wait_ge(v_ev.h, v_ev.n)
                        e.wait_ge(a_ev.h, a_ev.base + (2 * mi + 2))
                        inst = e.scalar_tensor_tensor(
                            out=h_q[0:wd, ((kg * NOI + oi)) * TQ:((kg * NOI + oi)) * TQ + TQ],
                            in0=fevac[0:wd, fb + TQ:fb + 2 * TQ],
                            scalar=sc_u[0:wd, 0:1],
                            in1=fevac[0:wd, fb + 2 * TQ:fb + 3 * TQ],
                            op0=ALU.mult, op1=ALU.mult)
                        s_hq.inc(inst)
                        e.wait_ge(s_hq.h, s_hq.n)
                for ocw in range(4):
                    for kg in range(8):
                        ni = (qt * 4 + ocw) * 8 + kg
                        par = ni % 2
                        fb = par * 3 * TQ
                        e.wait_ge(a_dv.h, a_dv.base + (ni + 1))
                        inst = e.tensor_sub(out=fevac[:, fb:fb + TQ], in0=ps_gp[par][:, :],
                                            in1=fevac[:, fb:fb + TQ])
                        v_dv.inc(inst)
                        e.wait_ge(v_dv.h, v_dv.n)

    quiesce_and_free("xnl_0", "s_win0", "s_win1", "s_dwin",
                     "s_gu", "a_ev", "v_ev",
                     "s_hq", "s_dmm", "a_dv", "v_dv", "s_gd", "gst_0", "gst_1")
    ynl = Stream("ynl", 1)
    wo_st = Stream("wo_st", 3, 16, fresh=True)
    gld = Stream("gld", 2)
    xld = Stream("xld", 2)
    ost = Stream("ost", 2)
    s_omm = S("s_omm")
    s_ofin = S("s_ofin")

    @B.sync
    def _(e):
        ynl.put(0, e.dma_start(
            out=yn[:, :].rearrange("p (cc t) -> p cc t", t=T),
            in_=yn_hbm.ap().rearrange("(cc p) t -> p cc t", p=128)))
        for oc in range(32):
            if oc >= 2:
                e.wait_ge(s_ofin.h, s_ofin.base + (2 * (oc - 1)))
            gld.put(oc, e.dma_start(out=gbuf[:, (oc % 2) * T:(oc % 2) * T + T],
                                    in_=g_hbm.ap()[oc * 128:(oc + 1) * 128, :]))
            xld.put(oc, e.dma_start(out=xin6[:, (oc % 2) * T:(oc % 2) * T + T],
                                    in_=xT[oc * 128:(oc + 1) * 128, 3:TH]))

    @B.gpsimd
    def _(e):
        for oc in range(32):
            if oc >= 3:
                e.wait_ge(s_omm.h, s_omm.base + (oc - 2))
            dst = wslab6[:, (oc % 3) * 4096:(oc % 3) * 4096 + 4096].rearrange(
                "p (k c) -> p k c", c=128)
            srcv = w_out[:, oc * 128:(oc + 1) * 128].rearrange("(k p) c -> p k c", p=128)
            wo_st.put(oc, e.dma_start(out=dst, in_=srcv))

    with nc.psum_tensor("ps_o0", [128, T], F32) as ps_o0, \
         nc.psum_tensor("ps_o1", [128, T], F32) as ps_o1, \
         nc.psum_tensor("ps_o2", [128, T], F32) as ps_o2:
        ps_o = [ps_o0, ps_o1, ps_o2]

        @B.tensor
        def _(e):
            ynl.wait(e, 0)
            for oc in range(32):
                wo_st.wait(e, oc)
                if oc >= 3:
                    e.wait_ge(s_ofin.h, s_ofin.base + (2 * (oc - 2)))
                pst = ps_o[oc % 3]
                inst = None
                for k in range(32):
                    lhsT = wslab6[:, (oc % 3) * 4096 + k * 128:(oc % 3) * 4096 + (k + 1) * 128]
                    for (lo, n) in ((0, 512), (512, 512)):
                        inst = e.matmul(pst[:, lo:lo + n], lhsT,
                                        yn[:, k * T + lo:k * T + lo + n],
                                        start=(k == 0), stop=(k == 31))
                s_omm.inc(inst)

        @B.vector
        def _(e):
            for oc in range(32):
                e.wait_ge(s_omm.h, s_omm.base + (oc + 1))
                gld.wait(e, oc)
                xld.wait(e, oc)
                if oc >= 2:
                    ost.wait(e, oc - 2)
                pst = ps_o[oc % 3]
                stv = stg32[:, (oc % 2) * T:(oc % 2) * T + T]
                inst = e.tensor_mul(out=stv, in0=pst[:, :],
                                    in1=gbuf[:, (oc % 2) * T:(oc % 2) * T + T])
                s_ofin.inc(inst)
                e.wait_ge(s_ofin.h, s_ofin.n)
                inst = e.tensor_add(out=stv, in0=stv,
                                    in1=xin6[:, (oc % 2) * T:(oc % 2) * T + T])
                s_ofin.inc(inst)
                e.wait_ge(s_ofin.h, s_ofin.n)

        @B.scalar
        def _(e):
            for oc in range(32):
                e.wait_ge(s_ofin.h, s_ofin.base + (2 * oc + 2))
                ost.put(oc, e.dma_start(out=outT[oc * 128:(oc + 1) * 128, :],
                                        in_=stg32[:, (oc % 2) * T:(oc % 2) * T + T]))
            for _s in ost.sems:
                e.wait_ge(_s.h, _s.n)

    ctx.close()
    build._sems = dict(sems)
    build._pool = list(free_pool)
    return nc, dbg


_BUILT = {}

def _get_nc(debug_taps=(), cc_group=8):
    key = (tuple(sorted(debug_taps)), cc_group)
    if key not in _BUILT:
        _BUILT[key] = build(debug_taps, cc_group)
    return _BUILT[key]


def _prep_inputs(L0_attn_hidden_states, L0_attn_residual, L0_attn_ln_w, L0_attn_in_proj_w,
                 L0_attn_conv_w, L0_attn_A_log, L0_attn_dt_bias, L0_attn_D, L0_attn_norm_w,
                 L0_attn_out_proj_w, L0_attn_mup_vector, L0_ffn_w_gate, L0_ffn_w_up,
                 L0_ffn_w_down):
    x = np.ascontiguousarray(np.asarray(L0_attn_hidden_states, dtype=np.float32)[0])
    xT_full = np.ascontiguousarray(x.T)
    common = dict(
        w_in=np.ascontiguousarray(np.asarray(L0_attn_in_proj_w, np.float32)),
        w_out=np.ascontiguousarray(np.asarray(L0_attn_out_proj_w, np.float32)),
        wg_t=np.ascontiguousarray(np.asarray(L0_ffn_w_gate, np.float32).transpose(0, 2, 1)),
        wu_t=np.ascontiguousarray(np.asarray(L0_ffn_w_up, np.float32).transpose(0, 2, 1)),
        wd_t=np.ascontiguousarray(np.asarray(L0_ffn_w_down, np.float32).transpose(0, 2, 1)),
        convw=np.ascontiguousarray(np.asarray(L0_attn_conv_w, np.float32)),
        ln_w=np.ascontiguousarray(np.asarray(L0_attn_ln_w, np.float32)),
        norm_w=np.ascontiguousarray(np.asarray(L0_attn_norm_w, np.float32)),
        mup=np.ascontiguousarray(np.asarray(L0_attn_mup_vector, np.float32)),
        a_log=np.ascontiguousarray(np.asarray(L0_attn_A_log, np.float32)),
        dt_bias=np.ascontiguousarray(np.asarray(L0_attn_dt_bias, np.float32)),
        d_vec=np.ascontiguousarray(np.asarray(L0_attn_D, np.float32)),
    )
    in_maps = []
    for c in range(NCORE):
        xs = np.zeros((H, TH), np.float32)
        lo = c * T
        xs[:, 3:] = xT_full[:, lo:lo + T]
        if c > 0:
            xs[:, 0:3] = xT_full[:, lo - 3:lo]
        m = dict(common)
        m["xT"] = xs
        m["maskv"] = (np.arange(8) < c).astype(np.float32)
        in_maps.append(m)
    return in_maps


def kernel(**inputs):
    in_maps = _prep_inputs(**inputs)
    nc, _ = _get_nc()
    res = run_bass_kernel_spmd(nc, in_maps, core_ids=list(range(NCORE)))
    parts = [res.results[c]["outT"].T for c in range(NCORE)]
    out = np.concatenate(parts, axis=0)[None]
    return out.astype(np.float32)


# version 12 (final)
# speedup vs baseline: 1.1275x; 1.1275x over previous
import numpy as np
from contextlib import ExitStack

import concourse.bass as bass
import concourse.mybir as mybir
from concourse.bass_utils import run_bass_kernel_spmd

F32 = mybir.dt.float32
F16 = mybir.dt.float16
AF = mybir.ActivationFunctionType
ALU = mybir.AluOpType
AX = mybir.AxisListType

H = 4096
L = 8192
NCORE = 8
T = L // NCORE
TH = T + 3
PROJ = 8512
DS = 4096
CONV = 4352
NH = 64
HD = 64
NST = 128
Q = 128
NCH = 8
HID = 10928
H8 = 1366
D8 = 512
EPS = 1e-5

_TRIPLES = [(1, 2, 3), (1, 4, 5), (1, 7, 6), (2, 4, 6), (2, 5, 7), (3, 4, 7), (3, 6, 5)]

def _oct_tables():
    S = np.zeros((8, 8, 8), np.float32)
    S[0, 0, 0] = 1.0
    for i in range(1, 8):
        S[0, i, i] = 1.0
        S[i, 0, i] = 1.0
        S[i, i, 0] = -1.0
    for a, b, c in _TRIPLES:
        for x, y, z in ((a, b, c), (b, c, a), (c, a, b)):
            S[x, y, z] = 1.0
            S[y, x, z] = -1.0
    jtab = np.zeros((8, 8), np.int64)
    stab = np.zeros((8, 8), np.float32)
    for i in range(8):
        for k in range(8):
            js = np.nonzero(S[:, i, k])[0]
            assert len(js) == 1
            jtab[i, k] = int(js[0])
            stab[i, k] = S[js[0], i, k]
    return jtab, stab

JTAB, STAB = _oct_tables()
OCH = [(oi * 128, min(128, H8 - oi * 128)) for oi in range(11)]
GU_POS = [[kc for kc in range(32) if STAB[kc // 4, kg] > 0] for kg in range(8)]
GU_NEG = [[kc for kc in range(32) if STAB[kc // 4, kg] < 0] for kg in range(8)]
D_POS = [[i for i in range(8) if STAB[i, kg] > 0] for kg in range(8)]
D_NEG = [[i for i in range(8) if STAB[i, kg] < 0] for kg in range(8)]
for _kg in range(8):
    assert GU_POS[_kg] and GU_NEG[_kg] and D_POS[_kg] and D_NEG[_kg]
NOI = len(OCH)
N_WG = 8 * H8 * D8


class Sem:
    def __init__(self, ctx, nc, name):
        self.h = ctx.enter_context(nc.semaphore(name))
        self.n = 0
        self.base = 0
    def inc(self, inst, k=1):
        inst.then_inc(self.h, k)
        self.n += k
        return self.n
    def dma(self, inst):
        return self.inc(inst, 16)


def build(debug_taps=(), cc_group=8):
    taps = set(debug_taps)
    nc = bass.Bass("TRN2", target_bir_lowering=False)

    xT = nc.dram_tensor("xT", [H, TH], F32, kind="ExternalInput").ap()
    w_in = nc.dram_tensor("w_in", [H, PROJ], F32, kind="ExternalInput").ap()
    w_out = nc.dram_tensor("w_out", [DS, H], F32, kind="ExternalInput").ap()
    wg_t = nc.dram_tensor("wg_t", [8, D8, H8], F32, kind="ExternalInput").ap()
    wu_t = nc.dram_tensor("wu_t", [8, D8, H8], F32, kind="ExternalInput").ap()
    wd_t = nc.dram_tensor("wd_t", [8, H8, D8], F32, kind="ExternalInput").ap()
    convw = nc.dram_tensor("convw", [CONV, 4], F32, kind="ExternalInput").ap()
    ln_w = nc.dram_tensor("ln_w", [H], F32, kind="ExternalInput").ap()
    norm_w = nc.dram_tensor("norm_w", [DS], F32, kind="ExternalInput").ap()
    mup = nc.dram_tensor("mup", [PROJ], F32, kind="ExternalInput").ap()
    a_log = nc.dram_tensor("a_log", [NH], F32, kind="ExternalInput").ap()
    dt_bias = nc.dram_tensor("dt_bias", [NH], F32, kind="ExternalInput").ap()
    d_vec = nc.dram_tensor("d_vec", [NH], F32, kind="ExternalInput").ap()
    maskv = nc.dram_tensor("maskv", [8], F32, kind="ExternalInput").ap()
    outT = nc.dram_tensor("outT", [H, T], F32, kind="ExternalOutput").ap()

    z_hbm = nc.dram_tensor("z_hbm", [DS, T], F16)
    xn_hbm = nc.dram_tensor("xn_hbm", [H, T], F16)
    yn_hbm = nc.dram_tensor("yn_hbm", [DS, T], F16)
    g_hbm = nc.dram_tensor("g_hbm", [H, T], F16)
    qg_hbm = nc.dram_tensor("qg_hbm", [8, D8, H8], F16)
    qu_hbm = nc.dram_tensor("qu_hbm", [8, D8, H8], F16)
    qd_hbm = nc.dram_tensor("qd_hbm", [8, H8, D8], F16)
    lcrow = nc.dram_tensor("lcrow", [NCH, NH * Q], F32)
    cq_dram = nc.dram_tensor("cq_dram", [NCH, NH], F32)
    rowbc = nc.dram_tensor("rowbc", [16, 1088], F32)
    ccb_in = nc.dram_tensor("ccb_in", [NST, NH * HD], F16)
    ccb_out = nc.dram_tensor("ccb_out", [NCORE * NST, NH * HD], F16)
    cclc_in = nc.dram_tensor("cclc_in", [1, NH], F32)
    cclc_out = nc.dram_tensor("cclc_out", [NCORE, NH], F32)

    dbg = {}
    def tap(name, shape, dtype=F32):
        if name in taps:
            dbg[name] = nc.dram_tensor("dbg_" + name, shape, dtype, kind="ExternalOutput").ap()
            return dbg[name]
        return None

    ctx = ExitStack()

    ARENA = 209000
    SBUF_BASE = nc.bump_sbuf(ARENA)[0]
    C0, A0, B0 = SBUF_BASE, SBUF_BASE + 65728, SBUF_BASE + 135584
    P0OFF = SBUF_BASE + 205216
    def sb(name, shape, dtype, off):
        return nc.alloc_sbuf_tensor_at(name, shape, dtype, offset=off)

    _bump = [P0OFF]
    def sbp(name, shape, dtype):
        elems = 1
        for s in shape[1:]:
            elems *= s
        nbytes = elems * (2 if dtype == F16 else 4)
        off = (_bump[0] + 31) // 32 * 32
        _bump[0] = off + nbytes
        assert _bump[0] <= SBUF_BASE + ARENA, (name, _bump[0])
        return sb(name, shape, dtype, off)

    ident16 = sbp("ident16", [128, 128], F16)
    ident32 = sbp("ident32", [128, 128], F32)
    ones16_col = sbp("ones16_col", [128, 1], F16)
    ones32_col = sbp("ones32_col", [128, 1], F32)
    onesQ = sbp("onesQ", [NH, Q], F32)
    mup_sb = sbp("mup_sb", [128, 67], F32)
    lnw_sb = sbp("lnw_sb", [128, 32], F32)
    nw_sb = sbp("nw_sb", [128, 32], F32)
    cw_sb = sbp("cw_sb", [128, 34 * 4], F32)
    acol = sbp("acol", [NH, 1], F32)
    dtb_col = sbp("dtb_col", [NH, 1], F32)
    dall = sbp("dall", [128, 64], F32)
    db_pair = sbp("db_pair", [128, 32], F32)
    mask_sb = sbp("mask_sb", [8, 1], F32)
    lcq_acc = sbp("lcq_acc", [NH, 1], F32)
    sc_g = sbp("sc_g", [128, 1], F32)
    sc_u = sbp("sc_u", [128, 1], F32)
    sc_d = sbp("sc_d", [128, 1], F32)
    thr_g = sbp("thr_g", [128, 2], F32)
    thr_u = sbp("thr_u", [128, 2], F32)
    thr_d = sbp("thr_d", [128, 2], F32)
    facc = sbp("facc", [128, 3], F32)
    tsums = sbp("tsums", [128, 33], F32)
    tri8 = sbp("tri8", [8, 8], F32)
    ones8 = sbp("ones8", [1, 8], F32)
    tcol = sbp("tcol", [128, 1], F32)
    scmeans = sbp("scmeans", [128, 3], F32)
    constt = sbp("constt", [1, 16], F32)
    _consts = {}
    def reg_const(v):
        if v not in _consts:
            i = len(_consts)
            assert i < 16
            ap = constt[0:1, i:i + 1]
            nc.const_aps.register(F32, float(v), ap)
            _consts[v] = (i, float(v))
        return _consts[v]
    for _v in (EPS, 1.0 / H, 1.0 / DS, 1.0 / N_WG):
        reg_const(_v)

    xr = sb("xr", [128, 32 * TH], F16, C0)
    dt_sb = sb("dt_sb", [NH, T], F32, C0 + 0)
    lt_sb = sb("lt_sb", [NH, T], F32, C0 + 4128)
    lc_sb = sb("lc_sb", [NH, T], F32, C0 + 8256)
    h_all = sb("h_all", [128, NH * HD], F16, C0 + 12384)
    _cb = [C0 + 20576]
    _coff = {}
    def sbc(name, shape, dtype):
        elems = 1
        for s in shape[1:]:
            elems *= s
        nbytes = elems * (2 if dtype == F16 else 4)
        off = (_cb[0] + 31) // 32 * 32
        _cb[0] = off + nbytes
        _coff[name] = off
        assert _cb[0] <= C0 + 65728, (name, _cb[0])
        return sb(name, shape, dtype, off)

    dtT = sbc("dtT", [128, NH], F32)
    lcTn = sbc("lcTn", [128, NH], F32)
    wT = sbc("wT", [128, NH], F32)
    cumq = sbc("cumq", [128, NH], F32)
    btok = sbc("btok", [128, 128], F16)
    sbct = sbc("sbct", [128, 128], F16)
    wsc_t = sbc("wsc_t", [NH, 128], F32)
    combT = sbc("combT", [128, NH], F32)
    xsw = sbc("xsw", [128, 4 * 64], F16)
    xst = sbc("xst", [128, 4096], F16)
    lcb = sbc("lcb", [128, 4096], F32)
    mbig = sbc("mbig", [128, 4096], F16)
    cwt = sbc("cwt", [128, 4096], F16)
    csc = [sb(f"csc{i}", [128, T], F32, _coff["xst"] + i * 4128) for i in range(8)]
    hacc32 = sb("hacc32", [128, 4096], F32, _coff["lcb"])
    mfin = sb("mfin", [128, 4096], F16, _coff["lcb"])
    wful = sb("wful", [128, 8 * 64], F32, _coff["xst"])
    lcall = sb("lcall", [8, 64], F32, _coff["xst"] + 2048)
    lcq8 = sb("lcq8", [8, 64], F32, _coff["xst"] + 2336)
    lcp = sb("lcp", [1, 64], F32, _coff["xst"] + 2624)
    wcore = sb("wcore", [8, 64], F32, _coff["xst"] + 2912)
    bslice = sb("bslice", [128, 2 * 4096], F16, _coff["mbig"])
    zst = sb("zst", [128, 2 * T], F16, C0 + 0)
    rstd2_bc = sb("rstd2_bc", [128, T], F32, C0 + 4128)
    ynstg = sb("ynstg", [128, 2 * T], F16, C0 + 8352)
    row_d = sb("row_d", [1, 1088], F32, C0 + 12576)
    row_e = sb("row_e", [1, 1088], F32, C0 + 16928)
    sgl = sb("sgl", [128, 2 * T], F32, C0 + 21312)
    sqb = sb("sqb", [128, 2 * T], F16, C0 + 29536)
    qwin_g = [sb("qwin_g0", [128, 4096], F16, C0 + 0),
              sb("qwin_g1", [128, 4096], F16, C0 + 16384)]
    qwin_u = [sb("qwin_u0", [128, 4096], F16, C0 + 8192),
              sb("qwin_u1", [128, 4096], F16, C0 + 24576)]
    qwin_d = [sb("qwin_d0", [128, 11264], F16, C0 + 32768),
              sb("qwin_d1", [128, 11264], F16, A0 + 45056)]
    gstg = sb("gstg", [128, 2 * 256], F16, C0 + 55296)
    fevac = sb("fevac", [128, 6 * 256], F32, C0 + 56352)
    yn = sb("yn", [128, 32 * T], F16, C0 + 0)
    ph32 = sb("ph32", [128, T], F32, B0 + 0)

    xbc = sb("xbc", [128, 34 * TH], F16, A0)
    wstream = sb("wstream", [128, 4 * 4096], F32, A0)
    yfeat = sb("yfeat", [128, 32 * T], F16, A0)
    h_q = sb("h_q", [128, 88 * 256], F16, A0)
    wslab6 = sb("wslab6", [128, 3 * 4096], F16, A0)
    xin6 = sb("xin6", [128, 2 * T], F32, A0 + 24576)
    gbuf = sb("gbuf", [128, 2 * T], F16, A0 + 32768)
    stg32 = sb("stg32", [128, 2 * T], F32, A0 + 36864)

    xin = sb("xin", [128, 2 * TH], F32, B0)
    f16a = sb("f16a", [128, 2 * TH], F16, B0 + 16448)
    f16b = sb("f16b", [128, 2 * TH], F16, B0 + 20576)
    f32a = sb("f32a", [128, TH], F32, B0 + 24704)
    mean_bc = sb("mean_bc", [128, TH], F32, B0 + 28832)
    rstd_bc = sb("rstd_bc", [128, TH], F32, B0 + 32960)
    stg = sb("stg", [128, 2 * T], F16, B0 + 37088)
    row_a = sb("row_a", [1, 1088], F32, B0 + 41184)
    row_b = sb("row_b", [1, 1088], F32, B0 + 45536)
    row_c = sb("row_c", [1, 1088], F32, B0 + 49888)
    qin = sb("qin", [128, 2 * 2048], F32, B0)
    qpos = sb("qpos", [128, 2048], F16, B0 + 16384)
    qneg = sb("qneg", [128, 2048], F16, B0 + 20480)
    qout = sb("qout", [128, 2 * 2048], F16, B0 + 24576)
    wslab = sb("wslab", [128, 3 * 4096], F16, B0 + 32768)
    zstg1 = sb("zstg1", [128, 2 * T], F16, B0 + 57344)
    dt_tmp = sb("dt_tmp", [NH, T], F32, B0 + 61440)
    convo = sb("convo", [128, 34 * T], F16, B0)
    yg = sb("yg", [128, 32 * T], F16, B0)
    xn_sb = sb("xn_sb", [128, 32 * T], F16, B0)

    sems = {}
    free_pool = []
    build._alloc_log = []
    def S(name, fresh=False):
        if name not in sems:
            if free_pool and not fresh:
                old = free_pool.pop()
                s = Sem.__new__(Sem)
                s.h = old.h
                s.n = old.n
                s.base = old.n
                sems[name] = s
                build._alloc_log.append((name, getattr(old.h, "name", "?"), old.n))
            else:
                sems[name] = Sem(ctx, nc, name)
                build._alloc_log.append((name, "FRESH", 0))
        return sems[name]

    def sfree(*names):
        for nm in names:
            if nm in sems:
                free_pool.append(sems.pop(nm))

    s_bar = Sem(ctx, nc, "s_bar")

    class GatedBlock:
        def __getattr__(self, eng):
            dec = getattr(block, eng)
            def wrap(f):
                def g(e):
                    if s_bar.n:
                        e.wait_ge(s_bar.h, s_bar.n)
                    return f(e)
                return dec(g)
            return wrap
    B = GatedBlock()

    def quiesce_and_free(*names):
        @block.sync
        def _(e):
            if s_bar.n:
                e.wait_ge(s_bar.h, s_bar.n)
            for nm in names:
                if nm in sems:
                    e.wait_ge(sems[nm].h, sems[nm].n)
            s_bar.inc(e.dma_start(out=rowbc.ap()[15:16, 0:4], in_=xT[0:1, 0:4]), 16)
        sfree(*names)

    def fence(e, ename, inst):
        s = S("self_" + ename)
        s.inc(inst)
        e.wait_ge(s.h, s.n)

    class Stream:
        def __init__(self, name, nbufs, step=16, fresh=False):
            self.sems = [S(f"{name}_{i}", fresh=fresh) for i in range(nbufs)]
            self.nb = nbufs
            self.step = step
        def put(self, i, inst):
            self.sems[i % self.nb].inc(inst, self.step)
        def wait(self, e, i):
            s = self.sems[i % self.nb]
            e.wait_ge(s.h, s.base + self.step * (i // self.nb + 1))

    block = ctx.enter_context(nc.Block())

    s_const = S("s_const")
    s_cmem = S("s_cmem")
    s_cder = S("s_cder")

    @B.scalar
    def _(e):
        for c in range(66):
            s_const.dma(e.dma_start(out=mup_sb[:, c:c + 1], in_=mup[c * 128:(c + 1) * 128][:, None]))
        s_const.dma(e.dma_start(out=mup_sb[0:64, 66:67], in_=mup[8448:8512][:, None]))
        for c in range(32):
            s_const.dma(e.dma_start(out=lnw_sb[:, c:c + 1], in_=ln_w[c * 128:(c + 1) * 128][:, None]))
            s_const.dma(e.dma_start(out=nw_sb[:, c:c + 1], in_=norm_w[c * 128:(c + 1) * 128][:, None]))
        for c in range(34):
            s_const.dma(e.dma_start(out=cw_sb[:, c * 4:(c + 1) * 4], in_=convw[c * 128:(c + 1) * 128, :]))
        s_const.dma(e.dma_start(out=acol[:, 0:1], in_=a_log[:, None]))
        s_const.dma(e.dma_start(out=dtb_col[:, 0:1], in_=dt_bias[:, None]))
        s_const.dma(e.dma_start(out=mask_sb[:, 0:1], in_=maskv[:, None]))
        s_const.dma(e.dma_start(out=rowbc.ap()[0:1, 0:64], in_=d_vec[None, :]))
        e.wait_ge(s_const.h, s_const.n)
        s_const.dma(e.dma_start(out=dall[:, :], in_=rowbc.ap()[0, 0:64][None, :].partition_broadcast(128)))

    @B.gpsimd
    def _(e):
        fence(e, "pl", e.memset(ident16[:], 1.0))
        e.affine_select(out=ident16[:], in_=ident16[:], pattern=[[1, 128]],
                        compare_op=ALU.is_equal, fill=0.0, base=0, channel_multiplier=-1)
        fence(e, "pl", e.memset(ident32[:], 1.0))
        e.affine_select(out=ident32[:], in_=ident32[:], pattern=[[1, 128]],
                        compare_op=ALU.is_equal, fill=0.0, base=0, channel_multiplier=-1)
        e.memset(ones16_col[:], 1.0)
        e.memset(ones32_col[:], 1.0)
        e.memset(ones8[:], 1.0)
        fence(e, "pl", e.memset(tri8[:], 1.0))
        e.affine_select(out=tri8[:], in_=tri8[:], pattern=[[1, 8]],
                        compare_op=ALU.is_ge, fill=0.0, base=0, channel_multiplier=-1)
        e.memset(onesQ[:], 1.0)
        e.memset(facc[:], 0.0)
        e.memset(lcq_acc[:], 0.0)
        inst = None
        for _v, (_i, _fv) in sorted(_consts.items(), key=lambda kv: kv[1][0]):
            inst = e.memset(constt[0:1, _i:_i + 1], _fv)
        s_cmem.inc(inst)

    @B.scalar
    def _(e):
        fence(e, "act", e.activation(out=acol[:, 0:1], in_=acol[:, 0:1], func=AF.Exp))
        inst = e.mul(out=acol[:, 0:1], in_=acol[:, 0:1], mul=-1.0)
        s_cder.inc(inst)

    s_cder2 = S("s_cder2")

    @B.vector
    def _(e):
        e.wait_ge(s_const.h, s_const.n)
        e.tensor_copy(out=db_pair[0:64, :], in_=dall[0:64, 0:64:2])
        inst = e.tensor_copy(out=db_pair[64:128, :], in_=dall[64:128, 1:64:2])
        s_cder2.inc(inst)

    xin_st = Stream("xin_st", 2)
    s_xpre = S("s_xpre")
    s_mmp = S("s_mmp")
    s_rows = S("s_rows")
    s_bc = S("s_bc")
    xin2_st = Stream("xin2_st", 2)
    s_xnv = S("s_xnv")
    s_xns = S("s_xns")
    xnh_st = Stream("xnh_st", 2)
    s_rmsp = S("s_rmsp")
    s_rms = S("s_rms")
    s_xr = S("s_xr")

    with nc.psum_tensor("ps_sum", [1, TH], F32) as ps_sum, \
         nc.psum_tensor("ps_sq", [1, TH], F32) as ps_sq:

        @B.sync
        def _(e):
            for c in range(32):
                if c >= 2:
                    e.wait_ge(s_xpre.h, s_xpre.base + (2 * (c - 1)))
                xin_st.put(c, e.dma_start(out=xin[:, (c % 2) * TH:(c % 2) * TH + TH],
                                          in_=xT[c * 128:(c + 1) * 128, :]))

        @B.scalar
        def _(e):
            for c in range(32):
                xin_st.wait(e, c)
                if c >= 2:
                    e.wait_ge(s_mmp.h, s_mmp.base + (c - 1))
                xc = xin[:, (c % 2) * TH:(c % 2) * TH + TH]
                fa = f16a[:, (c % 2) * TH:(c % 2) * TH + TH]
                fb = f16b[:, (c % 2) * TH:(c % 2) * TH + TH]
                s_xpre.inc(e.activation(out=fa, in_=xc, func=AF.Copy))
                s_xpre.inc(e.activation(out=fb, in_=xc, func=AF.Square))

        @B.tensor
        def _(e):
            e.wait_ge(s_cmem.h, s_cmem.base + (1))
            for c in range(32):
                e.wait_ge(s_xpre.h, s_xpre.base + (2 * (c + 1)))
                fa = f16a[:, (c % 2) * TH:(c % 2) * TH + TH]
                fb = f16b[:, (c % 2) * TH:(c % 2) * TH + TH]
                for (lo, n) in ((0, 512), (512, 512), (1024, 3)):
                    e.matmul(ps_sum[:, lo:lo + n], ones16_col[:], fa[:, lo:lo + n],
                             start=(c == 0), stop=(c == 31))
                inst = None
                for (lo, n) in ((0, 512), (512, 512), (1024, 3)):
                    inst = e.matmul(ps_sq[:, lo:lo + n], ones16_col[:], fb[:, lo:lo + n],
                                    start=(c == 0), stop=(c == 31))
                s_mmp.inc(inst)

        @B.vector
        def _(e):
            e.wait_ge(s_mmp.h, s_mmp.base + (32))
            e.tensor_scalar(out=row_a[:, 0:TH], in0=ps_sum[:, :], scalar1=1.0 / H,
                            scalar2=None, op0=ALU.mult)
            fence(e, "dve", e.tensor_scalar(out=row_b[:, 0:TH], in0=ps_sq[:, :], scalar1=1.0 / H,
                                            scalar2=None, op0=ALU.mult))
            fence(e, "dve", e.tensor_mul(out=row_c[:, 0:TH], in0=row_a[:, 0:TH], in1=row_a[:, 0:TH]))
            inst = e.tensor_sub(out=row_b[:, 0:TH], in0=row_b[:, 0:TH], in1=row_c[:, 0:TH])
            s_rows.inc(inst)

        @B.scalar
        def _(e):
            e.wait_ge(s_rows.h, s_rows.base + (1))
            e.wait_ge(s_cmem.h, s_cmem.base + (1))
            inst = e.activation(out=row_c[:, 0:TH], in_=row_b[:, 0:TH], func=AF.Sqrt, bias=EPS)
            s_rows.inc(inst)

        @B.vector
        def _(e):
            e.wait_ge(s_rows.h, s_rows.base + (2))
            inst = e.reciprocal(out=row_c[:, 0:TH], in_=row_c[:, 0:TH])
            s_rows.inc(inst)

        @B.sync
        def _(e):
            e.wait_ge(s_rows.h, s_rows.base + (3))
            s_bc.dma(e.dma_start(out=rowbc.ap()[1:2, 0:TH], in_=row_a[:, 0:TH]))
            s_bc.dma(e.dma_start(out=rowbc.ap()[2:3, 0:TH], in_=row_c[:, 0:TH]))
            e.wait_ge(s_bc.h, s_bc.base + (32))
            s_bc.dma(e.dma_start(out=mean_bc[:, :],
                                 in_=rowbc.ap()[1, 0:TH][None, :].partition_broadcast(128)))
            s_bc.dma(e.dma_start(out=rstd_bc[:, :],
                                 in_=rowbc.ap()[2, 0:TH][None, :].partition_broadcast(128)))

    with nc.psum_tensor("ps_rms", [1, TH], F32) as ps_rms:

        @B.sync
        def _(e):
            for c in range(32):
                if c >= 2:
                    e.wait_ge(s_xnv.h, s_xnv.base + (c - 1))
                xin2_st.put(c, e.dma_start(out=xin[:, (c % 2) * TH:(c % 2) * TH + TH],
                                           in_=xT[c * 128:(c + 1) * 128, :]))

        @B.vector
        def _(e):
            e.wait_ge(s_bc.h, s_bc.base + (64))
            for c in range(32):
                xin2_st.wait(e, c)
                if c >= 1:
                    e.wait_ge(s_xns.h, s_xns.base + (2 * c))
                xc = xin[:, (c % 2) * TH:(c % 2) * TH + TH]
                fa = f16a[:, (c % 2) * TH:(c % 2) * TH + TH]
                fence(e, "dve", e.tensor_sub(out=f32a[:], in0=xc, in1=mean_bc[:]))
                inst = e.tensor_mul(out=fa, in0=f32a[:], in1=rstd_bc[:])
                s_xnv.inc(inst)

        @B.scalar
        def _(e):
            for c in range(32):
                e.wait_ge(s_xnv.h, s_xnv.base + (c + 1))
                if c >= 2:
                    e.wait_ge(s_rmsp.h, s_rmsp.base + (c - 1))
                    xnh_st.wait(e, c - 2)
                fa = f16a[:, (c % 2) * TH:(c % 2) * TH + TH]
                fb = f16b[:, (c % 2) * TH:(c % 2) * TH + TH]
                s_xns.inc(e.activation(out=fb, in_=fa, func=AF.Square))
                fence(e, "act", e.activation(out=stg[:, (c % 2) * T:(c % 2) * T + T],
                                             in_=fa[:, 3:TH], func=AF.Copy))
                xnh_st.put(c, e.dma_start(out=xn_hbm.ap()[c * 128:(c + 1) * 128, :],
                                          in_=stg[:, (c % 2) * T:(c % 2) * T + T]))
                s_xns.inc(e.activation(out=xr[:, c * TH:(c + 1) * TH], in_=fa, func=AF.Copy))

        @B.tensor
        def _(e):
            e.wait_ge(s_cmem.h, s_cmem.base + (1))
            inst = None
            for c in range(32):
                e.wait_ge(s_xns.h, s_xns.base + (2 * c + 1))
                fb = f16b[:, (c % 2) * TH:(c % 2) * TH + TH]
                for (lo, n) in ((0, 512), (512, 512), (1024, 3)):
                    inst = e.matmul(ps_rms[:, lo:lo + n], ones16_col[:], fb[:, lo:lo + n],
                                    start=(c == 0), stop=(c == 31))
                s_rmsp.inc(inst)

        @B.scalar
        def _(e):
            e.wait_ge(s_rmsp.h, s_rmsp.base + (32))
            e.wait_ge(s_cmem.h, s_cmem.base + (1))
            inst = e.activation(out=row_a[:, 0:TH], in_=ps_rms[:, :], func=AF.Sqrt,
                                scale=1.0 / DS, bias=EPS)
            s_rms.inc(inst)

        @B.vector
        def _(e):
            e.wait_ge(s_rms.h, s_rms.base + (1))
            inst = e.reciprocal(out=row_a[:, 0:TH], in_=row_a[:, 0:TH])
            s_rms.inc(inst)

        @B.sync
        def _(e):
            e.wait_ge(s_rms.h, s_rms.base + (2))
            s_bc.dma(e.dma_start(out=rowbc.ap()[3:4, 0:TH], in_=row_a[:, 0:TH]))
            e.wait_ge(s_bc.h, s_bc.n)
            s_bc.dma(e.dma_start(out=rstd_bc[:, :],
                                 in_=rowbc.ap()[3, 0:TH][None, :].partition_broadcast(128)))

        @B.vector
        def _(e):
            e.wait_ge(s_bc.h, s_bc.n)
            e.wait_ge(s_xns.h, s_xns.base + (64))
            for c in range(32):
                if c >= 1:
                    e.wait_ge(s_xr.h, s_xr.base + (c))
                fence(e, "dve", e.tensor_mul(out=f32a[:], in0=xr[:, c * TH:(c + 1) * TH], in1=rstd_bc[:]))
                inst = e.tensor_scalar(out=xr[:, c * TH:(c + 1) * TH], in0=f32a[:],
                                       scalar1=lnw_sb[:, c:c + 1], scalar2=None, op0=ALU.mult)
                s_xr.inc(inst)


    wsc_st = Stream("wsc_st", 2)
    s_wred = S("s_wred")
    s_thr = S("s_thr")

    WTILES = []
    for pj in range(3):
        for t0 in range(11):
            lo = t0 * 4096
            wdt = min(4096, 43712 - lo)
            WTILES.append((pj, lo, wdt))

    s_scbc = S("s_scbc")

    @B.gpsimd
    def _(e):
        e.wait_ge(s_cmem.h, s_cmem.base + (1))
        srcs = [wg_t, wu_t, wd_t]
        for i, (pj, lo, wdt) in enumerate(WTILES):
            if i >= 4:
                e.wait_ge(s_wred.h, s_wred.base + (i - 3))
            flat = srcs[pj].rearrange("j a b -> (j a b)").rearrange("(p f) -> p f", p=128)
            wsc_st.put(i, e.dma_start(out=wstream[:, (i % 4) * 4096:(i % 4) * 4096 + wdt],
                                      in_=flat[:, lo:lo + wdt]))

    @B.scalar
    def _(e):
        e.wait_ge(s_cmem.h, s_cmem.base + (1))
        for i, (pj, lo, wdt) in enumerate(WTILES):
            wsc_st.wait(e, i)
            ws = wstream[:, (i % 4) * 4096:(i % 4) * 4096 + wdt]
            inst = e.activation(out=ws, in_=ws, func=AF.Abs, accum_out=tsums[:, i:i + 1])
            s_wred.inc(inst)
            e.wait_ge(s_wred.h, s_wred.n)

    @B.gpsimd
    def _(e):
        e.wait_ge(s_scbc.h, s_scbc.base + (32))
        inst = None
        for pj, (sc, th) in enumerate(((sc_g, thr_g), (sc_u, thr_u), (sc_d, thr_d))):
            e.tensor_copy(out=sc[:, 0:1], in_=scmeans[:, pj:pj + 1])
            e.tensor_scalar(out=th[:, 0:1], in0=scmeans[:, pj:pj + 1], scalar1=0.5,
                            scalar2=None, op0=ALU.mult)
            inst = e.tensor_scalar(out=th[:, 1:2], in0=scmeans[:, pj:pj + 1], scalar1=-0.5,
                                   scalar2=None, op0=ALU.mult)
        s_thr.inc(inst)

    with nc.psum_tensor("ps_wtot", [1, 33], F32) as ps_wtot:
        @B.tensor
        def _(e):
            e.wait_ge(s_wred.h, s_wred.base + (33))
            inst = None
            for pj in range(3):
                inst = e.matmul(ps_wtot[0:1, pj * 11:(pj + 1) * 11], ones32_col[:],
                                tsums[:, pj * 11:(pj + 1) * 11], start=True, stop=True)
            s_thr.inc(inst)

        @B.scalar
        def _(e):
            e.wait_ge(s_thr.h, s_thr.base + (1))
            inst = None
            for pj in range(3):
                inst = e.activation(out=row_c[0:1, pj * 16:pj * 16 + 11],
                                    in_=ps_wtot[0:1, pj * 11:(pj + 1) * 11],
                                    func=AF.Copy, scale=1.0 / N_WG,
                                    accum_out=row_b[0:1, pj:pj + 1])
            s_thr.inc(inst)

    @B.sync
    def _(e):
        e.wait_ge(s_thr.h, s_thr.base + (2))
        s_scbc.dma(e.dma_start(out=rowbc.ap()[5:6, 0:3], in_=row_b[0:1, 0:3]))
        e.wait_ge(s_scbc.h, s_scbc.base + (16))
        s_scbc.dma(e.dma_start(out=scmeans[:, :],
                               in_=rowbc.ap()[5, 0:3][None, :].partition_broadcast(128)))

    wst = Stream("wst", 3)
    zhst = Stream("zhst", 2)
    qld = Stream("qld", 2)
    qst = Stream("qst", 2)
    s_mmoc = S("s_mmoc")
    s_evoc = S("s_evoc")
    s_tail = S("s_tail")
    s_qrd = S("s_qrd")
    s_qv = S("s_qv")

    @B.gpsimd
    def _(e):
        e.wait_ge(s_xr.h, s_xr.base + (32))
        e.wait_ge(s_scbc.h, s_scbc.base + (16))
        for _s in xnh_st.sems:
            e.wait_ge(_s.h, _s.n)
        for oc in range(67):
            if oc >= 3:
                e.wait_ge(s_mmoc.h, s_mmoc.base + (oc - 2))
            w = 64 if oc == 66 else 128
            dst = wslab[:, (oc % 3) * 4096:(oc % 3) * 4096 + 32 * w].rearrange(
                "p (k c) -> p k c", c=w)
            srcv = w_in[:, oc * 128: oc * 128 + w].rearrange("(k p) c -> p k c", p=128)
            wst.put(oc, e.dma_start(out=dst, in_=srcv))

    with nc.psum_tensor("ps_ip0", [128, 1024], F32) as ps_ip0, \
         nc.psum_tensor("ps_ip1", [128, 1024], F32) as ps_ip1, \
         nc.psum_tensor("ps_ip2", [128, 1024], F32) as ps_ip2, \
         nc.psum_tensor("ps_tail", [128, 102], F32) as ps_tail:
        ps_ip = [ps_ip0, ps_ip1, ps_ip2]

        @B.tensor
        def _(e):
            e.wait_ge(s_xr.h, s_xr.base + (32))
            for oc in range(67):
                wst.wait(e, oc)
                if oc >= 3:
                    e.wait_ge(s_evoc.h, s_evoc.base + (oc - 2))
                w = 64 if oc == 66 else 128
                buf = (oc % 3) * 4096
                pst = ps_ip[oc % 3]
                is_xbc = 32 <= oc <= 65
                inst = None
                for k in range(32):
                    lhsT = wslab[:, buf + k * w: buf + (k + 1) * w]
                    e.matmul(pst[0:w, 0:512], lhsT, xr[:, k * TH + 3: k * TH + 515],
                             start=(k == 0), stop=(k == 31))
                    inst = e.matmul(pst[0:w, 512:1024], lhsT, xr[:, k * TH + 515: k * TH + 1027],
                                    start=(k == 0), stop=(k == 31))
                    if is_xbc:
                        inst = e.matmul(ps_tail[:, (oc - 32) * 3:(oc - 32) * 3 + 3], lhsT,
                                        xr[:, k * TH: k * TH + 3],
                                        start=(k == 0), stop=(k == 31))
                s_mmoc.inc(inst)

        @B.scalar
        def _(e):
            e.wait_ge(s_thr.h, s_thr.base + (3))
            for oc in range(67):
                e.wait_ge(s_mmoc.h, s_mmoc.base + (oc + 1))
                pst = ps_ip[oc % 3]
                if oc < 32:
                    if oc >= 2:
                        zhst.wait(e, oc - 2)
                    ist = e.activation(out=zstg1[:, (oc % 2) * T:(oc % 2) * T + T],
                                       in_=pst[:, 0:1024], func=AF.Copy,
                                       scale=mup_sb[:, oc:oc + 1])
                    s_evoc.inc(ist)
                    e.wait_ge(s_evoc.h, s_evoc.n)
                    zhst.put(oc, e.dma_start(out=z_hbm.ap()[oc * 128:(oc + 1) * 128, :],
                                             in_=zstg1[:, (oc % 2) * T:(oc % 2) * T + T]))
                elif oc < 66:
                    cb = oc - 32
                    s_evoc.inc(e.activation(out=xbc[:, cb * TH + 3: cb * TH + TH],
                                            in_=pst[:, 0:1024], func=AF.Copy,
                                            scale=mup_sb[:, oc:oc + 1]))
                else:
                    s_evoc.inc(e.activation(out=dt_tmp[:, :], in_=pst[0:64, 0:1024],
                                            func=AF.Copy, scale=mup_sb[0:64, 66:67]))
            for cb in range(34):
                s_tail.inc(e.activation(out=xbc[:, cb * TH: cb * TH + 3],
                                        in_=ps_tail[:, cb * 3: cb * 3 + 3], func=AF.Copy,
                                        scale=mup_sb[:, 32 + cb:33 + cb]))

    QT = []
    for pj in range(3):
        for t0 in range(22):
            lo = t0 * 2048
            wdt = min(2048, 43712 - lo)
            QT.append((pj, lo, wdt))
    qsrc = [wg_t, wu_t, wd_t]
    qdst = [qg_hbm, qu_hbm, qd_hbm]

    def _qflat(t):
        a = t.ap() if not isinstance(t, bass.AP) else t
        return a.rearrange("j a b -> (j a b)").rearrange("(p f) -> p f", p=128)

    @B.sync
    def _(e):
        e.wait_ge(s_thr.h, s_thr.base + (3))
        e.wait_ge(s_xr.h, s_xr.base + (32))
        for i, (pj, lo, wdt) in enumerate(QT):
            if i >= 2:
                e.wait_ge(s_qrd.h, s_qrd.base + (i - 1))
            qld.put(i, e.dma_start(out=qin[:, (i % 2) * 2048:(i % 2) * 2048 + wdt],
                                   in_=_qflat(qsrc[pj])[:, lo:lo + wdt]))
            if i >= 1:
                pjp, lop, wdtp = QT[i - 1]
                e.wait_ge(s_qv.h, s_qv.base + (i))
                qst.put(i - 1, e.dma_start(out=_qflat(qdst[pjp])[:, lop:lop + wdtp],
                                           in_=qout[:, ((i - 1) % 2) * 2048:((i - 1) % 2) * 2048 + wdtp]))
        pjp, lop, wdtp = QT[-1]
        e.wait_ge(s_qv.h, s_qv.base + (len(QT)))
        qst.put(len(QT) - 1, e.dma_start(out=_qflat(qdst[pjp])[:, lop:lop + wdtp],
                                         in_=qout[:, ((len(QT) - 1) % 2) * 2048:((len(QT) - 1) % 2) * 2048 + wdtp]))

    @B.vector
    def _(e):
        thr = [thr_g, thr_u, thr_d]
        for i, (pj, lo, wdt) in enumerate(QT):
            qld.wait(e, i)
            if i >= 1:
                e.wait_ge(s_qv.h, s_qv.base + (i))
            if i >= 2:
                qst.wait(e, i - 2)
            src = qin[:, (i % 2) * 2048:(i % 2) * 2048 + wdt]
            e.tensor_scalar(out=qpos[:, 0:wdt], in0=src, scalar1=thr[pj][:, 0:1],
                            scalar2=None, op0=ALU.is_gt)
            inst = e.tensor_scalar(out=qneg[:, 0:wdt], in0=src, scalar1=thr[pj][:, 1:2],
                                   scalar2=None, op0=ALU.is_lt)
            s_qrd.inc(inst)
            e.wait_ge(s_qrd.h, s_qrd.n)
            inst = e.tensor_sub(out=qout[:, (i % 2) * 2048:(i % 2) * 2048 + wdt],
                                in0=qpos[:, 0:wdt], in1=qneg[:, 0:wdt])
            s_qv.inc(inst)


    quiesce_and_free("xin_st_0", "xin_st_1", "s_xpre", "s_mmp", "s_rows", "s_bc",
                     "xin2_st_0", "xin2_st_1", "s_xnv", "s_xns", "xnh_st_0", "xnh_st_1",
                     "s_rmsp", "s_rms", "s_xr", "s_wred", "s_scbc")
    s_dtc = S("s_dtc")
    s_cnvd = S("s_cnvd")
    s_cnvp = S("s_cnvp")
    s_cnva = S("s_cnva")
    s_dec = S("s_dec")
    s_lcch = S("s_lcch")
    s_lcqa = S("s_lcqa")
    lcrow_st = Stream("lcrow_st", 8)
    cq_st = Stream("cq_st", 8)

    def tapwait(e):
        for nm, sm in list(sems.items()):
            if nm.startswith("s_dbg_"):
                e.wait_ge(sm.h, sm.n)

    def conv_body(e, ename, chunks, base, smine):
        e.wait_ge(s_evoc.h, s_evoc.base + (67))
        e.wait_ge(s_tail.h, s_tail.base + (34))
        e.wait_ge(s_const.h, s_const.n)
        for idx, cb in enumerate(chunks):
            if idx >= 2:
                e.wait_ge(s_cnva.h, s_cnva.base + (chunks[idx - 2] + 1))
            pair = idx % 2
            a = csc[base + 2 * pair]
            b = csc[base + 2 * pair + 1]
            fence(e, ename, e.tensor_scalar(out=a[:, 0:T], in0=xbc[:, cb * TH: cb * TH + T],
                                            scalar1=cw_sb[:, cb * 4: cb * 4 + 1], scalar2=None,
                                            op0=ALU.mult))
            fence(e, ename, e.scalar_tensor_tensor(out=b[:, 0:T], in0=xbc[:, cb * TH + 1: cb * TH + 1 + T],
                                                   scalar=cw_sb[:, cb * 4 + 1: cb * 4 + 2], in1=a[:, 0:T],
                                                   op0=ALU.mult, op1=ALU.add))
            fence(e, ename, e.scalar_tensor_tensor(out=a[:, 0:T], in0=xbc[:, cb * TH + 2: cb * TH + 2 + T],
                                                   scalar=cw_sb[:, cb * 4 + 2: cb * 4 + 3], in1=b[:, 0:T],
                                                   op0=ALU.mult, op1=ALU.add))
            inst = e.scalar_tensor_tensor(out=b[:, 0:T], in0=xbc[:, cb * TH + 3: cb * TH + 3 + T],
                                          scalar=cw_sb[:, cb * 4 + 3: cb * 4 + 4], in1=a[:, 0:T],
                                          op0=ALU.mult, op1=ALU.add)
            smine.inc(inst)

    CONV_D = list(range(34))
    CONV_P = []

    @B.vector
    def _(e):
        e.wait_ge(s_mmoc.h, s_mmoc.base + (67))
        e.wait_ge(s_evoc.h, s_evoc.base + (67))
        tapwait(e)
        inst = e.tensor_copy(out=dt_sb[:, :], in_=dt_tmp[:, :])
        s_dtc.inc(inst)
        conv_body(e, "dve", CONV_D, 0, s_cnvd)

    s_hini = S("s_hini")

    @B.gpsimd
    def _(e):
        e.wait_ge(s_mmoc.h, s_mmoc.base + (67))
        tapwait(e)
        inst = e.memset(h_all[:], 0.0)
        s_hini.inc(inst)

    @B.scalar
    def _(e):
        e.wait_ge(s_qv.h, s_qv.base + 66)
        for _s in qst.sems:
            e.wait_ge(_s.h, _s.n)
        for _s in zhst.sems:
            e.wait_ge(_s.h, _s.n)
        e.wait_ge(s_dtc.h, s_dtc.base + 1)
        e.wait_ge(s_mmoc.h, s_mmoc.base + 67)
        for cb in range(34):
            e.wait_ge(s_cnvd.h, s_cnvd.base + (cb + 1))
            b = csc[2 * (cb % 2) + 1]
            inst = e.activation(out=convo[:, cb * T:(cb + 1) * T], in_=b[:, 0:T], func=AF.Silu)
            s_cnva.inc(inst)

    @B.vector
    def _(e):
        e.wait_ge(s_dtc.h, s_dtc.base + (1))
        inst = e.tensor_scalar(out=dt_sb[:, :], in0=dt_sb[:, :], scalar1=dtb_col[:, 0:1],
                               scalar2=None, op0=ALU.add)
        s_dec.inc(inst)
        e.wait_ge(s_dec.h, s_dec.n)
        inst = e.tensor_scalar(out=lc_sb[:, :], in0=dt_sb[:, :], scalar1=0.0, scalar2=None,
                               op0=ALU.max)
        s_dec.inc(inst)

    @B.scalar
    def _(e):
        e.wait_ge(s_dec.h, s_dec.base + (2))
        fence(e, "act", e.activation(out=lt_sb[:, :], in_=dt_sb[:, :], func=AF.Abs))
        fence(e, "act", e.activation(out=lt_sb[:, :], in_=lt_sb[:, :], func=AF.Exp, scale=-1.0))
        inst = e.activation(out=lt_sb[:, :], in_=lt_sb[:, :], func=AF.Ln, bias=1.0)
        s_dec.inc(inst)

    @B.vector
    def _(e):
        e.wait_ge(s_dec.h, s_dec.base + (3))
        inst = e.tensor_add(out=dt_sb[:, :], in0=lt_sb[:, :], in1=lc_sb[:, :])
        s_dec.inc(inst)
        e.wait_ge(s_dec.h, s_dec.n)
        e.wait_ge(s_cder.h, s_cder.base + (1))
        inst = e.tensor_scalar(out=lt_sb[:, :], in0=dt_sb[:, :], scalar1=acol[:, 0:1],
                               scalar2=None, op0=ALU.mult)
        s_dec.inc(inst)
        e.wait_ge(s_dec.h, s_dec.n)
        for ch in range(NCH):
            inst = e.tensor_tensor_scan(out=lc_sb[:, ch * Q:(ch + 1) * Q], data0=onesQ[:, :],
                                        data1=lt_sb[:, ch * Q:(ch + 1) * Q], initial=0.0,
                                        op0=ALU.mult, op1=ALU.add)
            s_lcch.inc(inst)
            e.wait_ge(s_lcch.h, s_lcch.n)
            inst = e.tensor_add(out=lcq_acc[:, 0:1], in0=lcq_acc[:, 0:1],
                                in1=lc_sb[:, ch * Q + Q - 1: ch * Q + Q])
            s_lcqa.inc(inst)
            e.wait_ge(s_lcqa.h, s_lcqa.n)

    @B.sync
    def _(e):
        for ch in range(NCH):
            e.wait_ge(s_lcch.h, s_lcch.base + (ch + 1))
            lcrow_st.put(ch, e.dma_start(out=lcrow.ap()[ch:ch + 1, :],
                                         in_=lc_sb[:, ch * Q:(ch + 1) * Q]))
            lcrow_st.wait(e, ch)
            cq_st.put(ch, e.dma_start(out=cq_dram.ap()[ch:ch + 1, :],
                                      in_=lcrow.ap()[ch, Q - 1::Q][None, :],
                                      allow_slow_non_contiguous=True))

    with nc.psum_tensor("tp16a", [128, 128], F16) as tp16a, \
         nc.psum_tensor("tp16b", [128, 128], F16) as tp16b, \
         nc.psum_tensor("tp32", [128, 64], F32) as tp32, \
         nc.psum_tensor("sbct_ps", [128, 128], F32) as sbct_ps, \
         nc.psum_tensor("psY0", [128, 128], F32) as psY0, \
         nc.psum_tensor("psY1", [128, 128], F32) as psY1, \
         nc.psum_tensor("psH0", [128, 64], F32) as psH0, \
         nc.psum_tensor("psH1", [128, 64], F32) as psH1:
        tp16 = [tp16a, tp16b]
        psY = [psY0, psY1]
        psH = [psH0, psH1]

        def emit_scan_pass(pfx, full):
            tr16 = S(pfx + "tr16")
            t16 = S(pfx + "t16")
            tr32 = S(pfx + "tr32")
            t32 = S(pfx + "t32")
            wsc0 = S(pfx + "wsc0")
            wsc1 = S(pfx + "wsc1")
            wscv = S(pfx + "wscv")
            cq = S(pfx + "cq")
            xsw_s = S(pfx + "xsw")
            psh = S(pfx + "psh")
            hstt = S(pfx + "hstt")
            n32 = 3 if full else 1
            psHp = psH if full else [psH0, psH1, psY0, psY1]
            DPH = len(psHp)
            LAG = 0 if full else 3
            if full:
                sbctmm = S(pfx + "sbctmm")
                sbcte = S(pfx + "sbcte")
                cmbL = S(pfx + "cmbL")
                cmb = S(pfx + "cmb")
                cwx = S(pfx + "cwx")
                cwm = S(pfx + "cwm")
                bsub = S(pfx + "bsub")
                mbx = S(pfx + "mbx")
                msk = S(pfx + "msk")
                m_s = S(pfx + "m")
                y_s = S(pfx + "y")
                yev = S(pfx + "yev")
                lcb_st = Stream(pfx + "lcb", 2)
            else:
                cqb_st = Stream(pfx + "cqb", 2)

            @B.sync
            def _(e):
                if full:
                    e.wait_ge(s_gath.h, s_gath.base + (2))
                    for ch in range(NCH):
                        for hl in range(2):
                            g2 = 2 * ch + hl
                            if g2 >= 1:
                                e.wait_ge(y_s.h, y_s.base + (32 * g2))
                            lcb_st.put(g2, e.dma_start(
                                out=lcb[:, :],
                                in_=lcrow.ap()[ch, hl * 4096:(hl + 1) * 4096][None, :].partition_broadcast(128)))
                else:
                    for ch in range(NCH):
                        cq_st.wait(e, ch)
                        if ch >= 1:
                            e.wait_ge(hstt.h, hstt.base + (ch * 64))
                        cqb_st.put(ch, e.dma_start(
                            out=cumq[:, 0:64],
                            in_=cq_dram.ap()[ch, :][None, :].partition_broadcast(128)))

            @B.tensor
            def _(e):
                e.wait_ge(s_cnva.h, s_cnva.base + (34))
                if full:
                    e.wait_ge(s_gath.h, s_gath.base + (2))
                for ch in range(NCH):
                    for j in range(33):
                        gt = ch * 33 + j
                        if gt >= 2:
                            e.wait_ge(t16.h, t16.base + (gt - 1))
                        src_off = j * T + ch * 128
                        inst = e.transpose(tp16[gt % 2][:, :], convo[:, src_off:src_off + 128],
                                           ident16[:])
                        tr16.inc(inst)
                    f32srcs = []
                    if full:
                        f32srcs = [dt_sb[:, ch * Q:(ch + 1) * Q], lc_sb[:, ch * Q:(ch + 1) * Q]]
                    for jj, srcv in enumerate(f32srcs):
                        g32 = ch * n32 + jj
                        if g32 >= 1:
                            e.wait_ge(t32.h, t32.base + (g32))
                        inst = e.transpose(tp32[:, 0:64], srcv, ident32[0:64, 0:64])
                        tr32.inc(inst)
                    e.wait_ge(wscv.h, wscv.base + (ch + 1))
                    g32 = ch * n32 + n32 - 1
                    if g32 >= 1:
                        e.wait_ge(t32.h, t32.base + (g32))
                    inst = e.transpose(tp32[:, 0:64], wsc_t[:, :], ident32[0:64, 0:64])
                    tr32.inc(inst)
                    if full:
                        if ch >= 1:
                            e.wait_ge(sbcte.h, sbcte.base + (ch))
                        inst = e.matmul(sbct_ps[:, :], convo[:, 32 * T + ch * 128: 32 * T + ch * 128 + 128],
                                        convo[:, 33 * T + ch * 128: 33 * T + ch * 128 + 128],
                                        start=True, stop=True)
                        sbctmm.inc(inst)
                    for r in range(64):
                        g = ch * 64 + r
                        hl = r // 32
                        if full:
                            p = r // 2
                            gp = ch * 32 + p
                            e.wait_ge(m_s.h, m_s.base + (2 * ch + hl + 1))
                            e.wait_ge(t16.h, t16.base + (ch * 33 + 33))
                            if gp >= 2:
                                e.wait_ge(yev.h, yev.base + (gp - 1))
                            e.matmul(psY[gp % 2][64 * (r % 2):64 * (r % 2) + 64, :],
                                     xst[:, (r // 2) * 128 + 64 * (r % 2): (r // 2) * 128 + 64 * (r % 2) + 64],
                                     mfin[:, (r % 32) * 128:(r % 32) * 128 + 128],
                                     start=True, stop=False,
                                     tile_position=(0, 64 * (r % 2)))
                            e.wait_ge(cwm.h, cwm.base + (2 * ch + hl + 1))
                            e.wait_ge(hstt.h, hstt.base + (ch * 64))
                            inst = e.matmul(psY[gp % 2][64 * (r % 2):64 * (r % 2) + 64, :],
                                            h_all[:, r * 64:(r + 1) * 64],
                                            cwt[:, (r % 32) * 128:(r % 32) * 128 + 128],
                                            start=False, stop=True,
                                            tile_position=(0, 64 * (r % 2)))
                            y_s.inc(inst)
                        e.wait_ge(xsw_s.h, xsw_s.base + (g + 1))
                        if g >= DPH:
                            e.wait_ge(hstt.h, hstt.base + (g - (DPH - 1)))
                        inst = e.matmul(psHp[r % DPH][:, 0:64], btok[:, :], xsw[:, (r % 4) * 64:(r % 4) * 64 + 64],
                                        start=True, stop=True)
                        psh.inc(inst)

            @B.scalar
            def _(e):
                if full:
                    e.wait_ge(s_gath.h, s_gath.base + (2))
                for ch in range(NCH):
                    for j in range(33):
                        gt = ch * 33 + j
                        e.wait_ge(tr16.h, tr16.base + (gt + 1))
                        if j < 32:
                            dst = xst[:, j * 128:(j + 1) * 128]
                        else:
                            dst = btok[:, :]
                        if full and gt >= 1 and j == 0:
                            e.wait_ge(psh.h, psh.base + (ch * 64))
                            if full:
                                e.wait_ge(y_s.h, y_s.base + (ch * 64))
                        if (not full) and gt >= 1 and j == 0:
                            e.wait_ge(psh.h, psh.base + (ch * 64))
                        inst = e.activation(out=dst, in_=tp16[gt % 2][:, :], func=AF.Copy)
                        t16.inc(inst)
                    e.wait_ge(wsc0.h, wsc0.base + (ch + 1))
                    inst = e.activation(out=wsc_t[:, :], in_=wsc_t[:, :], func=AF.Exp, scale=-1.0)
                    wsc1.inc(inst)
                    dsts = ([dtT, lcTn] if full else []) + [wT]
                    if ch >= 1:
                        e.wait_ge(xsw_s.h, xsw_s.base + (ch * 64))
                        if full:
                            e.wait_ge(cmbL.h, cmbL.base + (ch))
                            e.wait_ge(cmb.h, cmb.base + (ch))
                    for jj, dstv in enumerate(dsts):
                        g32 = ch * n32 + jj
                        e.wait_ge(tr32.h, tr32.base + (g32 + 1))
                        if full and dstv is lcTn:
                            inst = e.activation(out=dstv[:, 0:64], in_=tp32[:, 0:64], func=AF.Copy,
                                                scale=-1.0)
                        else:
                            inst = e.activation(out=dstv[:, 0:64], in_=tp32[:, 0:64], func=AF.Copy)
                        t32.inc(inst)
                    if full:
                        e.wait_ge(t32.h, t32.base + (ch * 3 + 1))
                        if ch >= 1:
                            e.wait_ge(mbx.h, mbx.base + (64 * ch))
                        inst = e.activation(out=combT[:, 0:64], in_=dtT[:, 0:64], func=AF.Ln)
                        cmbL.inc(inst)
                        e.wait_ge(sbctmm.h, sbctmm.base + (ch + 1))
                        inst = e.activation(out=sbct[:, :], in_=sbct_ps[:, :], func=AF.Copy)
                        sbcte.inc(inst)
                        for hl in range(2):
                            g2 = 2 * ch + hl
                            lcb_st.wait(e, g2)
                            e.wait_ge(m_s.h, m_s.base + (g2))
                            inst = e.activation(out=mbig[:, :], in_=lcb[:, :], func=AF.Exp)
                            cwx.inc(inst)
                            e.wait_ge(cwx.h, cwx.n)
                            if ch >= 1:
                                e.wait_ge(hstt.h, hstt.base + ((ch - 1)) * 64 + hl * 32 + 32)
                            ecv = bass.AP(mbig.ap().tensor, 127,
                                          [[4096, 128], [128, 32]])
                            inst = e.activation(out=cumq[:, hl * 32:(hl + 1) * 32], in_=ecv,
                                                func=AF.Copy)
                            cq.inc(inst)
                            e.wait_ge(cq.h, cq.n)
                            e.wait_ge(cwm.h, cwm.base + (g2 + 1))
                            e.wait_ge(msk.h, msk.base + (g2 + 1))
                            e.wait_ge(cmb.h, cmb.base + (ch + 1))
                            for r in range(hl * 32, hl * 32 + 32):
                                inst = e.activation(out=mbig[:, (r % 32) * 128:(r % 32) * 128 + 128],
                                                    in_=lcb[:, (r % 32) * 128:(r % 32) * 128 + 128],
                                                    func=AF.Exp, bias=combT[:, r:r + 1])
                                mbx.inc(inst)
                    else:
                        cqb_st.wait(e, ch)
                        inst = e.activation(out=cumq[:, 0:64], in_=cumq[:, 0:64], func=AF.Exp)
                        cq.inc(inst)
                        for r in range(64):
                            g = ch * 64 + r
                            if g >= 4:
                                e.wait_ge(psh.h, psh.base + (g - 3))
                            inst = e.activation(
                                out=xsw[:, (r % 4) * 64:(r % 4) * 64 + 64],
                                in_=xst[:, (r // 2) * 128 + 64 * (r % 2): (r // 2) * 128 + 64 * (r % 2) + 64],
                                func=AF.Copy, scale=wT[:, r:r + 1])
                            xsw_s.inc(inst)

            if full:
                @B.gpsimd
                def _(e):
                    e.wait_ge(s_gath.h, s_gath.base + (2))
                    for ch in range(NCH):
                        for hl in range(2):
                            g2 = 2 * ch + hl
                            e.wait_ge(cwx.h, cwx.base + (g2 + 1))
                            inst = e.affine_select(out=lcb[:, :], in_=lcb[:, :],
                                                   pattern=[[0, 32], [1, 128]],
                                                   compare_op=ALU.is_ge,
                                                   fill=-1.0e30, base=0,
                                                   channel_multiplier=-1)
                            msk.inc(inst)

            @B.vector
            def _(e):
                if full:
                    e.wait_ge(s_gath.h, s_gath.base + (2))
                    e.wait_ge(s_cder2.h, s_cder2.base + (1))
                else:
                    e.wait_ge(s_hini.h, s_hini.base + (1))
                e.wait_ge(s_lcch.h, s_lcch.base + (8))
                for ch in range(NCH):
                    if ch >= 1:
                        e.wait_ge(tr32.h, tr32.base + (ch * n32))
                    fence(e, "dve", e.tensor_scalar(out=wsc_t[:, :], in0=lc_sb[:, ch * Q:(ch + 1) * Q],
                                                    scalar1=lc_sb[:, ch * Q + Q - 1: ch * Q + Q],
                                                    scalar2=None, op0=ALU.subtract))
                    inst = e.tensor_copy(out=tcol[0:64, 0:1], in_=wsc_t[:, 0:1])
                    wsc0.inc(inst)
                    e.wait_ge(wsc1.h, wsc1.base + (ch + 1))
                    inst = e.tensor_mul(out=wsc_t[:, :], in0=wsc_t[:, :], in1=dt_sb[:, ch * Q:(ch + 1) * Q])
                    wscv.inc(inst)
                    e.wait_ge(wscv.h, wscv.n)
                    if full:
                        e.wait_ge(cmbL.h, cmbL.base + (ch + 1))
                        e.wait_ge(t32.h, t32.base + (ch * 3 + 2))
                        inst = e.tensor_add(out=combT[:, 0:64], in0=combT[:, 0:64],
                                            in1=lcTn[:, 0:64])
                        cmb.inc(inst)
                    if full:
                        for hl in range(2):
                            g2 = 2 * ch + hl
                            e.wait_ge(cwx.h, cwx.base + (g2 + 1))
                            e.wait_ge(y_s.h, y_s.base + (32 * g2))
                            cbc = convo.ap()[:, 33 * T + ch * 128: 33 * T + ch * 128 + 128]
                            cbc = cbc.unsqueeze(1).broadcast_to([128, 32, 128])
                            inst = e.tensor_tensor(
                                out=cwt.ap().rearrange("p (a b) -> p a b", b=128),
                                in0=cbc,
                                in1=mbig.ap().rearrange("p (a b) -> p a b", b=128),
                                op=ALU.mult)
                            cwm.inc(inst)
                            e.wait_ge(mbx.h, mbx.base + (32 * (g2 + 1)))
                            e.wait_ge(sbcte.h, sbcte.base + (ch + 1))
                            inst = e.tensor_tensor(
                                out=mfin.ap().rearrange("p (a b) -> p a b", b=128),
                                in0=sbct.ap().unsqueeze(1).broadcast_to([128, 32, 128]),
                                in1=mbig.ap().rearrange("p (a b) -> p a b", b=128),
                                op=ALU.mult)
                            m_s.inc(inst)
                            e.wait_ge(m_s.h, m_s.n)
                            for r in range(hl * 32, hl * 32 + 32):
                                g = ch * 64 + r
                                e.wait_ge(t16.h, t16.base + (ch * 33 + 33))
                                e.wait_ge(t32.h, t32.base + (ch * n32 + n32))
                                if g >= 4:
                                    e.wait_ge(psh.h, psh.base + (g - 3))
                                inst = e.tensor_scalar(
                                    out=xsw[:, (r % 4) * 64:(r % 4) * 64 + 64],
                                    in0=xst[:, (r // 2) * 128 + 64 * (r % 2): (r // 2) * 128 + 64 * (r % 2) + 64],
                                    scalar1=wT[:, r:r + 1], scalar2=None, op0=ALU.mult)
                                xsw_s.inc(inst)
                                e.wait_ge(psh.h, psh.base + (g + 1))
                                e.wait_ge(cq.h, cq.base + (2 * ch + hl + 1))
                                e.wait_ge(y_s.h, y_s.base + (g + 1))
                                inst = e.scalar_tensor_tensor(out=h_all[:, r * 64:(r + 1) * 64],
                                                              in0=h_all[:, r * 64:(r + 1) * 64],
                                                              scalar=cumq[:, r:r + 1],
                                                              in1=psH[r % 2][:, 0:64],
                                                              op0=ALU.mult, op1=ALU.add)
                                hstt.inc(inst)
                                e.wait_ge(hstt.h, hstt.n)
                                if r % 2 == 1:
                                    p = r // 2
                                    gp = ch * 32 + p
                                    inst = e.scalar_tensor_tensor(
                                        out=yfeat[:, p * T + ch * 128: p * T + ch * 128 + 128],
                                        in0=convo[:, p * T + ch * 128: p * T + ch * 128 + 128],
                                        scalar=db_pair[:, p:p + 1],
                                        in1=psY[gp % 2][:, :],
                                        op0=ALU.mult, op1=ALU.add)
                                    yev.inc(inst)
                    else:
                        def h_upd(rr):
                            gg = ch * 64 + rr
                            e.wait_ge(psh.h, psh.base + (gg + 1))
                            e.wait_ge(cq.h, cq.base + (ch + 1))
                            inst = e.scalar_tensor_tensor(out=h_all[:, rr * 64:(rr + 1) * 64],
                                                          in0=h_all[:, rr * 64:(rr + 1) * 64],
                                                          scalar=cumq[:, rr:rr + 1],
                                                          in1=psHp[rr % DPH][:, 0:64],
                                                          op0=ALU.mult, op1=ALU.add)
                            hstt.inc(inst)
                        for r in range(64):
                            h_upd(r)
            return locals()

        s_gath = S("s_gath")
        s_cc = S("s_cc")
        pub_st = Stream("pub_st", 2)
        bsl_st = Stream("bsl_st", 2)
        gw_st = Stream("gw_st", 4)
        s_gmm = S("s_gmm")
        s_gev = S("s_gev")
        s_hac = S("s_hac")

        quiesce_and_free("qld_0", "qld_1", "qst_0", "qst_1", "s_qrd", "s_qv", "zhst_0",
                         "zhst_1", "s_mmoc", "s_evoc", "s_tail", "s_dec", "s_dtc")
        z = emit_scan_pass("z_", False)

        @B.sync
        def _(e):
            e.wait_ge(sems["z_hstt"].h, sems["z_hstt"].base + 512)
            pub_st.put(0, e.dma_start(out=ccb_in.ap()[:, :], in_=h_all[:, :]))
            e.wait_ge(s_lcqa.h, s_lcqa.base + (8))
            pub_st.put(1, e.dma_start(out=cclc_in.ap().rearrange("a b -> (a b)")[:, None],
                                      in_=lcq_acc[:, 0:1]))

        @B.gpsimd
        def _(e):
            pub_st.wait(e, 0)
            pub_st.wait(e, 1)
            e.collective_compute("AllGather", ALU.bypass, replica_groups=[list(range(cc_group))],
                                 ins=[ccb_in.ap().opt()], outs=[ccb_out.ap()[0:cc_group * NST, :].opt()]).then_inc(s_cc.h, 1)
            s_cc.n += 1
            e.collective_compute("AllGather", ALU.bypass, replica_groups=[list(range(cc_group))],
                                 ins=[cclc_in.ap().opt()], outs=[cclc_out.ap()[0:cc_group, :].opt()]).then_inc(s_cc.h, 1)
            s_cc.n += 1

        @B.sync
        def _(e):
            e.wait_ge(s_cc.h, s_cc.base + (2))
            gw_st.put(0, e.dma_start(out=lcall[:, :], in_=cclc_out.ap()[:, :]))

        @B.tensor
        def _(e):
            gw_st.wait(e, 0)
            e.wait_ge(s_cmem.h, s_cmem.base + (1))
            inst = e.matmul(psY0[0:8, 0:64], tri8[:, :], lcall[:, :], start=True, stop=True)
            inst = e.matmul(psY1[0:1, 0:64], mask_sb[:, 0:1], lcall[:, :], start=True, stop=True)
            s_gmm.inc(inst)

        @B.scalar
        def _(e):
            e.wait_ge(s_gmm.h, s_gmm.base + (1))
            fence(e, "act", e.activation(out=lcq8[:, :], in_=psY0[0:8, 0:64], func=AF.Copy))
            inst = e.activation(out=lcp[0:1, :], in_=psY1[0:1, 0:64], func=AF.Copy)
            s_gev.inc(inst)

        @B.tensor
        def _(e):
            e.wait_ge(s_gev.h, s_gev.base + (1))
            inst = e.matmul(psH0[0:8, 0:64], ones8[:, :], lcp[0:1, :], start=True, stop=True)
            s_gmm.inc(inst)

        @B.scalar
        def _(e):
            e.wait_ge(s_gmm.h, s_gmm.base + (2))
            inst = e.activation(out=wcore[:, :], in_=psH0[0:8, 0:64], func=AF.Copy)
            s_gev.inc(inst)

        @B.vector
        def _(e):
            e.wait_ge(s_gev.h, s_gev.base + (2))
            fence(e, "dve", e.tensor_sub(out=wcore[:, :], in0=wcore[:, :], in1=lcq8[:, :]))
            inst = e.tensor_scalar(out=wcore[:, :], in0=wcore[:, :], scalar1=0.0, scalar2=None,
                                   op0=ALU.min)
            s_gev.inc(inst)

        @B.scalar
        def _(e):
            e.wait_ge(s_gev.h, s_gev.base + (3))
            inst = e.activation(out=wcore[:, :], in_=wcore[:, :], func=AF.Exp)
            s_gev.inc(inst)

        @B.vector
        def _(e):
            e.wait_ge(s_gev.h, s_gev.base + (4))
            inst = e.tensor_scalar(out=wcore[:, :], in0=wcore[:, :], scalar1=mask_sb[:, 0:1],
                                   scalar2=None, op0=ALU.mult)
            s_gev.inc(inst)

        @B.sync
        def _(e):
            e.wait_ge(s_gev.h, s_gev.base + (5))
            gw_st.put(1, e.dma_start(out=rowbc.ap()[6:7, 0:512], in_=wcore[:, :]))
            gw_st.wait(e, 1)
            gw_st.put(2, e.dma_start(out=wful[:, :],
                                     in_=rowbc.ap()[6, 0:512][None, :].partition_broadcast(128)))
            for cp in range(8):
                if cp >= 2:
                    e.wait_ge(s_hac.h, s_hac.base + (cp - 1))
                bsl_st.put(cp, e.dma_start(out=bslice[:, (cp % 2) * 4096:(cp % 2) * 4096 + 4096],
                                           in_=ccb_out.ap()[cp * 128:(cp + 1) * 128, :]))

        @B.vector
        def _(e):
            gw_st.wait(e, 2)
            for cp in range(8):
                bsl_st.wait(e, cp)
                if cp >= 1:
                    e.wait_ge(s_hac.h, s_hac.base + (cp))
                inst = None
                for r in range(64):
                    bsl = bslice[:, (cp % 2) * 4096 + r * 64:(cp % 2) * 4096 + r * 64 + 64]
                    if cp == 0:
                        inst = e.tensor_scalar(out=hacc32[:, r * 64:(r + 1) * 64], in0=bsl,
                                               scalar1=wful[:, cp * 64 + r:cp * 64 + r + 1],
                                               scalar2=None, op0=ALU.mult)
                    else:
                        inst = e.scalar_tensor_tensor(out=hacc32[:, r * 64:(r + 1) * 64], in0=bsl,
                                                      scalar=wful[:, cp * 64 + r:cp * 64 + r + 1],
                                                      in1=hacc32[:, r * 64:(r + 1) * 64],
                                                      op0=ALU.mult, op1=ALU.add)
                s_hac.inc(inst)

        @B.scalar
        def _(e):
            e.wait_ge(s_hac.h, s_hac.base + (8))
            inst = e.activation(out=h_all[:, :], in_=hacc32[:, :], func=AF.Copy)
            s_gath.inc(inst, 1)
            e.wait_ge(s_gath.h, s_gath.n)
            inst = e.activation(out=tcol[:, 0:1], in_=h_all[:, 0:1], func=AF.Copy)
            s_gath.inc(inst, 1)

        quiesce_and_free("z_tr16", "z_t16", "z_tr32", "z_t32", "z_wsc0", "z_wsc1", "z_wscv",
                         "z_cq", "z_xsw", "z_psh", "z_hstt", "z_cqb_0", "z_cqb_1",
                         "pub_st_0", "pub_st_1", "bsl_st_0", "bsl_st_1",
                         "gw_st_0", "gw_st_1", "gw_st_2", "gw_st_3", "s_gmm", "s_gev", "s_hac",
                         "s_cnvd", "s_cnvp", "s_lcqa", "lcrow_st_0", "lcrow_st_1", "lcrow_st_2",
                         "lcrow_st_3", "lcrow_st_4", "lcrow_st_5", "lcrow_st_6", "lcrow_st_7",
                         "cq_st_0", "cq_st_1", "cq_st_2", "cq_st_3", "cq_st_4", "cq_st_5",
                         "cq_st_6", "cq_st_7")
        u = emit_scan_pass("u_", True)

    tz = tap("zx", [DS, T], F16)
    if tz is not None:
        @B.gpsimd
        def _(e):
            for _s in zhst.sems:
                e.wait_ge(_s.h, _s.n)
            sd = S("s_dbg_zx")
            sd.dma(e.dma_start(out=tz[:, :], in_=z_hbm.ap()[:, :]))
            e.wait_ge(sd.h, sd.base + (16))

    txb = tap("xbc", [128, 34 * TH], F16)
    if txb is not None:
        @B.sync
        def _(e):
            e.wait_ge(s_evoc.h, s_evoc.base + (67))
            e.wait_ge(s_tail.h, s_tail.base + (34))
            sd = S("s_dbg_xbc")
            sd.dma(e.dma_start(out=txb[:, :], in_=xbc[:, :]))
            e.wait_ge(sd.h, sd.base + (16))

    tdt = tap("dtraw", [NH, T], F32)
    if tdt is not None:
        @B.sync
        def _(e):
            e.wait_ge(s_evoc.h, s_evoc.base + (67))
            sd = S("s_dbg_dtraw")
            sd.dma(e.dma_start(out=tdt[:, :], in_=dt_tmp[:, :]))
            e.wait_ge(sd.h, sd.base + (16))

    tq = tap("qg", [8, D8, H8], F16)
    if tq is not None:
        @B.gpsimd
        def _(e):
            for _s in qst.sems:
                e.wait_ge(_s.h, _s.n)
            sd = S("s_dbg_qg")
            sd.dma(e.dma_start(out=tq[:, :, :], in_=qg_hbm.ap()[:, :, :]))
            e.wait_ge(sd.h, sd.base + (16))

    t = tap("xr", [128, 32 * TH], F16)
    if t is not None:
        @B.sync
        def _(e):
            e.wait_ge(s_xr.h, s_xr.base + (32))
            sd = S("s_dbg_xr")
            sd.dma(e.dma_start(out=t[:, :], in_=xr[:, :]))
            e.wait_ge(sd.h, sd.base + (16))

    t2 = tap("xn", [H, T], F16)
    if t2 is not None:
        @B.gpsimd
        def _(e):
            for _s in xnh_st.sems:
                e.wait_ge(_s.h, _s.n)
            sd = S("s_dbg_xn")
            sd.dma(e.dma_start(out=t2[:, :], in_=xn_hbm.ap()[:, :]))
            e.wait_ge(sd.h, sd.base + (16))


    ty = tap("yfeat", [128, 32 * T], F16)
    if ty is not None:
        @B.sync
        def _(e):
            e.wait_ge(sems["u_yev"].h, sems["u_yev"].base + 256)
            sd = S("s_dbg_yfeat")
            sd.dma(e.dma_start(out=ty[:, :], in_=yfeat[:, :]))
            e.wait_ge(sd.h, sd.base + 16)


    quiesce_and_free("u_tr16", "u_t16", "u_tr32", "u_t32", "u_wsc0", "u_wsc1",
                     "u_wscv", "u_cq", "u_xsw", "u_psh", "u_hstt", "u_sbctmm",
                     "u_sbcte", "u_cmbL", "u_cmb", "u_cwx", "u_cwm", "u_bsub",
                     "u_mbx", "u_msk", "u_m",
                     "u_y", "u_yev", "u_lcb_0", "u_lcb_1", "s_gath", "s_hini",
                     "s_cnva", "s_lcch")
    zld = Stream("zld", 2)
    ynh = Stream("ynh", 2)
    s_sz = S("s_sz")
    s_ygv = S("s_ygv")
    s_sq = S("s_sq")
    s_sqm = S("s_sqm")
    s_rst = S("s_rst")
    s_ynv = S("s_ynv")

    with nc.psum_tensor("ps_ssq", [1, T], F32) as ps_ssq:

        @B.sync
        def _(e):
            for cc in range(32):
                if cc >= 2:
                    e.wait_ge(s_sz.h, s_sz.base + cc - 1)
                zld.put(cc, e.dma_start(out=zst[:, (cc % 2) * T:(cc % 2) * T + T],
                                        in_=z_hbm.ap()[cc * 128:(cc + 1) * 128, :]))

        @B.scalar
        def _(e):
            for cc in range(32):
                zld.wait(e, cc)
                if cc >= 2:
                    e.wait_ge(s_ygv.h, s_ygv.base + cc - 1)
                inst = e.activation(out=sgl[:, (cc % 2) * T:(cc % 2) * T + T],
                                    in_=zst[:, (cc % 2) * T:(cc % 2) * T + T], func=AF.Silu)
                s_sz.inc(inst)
                e.wait_ge(s_ygv.h, s_ygv.base + cc + 1)
                if cc >= 2:
                    e.wait_ge(s_sqm.h, s_sqm.base + cc - 1)
                inst = e.activation(out=sqb[:, (cc % 2) * T:(cc % 2) * T + T],
                                    in_=yg[:, cc * T:(cc + 1) * T], func=AF.Square)
                s_sq.inc(inst)

        @B.vector
        def _(e):
            for cc in range(32):
                e.wait_ge(s_sz.h, s_sz.base + cc + 1)
                inst = e.tensor_mul(out=yg[:, cc * T:(cc + 1) * T],
                                    in0=yfeat[:, cc * T:(cc + 1) * T],
                                    in1=sgl[:, (cc % 2) * T:(cc % 2) * T + T])
                s_ygv.inc(inst)
                e.wait_ge(s_ygv.h, s_ygv.n)

        @B.tensor
        def _(e):
            inst = None
            for cc in range(32):
                e.wait_ge(s_sq.h, s_sq.base + cc + 1)
                for (lo, n) in ((0, 512), (512, 512)):
                    inst = e.matmul(ps_ssq[:, lo:lo + n], ones16_col[:],
                                    sqb[:, (cc % 2) * T + lo:(cc % 2) * T + lo + n],
                                    start=(cc == 0), stop=(cc == 31))
                s_sqm.inc(inst)

        @B.scalar
        def _(e):
            e.wait_ge(s_sqm.h, s_sqm.base + 32)
            inst = e.activation(out=row_d[0:1, 0:T], in_=ps_ssq[:, :], func=AF.Sqrt,
                                scale=1.0 / DS, bias=EPS)
            s_rst.inc(inst)

        @B.vector
        def _(e):
            e.wait_ge(s_rst.h, s_rst.base + 1)
            inst = e.reciprocal(out=row_d[0:1, 0:T], in_=row_d[0:1, 0:T])
            s_rst.inc(inst)

        @B.sync
        def _(e):
            e.wait_ge(s_rst.h, s_rst.base + 2)
            s_rst.inc(e.dma_start(out=rowbc.ap()[7:8, 0:T], in_=row_d[0:1, 0:T]), 16)
            e.wait_ge(s_rst.h, s_rst.n)
            s_rst.inc(e.dma_start(out=rstd2_bc[:, :],
                                  in_=rowbc.ap()[7, 0:T][None, :].partition_broadcast(128)), 16)

        @B.vector
        def _(e):
            e.wait_ge(s_rst.h, s_rst.n)
            for cc in range(32):
                if cc >= 2:
                    ynh.wait(e, cc - 2)
                inst = e.scalar_tensor_tensor(out=ynstg[:, (cc % 2) * T:(cc % 2) * T + T],
                                              in0=yg[:, cc * T:(cc + 1) * T],
                                              scalar=nw_sb[:, cc:cc + 1],
                                              in1=rstd2_bc[:, :],
                                              op0=ALU.mult, op1=ALU.mult)
                s_ynv.inc(inst)
                e.wait_ge(s_ynv.h, s_ynv.n)

        @B.sync
        def _(e):
            for cc in range(32):
                e.wait_ge(s_ynv.h, s_ynv.base + cc + 1)
                ynh.put(cc, e.dma_start(out=yn_hbm.ap()[cc * 128:(cc + 1) * 128, :],
                                        in_=ynstg[:, (cc % 2) * T:(cc % 2) * T + T]))

    quiesce_and_free("zld_0", "zld_1", "s_sz", "s_ygv", "s_sq", "s_sqm", "s_rst",
                     "s_ynv", "ynh_0", "ynh_1")
    TQ = 256
    NMC = 4 * NOI * 8
    xnl = Stream("xnl", 1)
    s_win = [S("s_win0"), S("s_win1")]
    s_uwin = [S("s_uwin0", fresh=True), S("s_uwin1", fresh=True)]
    s_dwin = S("s_dwin")
    s_gu = S("s_gu")
    a_ev = S("a_ev")
    v_ev = S("v_ev")
    s_hq = S("s_hq")
    s_dmm = S("s_dmm")
    a_dv = S("a_dv")
    v_dv = S("v_dv")
    s_gd = S("s_gd")
    gst = Stream("gst", 2)

    def win_dma(e, st, dst, src_t, oi):
        ow, wd = OCH[oi]
        for j in range(8):
            for db in range(4):
                st.inc(e.dma_start(out=dst[:, (j * 4 + db) * wd:(j * 4 + db) * wd + wd],
                                   in_=src_t[j, db * 128:(db + 1) * 128, ow:ow + wd]), 16)

    def dwin_dma(e, st, ocw, par):
        for j in range(8):
            for dd in range(NOI):
                do, cnt = OCH[dd]
                st.inc(e.dma_start(out=qwin_d[par][0:cnt, (j * NOI + dd) * 128:(j * NOI + dd) * 128 + 128],
                                   in_=qd_hbm.ap()[j, do:do + cnt, ocw * 128:(ocw + 1) * 128]), 16)

    @B.sync
    def _(e):
        xnl.put(0, e.dma_start(
            out=xn_sb[:, :].rearrange("p (cc t) -> p cc t", t=T),
            in_=xn_hbm.ap().rearrange("(cc p) t -> p cc t", p=128)))
        for qt in range(4):
            for oi in range(NOI):
                gi = qt * NOI + oi
                if gi >= 2:
                    e.wait_ge(s_gu.h, s_gu.base + ((gi - 1) * 8))
                win_dma(e, s_win[gi % 2], qwin_g[gi % 2], qg_hbm.ap(), oi)

    @B.gpsimd
    def _(e):
        for qt in range(4):
            for oi in range(NOI):
                gi = qt * NOI + oi
                if gi >= 2:
                    e.wait_ge(s_gu.h, s_gu.base + ((gi - 1) * 8))
                win_dma(e, s_uwin[gi % 2], qwin_u[gi % 2], qu_hbm.ap(), oi)
            for ocw in range(4):
                di = qt * 4 + ocw
                if di >= 2:
                    e.wait_ge(s_dmm.h, s_dmm.base + ((di - 1) * 8))
                dwin_dma(e, s_dwin, ocw, di % 2)

    with nc.psum_tensor("ps_gp0", [128, TQ], F32) as ps_gp0, \
         nc.psum_tensor("ps_gp1", [128, TQ], F32) as ps_gp1, \
         nc.psum_tensor("ps_gn0", [128, TQ], F32) as ps_gn0, \
         nc.psum_tensor("ps_gn1", [128, TQ], F32) as ps_gn1, \
         nc.psum_tensor("ps_up0", [128, TQ], F32) as ps_up0, \
         nc.psum_tensor("ps_up1", [128, TQ], F32) as ps_up1, \
         nc.psum_tensor("ps_un0", [128, TQ], F32) as ps_un0, \
         nc.psum_tensor("ps_un1", [128, TQ], F32) as ps_un1:
        ps_gp = [ps_gp0, ps_gp1]
        ps_gn = [ps_gn0, ps_gn1]
        ps_up = [ps_up0, ps_up1]
        ps_un = [ps_un0, ps_un1]

        @B.tensor
        def _(e):
            xnl.wait(e, 0)
            for qt in range(4):
                if qt >= 1:
                    e.wait_ge(s_gd.h, s_gd.base + (qt * 32))
                for oi in range(NOI):
                    gi = qt * NOI + oi
                    ow, wd = OCH[oi]
                    e.wait_ge(s_win[gi % 2].h, s_win[gi % 2].base + 512 * (gi // 2 + 1))
                    e.wait_ge(s_uwin[gi % 2].h, s_uwin[gi % 2].base + 512 * (gi // 2 + 1))
                    for kg in range(8):
                        mi = gi * 8 + kg
                        par = mi % 2
                        if mi >= 2:
                            e.wait_ge(s_hq.h, s_hq.base + (mi - 1))
                        inst = None
                        for pj in range(2):
                            qs = (qwin_g, qwin_u)[pj]
                            pp = (ps_gp, ps_up)[pj][par]
                            pn = (ps_gn, ps_un)[pj][par]
                            for kc in range(32):
                                i = kc // 4
                                j = int(JTAB[i, kg])
                                db = kc % 4
                                pos = STAB[i, kg] > 0
                                lst = GU_POS[kg] if pos else GU_NEG[kg]
                                inst = e.matmul((pp if pos else pn)[0:wd, :],
                                                qs[gi % 2][:, (j * 4 + db) * wd:(j * 4 + db) * wd + wd],
                                                xn_sb[:, kc * T + qt * TQ: kc * T + qt * TQ + TQ],
                                                start=(kc == lst[0]), stop=(kc == lst[-1]))
                        s_gu.inc(inst)
                for ocw in range(4):
                    di = qt * 4 + ocw
                    e.wait_ge(s_dwin.h, s_dwin.base + 16 * 88 * (di + 1))
                    e.wait_ge(s_hq.h, s_hq.base + ((qt + 1) * NOI * 8))
                    for kg in range(8):
                        ni = di * 8 + kg
                        par = ni % 2
                        if ni >= 2:
                            e.wait_ge(s_gd.h, s_gd.base + (ni - 1))
                        inst = None
                        for i in range(8):
                            j = int(JTAB[i, kg])
                            pos = STAB[i, kg] > 0
                            lst = D_POS[kg] if pos else D_NEG[kg]
                            dst = (ps_gp if pos else ps_gn)[par]
                            for dd in range(NOI):
                                do, cnt = OCH[dd]
                                inst = e.matmul(dst[:, :],
                                                qwin_d[di % 2][0:cnt, (j * NOI + dd) * 128:(j * NOI + dd) * 128 + 128],
                                                h_q[0:cnt, (i * NOI + dd) * TQ:(i * NOI + dd) * TQ + TQ],
                                                start=(i == lst[0] and dd == 0),
                                                stop=(i == lst[-1] and dd == NOI - 1))
                        s_dmm.inc(inst)

        @B.scalar
        def _(e):
            e.wait_ge(s_cmem.h, s_cmem.base + 1)
            for qt in range(4):
                for oi in range(NOI):
                    ow, wd = OCH[oi]
                    for kg in range(8):
                        mi = (qt * NOI + oi) * 8 + kg
                        par = mi % 2
                        fb = par * 3 * TQ
                        e.wait_ge(s_gu.h, s_gu.base + (mi + 1))
                        if mi >= 2:
                            e.wait_ge(s_hq.h, s_hq.base + (mi - 1))
                        e.activation(out=fevac[0:wd, fb:fb + TQ], in_=ps_gn[par][0:wd, :],
                                     func=AF.Copy)
                        inst = e.activation(out=fevac[0:wd, fb + TQ:fb + 2 * TQ],
                                            in_=ps_un[par][0:wd, :], func=AF.Copy)
                        a_ev.inc(inst)
                        e.wait_ge(a_ev.h, a_ev.n)
                        e.wait_ge(v_ev.h, v_ev.base + (2 * mi + 1))
                        inst = e.activation(out=fevac[0:wd, fb + 2 * TQ:fb + 3 * TQ],
                                            in_=fevac[0:wd, fb:fb + TQ], func=AF.Silu,
                                            scale=sc_g[0:wd, 0:1])
                        a_ev.inc(inst)
                        e.wait_ge(a_ev.h, a_ev.n)
                for ocw in range(4):
                    for kg in range(8):
                        ni = (qt * 4 + ocw) * 8 + kg
                        par = ni % 2
                        fb = par * 3 * TQ
                        e.wait_ge(s_dmm.h, s_dmm.base + (ni + 1))
                        if ni >= 2:
                            e.wait_ge(s_gd.h, s_gd.base + (ni - 1))
                        inst = e.activation(out=fevac[:, fb:fb + TQ], in_=ps_gn[par][:, :],
                                            func=AF.Copy)
                        a_dv.inc(inst)
                        e.wait_ge(a_dv.h, a_dv.n)
                        e.wait_ge(v_dv.h, v_dv.base + (ni + 1))
                        if ni >= 2:
                            gst.wait(e, ni - 2)
                        inst = e.activation(out=gstg[:, par * TQ:par * TQ + TQ],
                                            in_=fevac[:, fb:fb + TQ], func=AF.Silu,
                                            scale=sc_d[:, 0:1])
                        s_gd.inc(inst)
                        e.wait_ge(s_gd.h, s_gd.n)
                        orow = kg * 512 + ocw * 128
                        gst.put(ni, e.dma_start(
                            out=g_hbm.ap()[orow:orow + 128, qt * TQ:(qt + 1) * TQ],
                            in_=gstg[:, par * TQ:par * TQ + TQ]))

        @B.vector
        def _(e):
            for qt in range(4):
                if qt >= 1:
                    e.wait_ge(s_dmm.h, s_dmm.base + (qt * 32))
                for oi in range(NOI):
                    ow, wd = OCH[oi]
                    for kg in range(8):
                        mi = (qt * NOI + oi) * 8 + kg
                        par = mi % 2
                        fb = par * 3 * TQ
                        e.wait_ge(a_ev.h, a_ev.base + (2 * mi + 1))
                        inst = e.tensor_sub(out=fevac[0:wd, fb:fb + TQ],
                                            in0=ps_gp[par][0:wd, :],
                                            in1=fevac[0:wd, fb:fb + TQ])
                        v_ev.inc(inst, 1)
                        e.wait_ge(v_ev.h, v_ev.n)
                        inst = e.tensor_sub(out=fevac[0:wd, fb + TQ:fb + 2 * TQ],
                                            in0=ps_up[par][0:wd, :],
                                            in1=fevac[0:wd, fb + TQ:fb + 2 * TQ])
                        v_ev.inc(inst, 1)
                        e.wait_ge(v_ev.h, v_ev.n)
                        e.wait_ge(a_ev.h, a_ev.base + (2 * mi + 2))
                        inst = e.scalar_tensor_tensor(
                            out=h_q[0:wd, ((kg * NOI + oi)) * TQ:((kg * NOI + oi)) * TQ + TQ],
                            in0=fevac[0:wd, fb + TQ:fb + 2 * TQ],
                            scalar=sc_u[0:wd, 0:1],
                            in1=fevac[0:wd, fb + 2 * TQ:fb + 3 * TQ],
                            op0=ALU.mult, op1=ALU.mult)
                        s_hq.inc(inst)
                        e.wait_ge(s_hq.h, s_hq.n)
                for ocw in range(4):
                    for kg in range(8):
                        ni = (qt * 4 + ocw) * 8 + kg
                        par = ni % 2
                        fb = par * 3 * TQ
                        e.wait_ge(a_dv.h, a_dv.base + (ni + 1))
                        inst = e.tensor_sub(out=fevac[:, fb:fb + TQ], in0=ps_gp[par][:, :],
                                            in1=fevac[:, fb:fb + TQ])
                        v_dv.inc(inst)
                        e.wait_ge(v_dv.h, v_dv.n)

    quiesce_and_free("xnl_0", "s_win0", "s_win1", "s_dwin",
                     "s_gu", "a_ev", "v_ev",
                     "s_hq", "s_dmm", "a_dv", "v_dv", "s_gd", "gst_0", "gst_1")
    ynl = Stream("ynl", 1)
    wo_st = Stream("wo_st", 3, 16, fresh=True)
    gld = Stream("gld", 2)
    xld = Stream("xld", 2)
    ost = Stream("ost", 2)
    s_omm = S("s_omm")
    s_ofin = S("s_ofin")

    @B.sync
    def _(e):
        ynl.put(0, e.dma_start(
            out=yn[:, :].rearrange("p (cc t) -> p cc t", t=T),
            in_=yn_hbm.ap().rearrange("(cc p) t -> p cc t", p=128)))
        for oc in range(32):
            if oc >= 2:
                e.wait_ge(s_ofin.h, s_ofin.base + (2 * (oc - 1)))
            gld.put(oc, e.dma_start(out=gbuf[:, (oc % 2) * T:(oc % 2) * T + T],
                                    in_=g_hbm.ap()[oc * 128:(oc + 1) * 128, :]))
            xld.put(oc, e.dma_start(out=xin6[:, (oc % 2) * T:(oc % 2) * T + T],
                                    in_=xT[oc * 128:(oc + 1) * 128, 3:TH]))

    @B.gpsimd
    def _(e):
        for oc in range(32):
            if oc >= 3:
                e.wait_ge(s_omm.h, s_omm.base + (oc - 2))
            dst = wslab6[:, (oc % 3) * 4096:(oc % 3) * 4096 + 4096].rearrange(
                "p (k c) -> p k c", c=128)
            srcv = w_out[:, oc * 128:(oc + 1) * 128].rearrange("(k p) c -> p k c", p=128)
            wo_st.put(oc, e.dma_start(out=dst, in_=srcv))

    with nc.psum_tensor("ps_o0", [128, T], F32) as ps_o0, \
         nc.psum_tensor("ps_o1", [128, T], F32) as ps_o1, \
         nc.psum_tensor("ps_o2", [128, T], F32) as ps_o2:
        ps_o = [ps_o0, ps_o1, ps_o2]

        @B.tensor
        def _(e):
            ynl.wait(e, 0)
            for oc in range(32):
                wo_st.wait(e, oc)
                if oc >= 3:
                    e.wait_ge(s_ofin.h, s_ofin.base + (2 * (oc - 2)))
                pst = ps_o[oc % 3]
                inst = None
                for k in range(32):
                    lhsT = wslab6[:, (oc % 3) * 4096 + k * 128:(oc % 3) * 4096 + (k + 1) * 128]
                    for (lo, n) in ((0, 512), (512, 512)):
                        inst = e.matmul(pst[:, lo:lo + n], lhsT,
                                        yn[:, k * T + lo:k * T + lo + n],
                                        start=(k == 0), stop=(k == 31))
                s_omm.inc(inst)

        @B.vector
        def _(e):
            for oc in range(32):
                e.wait_ge(s_omm.h, s_omm.base + (oc + 1))
                gld.wait(e, oc)
                xld.wait(e, oc)
                if oc >= 2:
                    ost.wait(e, oc - 2)
                pst = ps_o[oc % 3]
                stv = stg32[:, (oc % 2) * T:(oc % 2) * T + T]
                inst = e.tensor_mul(out=stv, in0=pst[:, :],
                                    in1=gbuf[:, (oc % 2) * T:(oc % 2) * T + T])
                s_ofin.inc(inst)
                e.wait_ge(s_ofin.h, s_ofin.n)
                inst = e.tensor_add(out=stv, in0=stv,
                                    in1=xin6[:, (oc % 2) * T:(oc % 2) * T + T])
                s_ofin.inc(inst)
                e.wait_ge(s_ofin.h, s_ofin.n)

        @B.scalar
        def _(e):
            for oc in range(32):
                e.wait_ge(s_ofin.h, s_ofin.base + (2 * oc + 2))
                ost.put(oc, e.dma_start(out=outT[oc * 128:(oc + 1) * 128, :],
                                        in_=stg32[:, (oc % 2) * T:(oc % 2) * T + T]))
            for _s in ost.sems:
                e.wait_ge(_s.h, _s.n)

    ctx.close()
    build._sems = dict(sems)
    build._pool = list(free_pool)
    return nc, dbg


_BUILT = {}

def _get_nc(debug_taps=(), cc_group=8):
    key = (tuple(sorted(debug_taps)), cc_group)
    if key not in _BUILT:
        _BUILT[key] = build(debug_taps, cc_group)
    return _BUILT[key]


def _prep_inputs(L0_attn_hidden_states, L0_attn_residual, L0_attn_ln_w, L0_attn_in_proj_w,
                 L0_attn_conv_w, L0_attn_A_log, L0_attn_dt_bias, L0_attn_D, L0_attn_norm_w,
                 L0_attn_out_proj_w, L0_attn_mup_vector, L0_ffn_w_gate, L0_ffn_w_up,
                 L0_ffn_w_down):
    x = np.ascontiguousarray(np.asarray(L0_attn_hidden_states, dtype=np.float32)[0])
    xT_full = np.ascontiguousarray(x.T)
    common = dict(
        w_in=np.ascontiguousarray(np.asarray(L0_attn_in_proj_w, np.float32)),
        w_out=np.ascontiguousarray(np.asarray(L0_attn_out_proj_w, np.float32)),
        wg_t=np.ascontiguousarray(np.asarray(L0_ffn_w_gate, np.float32).transpose(0, 2, 1)),
        wu_t=np.ascontiguousarray(np.asarray(L0_ffn_w_up, np.float32).transpose(0, 2, 1)),
        wd_t=np.ascontiguousarray(np.asarray(L0_ffn_w_down, np.float32).transpose(0, 2, 1)),
        convw=np.ascontiguousarray(np.asarray(L0_attn_conv_w, np.float32)),
        ln_w=np.ascontiguousarray(np.asarray(L0_attn_ln_w, np.float32)),
        norm_w=np.ascontiguousarray(np.asarray(L0_attn_norm_w, np.float32)),
        mup=np.ascontiguousarray(np.asarray(L0_attn_mup_vector, np.float32)),
        a_log=np.ascontiguousarray(np.asarray(L0_attn_A_log, np.float32)),
        dt_bias=np.ascontiguousarray(np.asarray(L0_attn_dt_bias, np.float32)),
        d_vec=np.ascontiguousarray(np.asarray(L0_attn_D, np.float32)),
    )
    in_maps = []
    for c in range(NCORE):
        xs = np.zeros((H, TH), np.float32)
        lo = c * T
        xs[:, 3:] = xT_full[:, lo:lo + T]
        if c > 0:
            xs[:, 0:3] = xT_full[:, lo - 3:lo]
        m = dict(common)
        m["xT"] = xs
        m["maskv"] = (np.arange(8) < c).astype(np.float32)
        in_maps.append(m)
    return in_maps


def kernel(**inputs):
    in_maps = _prep_inputs(**inputs)
    nc, _ = _get_nc()
    res = run_bass_kernel_spmd(nc, in_maps, core_ids=list(range(NCORE)))
    parts = [res.results[c]["outT"].T for c in range(NCORE)]
    out = np.concatenate(parts, axis=0)[None]
    return out.astype(np.float32)


# version 13
# speedup vs baseline: 1.1789x; 1.0456x over previous
import numpy as np
from contextlib import ExitStack

import concourse.bass as bass
import concourse.mybir as mybir
from concourse.bass_utils import run_bass_kernel_spmd

F32 = mybir.dt.float32
F16 = mybir.dt.float16
AF = mybir.ActivationFunctionType
ALU = mybir.AluOpType
AX = mybir.AxisListType

H = 4096
L = 8192
NCORE = 8
T = L // NCORE
TH = T + 3
PROJ = 8512
DS = 4096
CONV = 4352
NH = 64
HD = 64
NST = 128
Q = 128
NCH = 8
HID = 10928
H8 = 1366
D8 = 512
EPS = 1e-5

_TRIPLES = [(1, 2, 3), (1, 4, 5), (1, 7, 6), (2, 4, 6), (2, 5, 7), (3, 4, 7), (3, 6, 5)]

def _oct_tables():
    S = np.zeros((8, 8, 8), np.float32)
    S[0, 0, 0] = 1.0
    for i in range(1, 8):
        S[0, i, i] = 1.0
        S[i, 0, i] = 1.0
        S[i, i, 0] = -1.0
    for a, b, c in _TRIPLES:
        for x, y, z in ((a, b, c), (b, c, a), (c, a, b)):
            S[x, y, z] = 1.0
            S[y, x, z] = -1.0
    jtab = np.zeros((8, 8), np.int64)
    stab = np.zeros((8, 8), np.float32)
    for i in range(8):
        for k in range(8):
            js = np.nonzero(S[:, i, k])[0]
            assert len(js) == 1
            jtab[i, k] = int(js[0])
            stab[i, k] = S[js[0], i, k]
    return jtab, stab

JTAB, STAB = _oct_tables()
OCH = [(oi * 128, min(128, H8 - oi * 128)) for oi in range(11)]
GU_POS = [[kc for kc in range(32) if STAB[kc // 4, kg] > 0] for kg in range(8)]
GU_NEG = [[kc for kc in range(32) if STAB[kc // 4, kg] < 0] for kg in range(8)]
D_POS = [[i for i in range(8) if STAB[i, kg] > 0] for kg in range(8)]
D_NEG = [[i for i in range(8) if STAB[i, kg] < 0] for kg in range(8)]
for _kg in range(8):
    assert GU_POS[_kg] and GU_NEG[_kg] and D_POS[_kg] and D_NEG[_kg]
NOI = len(OCH)
N_WG = 8 * H8 * D8


class Sem:
    def __init__(self, ctx, nc, name):
        self.h = ctx.enter_context(nc.semaphore(name))
        self.n = 0
        self.base = 0
    def inc(self, inst, k=1):
        inst.then_inc(self.h, k)
        self.n += k
        return self.n
    def dma(self, inst):
        return self.inc(inst, 16)


def build(debug_taps=(), cc_group=8):
    taps = set(debug_taps)
    nc = bass.Bass("TRN2", target_bir_lowering=False)

    xT = nc.dram_tensor("xT", [H, TH], F32, kind="ExternalInput").ap()
    w_in = nc.dram_tensor("w_in", [H, PROJ], F32, kind="ExternalInput").ap()
    w_out = nc.dram_tensor("w_out", [DS, H], F32, kind="ExternalInput").ap()
    wg_t = nc.dram_tensor("wg_t", [8, D8, H8], F32, kind="ExternalInput").ap()
    wu_t = nc.dram_tensor("wu_t", [8, D8, H8], F32, kind="ExternalInput").ap()
    wd_t = nc.dram_tensor("wd_t", [8, H8, D8], F32, kind="ExternalInput").ap()
    convw = nc.dram_tensor("convw", [CONV, 4], F32, kind="ExternalInput").ap()
    ln_w = nc.dram_tensor("ln_w", [H], F32, kind="ExternalInput").ap()
    norm_w = nc.dram_tensor("norm_w", [DS], F32, kind="ExternalInput").ap()
    mup = nc.dram_tensor("mup", [PROJ], F32, kind="ExternalInput").ap()
    a_log = nc.dram_tensor("a_log", [NH], F32, kind="ExternalInput").ap()
    dt_bias = nc.dram_tensor("dt_bias", [NH], F32, kind="ExternalInput").ap()
    d_vec = nc.dram_tensor("d_vec", [NH], F32, kind="ExternalInput").ap()
    maskv = nc.dram_tensor("maskv", [8], F32, kind="ExternalInput").ap()
    outT = nc.dram_tensor("outT", [H, T], F32, kind="ExternalOutput").ap()

    z_hbm = nc.dram_tensor("z_hbm", [DS, T], F16)
    xn_hbm = nc.dram_tensor("xn_hbm", [H, T], F16)
    yn_hbm = nc.dram_tensor("yn_hbm", [DS, T], F16)
    g_hbm = nc.dram_tensor("g_hbm", [H, T], F16)
    qg_hbm = nc.dram_tensor("qg_hbm", [8, D8, H8], F16)
    qu_hbm = nc.dram_tensor("qu_hbm", [8, D8, H8], F16)
    qd_hbm = nc.dram_tensor("qd_hbm", [8, H8, D8], F16)
    lcrow = nc.dram_tensor("lcrow", [NCH, NH * Q], F32)
    cq_dram = nc.dram_tensor("cq_dram", [NCH, NH], F32)
    rowbc = nc.dram_tensor("rowbc", [16, 1088], F32)
    ccb_in = nc.dram_tensor("ccb_in", [NST, NH * HD], F16)
    ccb_out = nc.dram_tensor("ccb_out", [NCORE * NST, NH * HD], F16)
    cclc_in = nc.dram_tensor("cclc_in", [1, NH], F32)
    cclc_out = nc.dram_tensor("cclc_out", [NCORE, NH], F32)

    dbg = {}
    def tap(name, shape, dtype=F32):
        if name in taps:
            dbg[name] = nc.dram_tensor("dbg_" + name, shape, dtype, kind="ExternalOutput").ap()
            return dbg[name]
        return None

    ctx = ExitStack()

    ARENA = 209000
    SBUF_BASE = nc.bump_sbuf(ARENA)[0]
    C0, A0, B0 = SBUF_BASE, SBUF_BASE + 65728, SBUF_BASE + 135584
    P0OFF = SBUF_BASE + 205216
    def sb(name, shape, dtype, off):
        return nc.alloc_sbuf_tensor_at(name, shape, dtype, offset=off)

    _bump = [P0OFF]
    def sbp(name, shape, dtype):
        elems = 1
        for s in shape[1:]:
            elems *= s
        nbytes = elems * (2 if dtype == F16 else 4)
        off = (_bump[0] + 31) // 32 * 32
        _bump[0] = off + nbytes
        assert _bump[0] <= SBUF_BASE + ARENA, (name, _bump[0])
        return sb(name, shape, dtype, off)

    ident16 = sbp("ident16", [128, 128], F16)
    ident32 = sbp("ident32", [128, 128], F32)
    ones16_col = sbp("ones16_col", [128, 1], F16)
    ones32_col = sbp("ones32_col", [128, 1], F32)
    onesQ = sbp("onesQ", [NH, Q], F32)
    mup_sb = sbp("mup_sb", [128, 67], F32)
    lnw_sb = sbp("lnw_sb", [128, 32], F32)
    nw_sb = sbp("nw_sb", [128, 32], F32)
    cw_sb = sbp("cw_sb", [128, 34 * 4], F32)
    acol = sbp("acol", [NH, 1], F32)
    dtb_col = sbp("dtb_col", [NH, 1], F32)
    dall = sbp("dall", [128, 64], F32)
    db_pair = sbp("db_pair", [128, 32], F32)
    mask_sb = sbp("mask_sb", [8, 1], F32)
    lcq_acc = sbp("lcq_acc", [NH, 1], F32)
    sc_g = sbp("sc_g", [128, 1], F32)
    sc_u = sbp("sc_u", [128, 1], F32)
    sc_d = sbp("sc_d", [128, 1], F32)
    thr_g = sbp("thr_g", [128, 2], F32)
    thr_u = sbp("thr_u", [128, 2], F32)
    thr_d = sbp("thr_d", [128, 2], F32)
    facc = sbp("facc", [128, 3], F32)
    tsums = sbp("tsums", [128, 33], F32)
    tri8 = sbp("tri8", [8, 8], F32)
    ones8 = sbp("ones8", [1, 8], F32)
    tcol = sbp("tcol", [128, 1], F32)
    scmeans = sbp("scmeans", [128, 3], F32)
    constt = sbp("constt", [1, 16], F32)
    _consts = {}
    def reg_const(v):
        if v not in _consts:
            i = len(_consts)
            assert i < 16
            ap = constt[0:1, i:i + 1]
            nc.const_aps.register(F32, float(v), ap)
            _consts[v] = (i, float(v))
        return _consts[v]
    for _v in (EPS, 1.0 / H, 1.0 / DS, 1.0 / N_WG):
        reg_const(_v)

    xr = sb("xr", [128, 32 * TH], F16, C0)
    dt_sb = sb("dt_sb", [NH, T], F32, C0 + 0)
    lt_sb = sb("lt_sb", [NH, T], F32, C0 + 4128)
    lc_sb = sb("lc_sb", [NH, T], F32, C0 + 8256)
    h_all = sb("h_all", [128, NH * HD], F16, C0 + 12384)
    _cb = [C0 + 20576]
    _coff = {}
    def sbc(name, shape, dtype):
        elems = 1
        for s in shape[1:]:
            elems *= s
        nbytes = elems * (2 if dtype == F16 else 4)
        off = (_cb[0] + 31) // 32 * 32
        _cb[0] = off + nbytes
        _coff[name] = off
        assert _cb[0] <= C0 + 65728, (name, _cb[0])
        return sb(name, shape, dtype, off)

    dtT = sbc("dtT", [128, NH], F32)
    lcTn = sbc("lcTn", [128, NH], F32)
    wT = sbc("wT", [128, NH], F32)
    cumq = sbc("cumq", [128, NH], F32)
    btok = sbc("btok", [128, 128], F16)
    sbct = sbc("sbct", [128, 128], F16)
    wsc_t = sbc("wsc_t", [NH, 128], F32)
    combT = sbc("combT", [128, NH], F32)
    xsw = sbc("xsw", [128, 4 * 64], F16)
    xst = sbc("xst", [128, 4096], F16)
    lcb = sbc("lcb", [128, 4096], F32)
    mbig = sbc("mbig", [128, 4096], F16)
    cwt = sbc("cwt", [128, 4096], F16)
    csc = [sb(f"csc{i}", [128, T], F32, _coff["xst"] + i * 4128) for i in range(8)]
    hacc32 = sb("hacc32", [128, 4096], F32, _coff["lcb"])
    mfin = sb("mfin", [128, 4096], F16, _coff["lcb"])
    wful = sb("wful", [128, 8 * 64], F32, _coff["xst"])
    lcall = sb("lcall", [8, 64], F32, _coff["xst"] + 2048)
    lcq8 = sb("lcq8", [8, 64], F32, _coff["xst"] + 2336)
    lcp = sb("lcp", [1, 64], F32, _coff["xst"] + 2624)
    wcore = sb("wcore", [8, 64], F32, _coff["xst"] + 2912)
    bslice = sb("bslice", [128, 2 * 4096], F16, _coff["mbig"])
    zst = sb("zst", [128, 2 * T], F16, C0 + 0)
    rstd2_bc = sb("rstd2_bc", [128, T], F32, C0 + 4128)
    ynstg = sb("ynstg", [128, 2 * T], F16, C0 + 8352)
    row_d = sb("row_d", [1, 1088], F32, C0 + 12576)
    row_e = sb("row_e", [1, 1088], F32, C0 + 16928)
    sgl = sb("sgl", [128, 2 * T], F32, C0 + 21312)
    sqb = sb("sqb", [128, 2 * T], F16, C0 + 29536)
    qwin_g = [sb("qwin_g0", [128, 4096], F16, C0 + 0),
              sb("qwin_g1", [128, 4096], F16, C0 + 16384)]
    qwin_u = [sb("qwin_u0", [128, 4096], F16, C0 + 8192),
              sb("qwin_u1", [128, 4096], F16, C0 + 24576)]
    qwin_d = [sb("qwin_d0", [128, 11264], F16, C0 + 32768),
              sb("qwin_d1", [128, 11264], F16, A0 + 45056)]
    gstg = sb("gstg", [128, 2 * 256], F16, C0 + 55296)
    fevac = sb("fevac", [128, 6 * 256], F32, C0 + 56352)
    yn = sb("yn", [128, 32 * T], F16, C0 + 0)
    ph32 = sb("ph32", [128, T], F32, B0 + 0)

    xbc = sb("xbc", [128, 34 * TH], F16, A0)
    wstream = sb("wstream", [128, 4 * 4096], F32, A0)
    yfeat = sb("yfeat", [128, 32 * T], F16, A0)
    h_q = sb("h_q", [128, 88 * 256], F16, A0)
    wslab6 = sb("wslab6", [128, 3 * 4096], F16, A0)
    xin6 = sb("xin6", [128, 2 * T], F32, A0 + 24576)
    gbuf = sb("gbuf", [128, 2 * T], F16, A0 + 32768)
    stg32 = sb("stg32", [128, 2 * T], F32, A0 + 36864)

    xin = sb("xin", [128, 2 * TH], F32, B0)
    f16a = sb("f16a", [128, 2 * TH], F16, B0 + 16448)
    f16b = sb("f16b", [128, 2 * TH], F16, B0 + 20576)
    f32a = sb("f32a", [128, TH], F32, B0 + 24704)
    mean_bc = sb("mean_bc", [128, TH], F32, B0 + 28832)
    rstd_bc = sb("rstd_bc", [128, TH], F32, B0 + 32960)
    stg = sb("stg", [128, 2 * T], F16, B0 + 37088)
    row_a = sb("row_a", [1, 1088], F32, B0 + 41184)
    row_b = sb("row_b", [1, 1088], F32, B0 + 45536)
    row_c = sb("row_c", [1, 1088], F32, B0 + 49888)
    qin = sb("qin", [128, 2 * 2048], F32, B0)
    qpos = sb("qpos", [128, 2048], F16, B0 + 16384)
    qneg = sb("qneg", [128, 2048], F16, B0 + 20480)
    qout = sb("qout", [128, 2 * 2048], F16, B0 + 24576)
    wslab = sb("wslab", [128, 3 * 4096], F16, B0 + 32768)
    zstg1 = sb("zstg1", [128, 2 * T], F16, B0 + 57344)
    dt_tmp = sb("dt_tmp", [NH, T], F32, B0 + 61440)
    convo = sb("convo", [128, 34 * T], F16, B0)
    yg = sb("yg", [128, 32 * T], F16, B0)
    xn_sb = sb("xn_sb", [128, 32 * T], F16, B0)

    sems = {}
    free_pool = []
    build._alloc_log = []
    def S(name, fresh=False):
        if name not in sems:
            if free_pool and not fresh:
                old = free_pool.pop()
                s = Sem.__new__(Sem)
                s.h = old.h
                s.n = old.n
                s.base = old.n
                sems[name] = s
                build._alloc_log.append((name, getattr(old.h, "name", "?"), old.n))
            else:
                sems[name] = Sem(ctx, nc, name)
                build._alloc_log.append((name, "FRESH", 0))
        return sems[name]

    def sfree(*names):
        for nm in names:
            if nm in sems:
                free_pool.append(sems.pop(nm))

    s_bar = Sem(ctx, nc, "s_bar")

    class GatedBlock:
        def __getattr__(self, eng):
            dec = getattr(block, eng)
            def wrap(f):
                def g(e):
                    if s_bar.n:
                        e.wait_ge(s_bar.h, s_bar.n)
                    return f(e)
                return dec(g)
            return wrap
    B = GatedBlock()

    def quiesce_and_free(*names):
        @block.sync
        def _(e):
            if s_bar.n:
                e.wait_ge(s_bar.h, s_bar.n)
            for nm in names:
                if nm in sems:
                    e.wait_ge(sems[nm].h, sems[nm].n)
            s_bar.inc(e.dma_start(out=rowbc.ap()[15:16, 0:4], in_=xT[0:1, 0:4]), 16)
        sfree(*names)

    def fence(e, ename, inst):
        s = S("self_" + ename)
        s.inc(inst)
        e.wait_ge(s.h, s.n)

    class Stream:
        def __init__(self, name, nbufs, step=16, fresh=False):
            self.sems = [S(f"{name}_{i}", fresh=fresh) for i in range(nbufs)]
            self.nb = nbufs
            self.step = step
        def put(self, i, inst):
            self.sems[i % self.nb].inc(inst, self.step)
        def wait(self, e, i):
            s = self.sems[i % self.nb]
            e.wait_ge(s.h, s.base + self.step * (i // self.nb + 1))

    block = ctx.enter_context(nc.Block())

    s_const = S("s_const")
    s_cmem = S("s_cmem")
    s_cder = S("s_cder")

    @B.scalar
    def _(e):
        for c in range(66):
            s_const.dma(e.dma_start(out=mup_sb[:, c:c + 1], in_=mup[c * 128:(c + 1) * 128][:, None]))
        s_const.dma(e.dma_start(out=mup_sb[0:64, 66:67], in_=mup[8448:8512][:, None]))
        for c in range(32):
            s_const.dma(e.dma_start(out=lnw_sb[:, c:c + 1], in_=ln_w[c * 128:(c + 1) * 128][:, None]))
            s_const.dma(e.dma_start(out=nw_sb[:, c:c + 1], in_=norm_w[c * 128:(c + 1) * 128][:, None]))
        for c in range(34):
            s_const.dma(e.dma_start(out=cw_sb[:, c * 4:(c + 1) * 4], in_=convw[c * 128:(c + 1) * 128, :]))
        s_const.dma(e.dma_start(out=acol[:, 0:1], in_=a_log[:, None]))
        s_const.dma(e.dma_start(out=dtb_col[:, 0:1], in_=dt_bias[:, None]))
        s_const.dma(e.dma_start(out=mask_sb[:, 0:1], in_=maskv[:, None]))
        s_const.dma(e.dma_start(out=rowbc.ap()[0:1, 0:64], in_=d_vec[None, :]))
        e.wait_ge(s_const.h, s_const.n)
        s_const.dma(e.dma_start(out=dall[:, :], in_=rowbc.ap()[0, 0:64][None, :].partition_broadcast(128)))

    @B.gpsimd
    def _(e):
        fence(e, "pl", e.memset(ident16[:], 1.0))
        e.affine_select(out=ident16[:], in_=ident16[:], pattern=[[1, 128]],
                        compare_op=ALU.is_equal, fill=0.0, base=0, channel_multiplier=-1)
        fence(e, "pl", e.memset(ident32[:], 1.0))
        e.affine_select(out=ident32[:], in_=ident32[:], pattern=[[1, 128]],
                        compare_op=ALU.is_equal, fill=0.0, base=0, channel_multiplier=-1)
        e.memset(ones16_col[:], 1.0)
        e.memset(ones32_col[:], 1.0)
        e.memset(ones8[:], 1.0)
        fence(e, "pl", e.memset(tri8[:], 1.0))
        e.affine_select(out=tri8[:], in_=tri8[:], pattern=[[1, 8]],
                        compare_op=ALU.is_ge, fill=0.0, base=0, channel_multiplier=-1)
        e.memset(onesQ[:], 1.0)
        e.memset(facc[:], 0.0)
        e.memset(lcq_acc[:], 0.0)
        inst = None
        for _v, (_i, _fv) in sorted(_consts.items(), key=lambda kv: kv[1][0]):
            inst = e.memset(constt[0:1, _i:_i + 1], _fv)
        s_cmem.inc(inst)

    @B.scalar
    def _(e):
        fence(e, "act", e.activation(out=acol[:, 0:1], in_=acol[:, 0:1], func=AF.Exp))
        inst = e.mul(out=acol[:, 0:1], in_=acol[:, 0:1], mul=-1.0)
        s_cder.inc(inst)

    s_cder2 = S("s_cder2")

    @B.vector
    def _(e):
        e.wait_ge(s_const.h, s_const.n)
        e.tensor_copy(out=db_pair[0:64, :], in_=dall[0:64, 0:64:2])
        inst = e.tensor_copy(out=db_pair[64:128, :], in_=dall[64:128, 1:64:2])
        s_cder2.inc(inst)

    xin_st = Stream("xin_st", 2)
    s_xpre = S("s_xpre")
    s_mmp = S("s_mmp")
    s_rows = S("s_rows")
    s_bc = S("s_bc")
    xin2_st = Stream("xin2_st", 2)
    s_xnv = S("s_xnv")
    s_xns = S("s_xns")
    xnh_st = Stream("xnh_st", 2)
    s_rmsp = S("s_rmsp")
    s_rms = S("s_rms")
    s_xr = S("s_xr")

    with nc.psum_tensor("ps_sum", [1, TH], F32) as ps_sum, \
         nc.psum_tensor("ps_sq", [1, TH], F32) as ps_sq:

        @B.sync
        def _(e):
            for c in range(32):
                if c >= 2:
                    e.wait_ge(s_xpre.h, s_xpre.base + (2 * (c - 1)))
                xin_st.put(c, e.dma_start(out=xin[:, (c % 2) * TH:(c % 2) * TH + TH],
                                          in_=xT[c * 128:(c + 1) * 128, :]))

        @B.scalar
        def _(e):
            for c in range(32):
                xin_st.wait(e, c)
                if c >= 2:
                    e.wait_ge(s_mmp.h, s_mmp.base + (c - 1))
                xc = xin[:, (c % 2) * TH:(c % 2) * TH + TH]
                fa = f16a[:, (c % 2) * TH:(c % 2) * TH + TH]
                fb = f16b[:, (c % 2) * TH:(c % 2) * TH + TH]
                s_xpre.inc(e.activation(out=fa, in_=xc, func=AF.Copy))
                s_xpre.inc(e.activation(out=fb, in_=xc, func=AF.Square))

        @B.tensor
        def _(e):
            e.wait_ge(s_cmem.h, s_cmem.base + (1))
            for c in range(32):
                e.wait_ge(s_xpre.h, s_xpre.base + (2 * (c + 1)))
                fa = f16a[:, (c % 2) * TH:(c % 2) * TH + TH]
                fb = f16b[:, (c % 2) * TH:(c % 2) * TH + TH]
                for (lo, n) in ((0, 512), (512, 512), (1024, 3)):
                    e.matmul(ps_sum[:, lo:lo + n], ones16_col[:], fa[:, lo:lo + n],
                             start=(c == 0), stop=(c == 31))
                inst = None
                for (lo, n) in ((0, 512), (512, 512), (1024, 3)):
                    inst = e.matmul(ps_sq[:, lo:lo + n], ones16_col[:], fb[:, lo:lo + n],
                                    start=(c == 0), stop=(c == 31))
                s_mmp.inc(inst)

        @B.vector
        def _(e):
            e.wait_ge(s_mmp.h, s_mmp.base + (32))
            e.tensor_scalar(out=row_a[:, 0:TH], in0=ps_sum[:, :], scalar1=1.0 / H,
                            scalar2=None, op0=ALU.mult)
            fence(e, "dve", e.tensor_scalar(out=row_b[:, 0:TH], in0=ps_sq[:, :], scalar1=1.0 / H,
                                            scalar2=None, op0=ALU.mult))
            fence(e, "dve", e.tensor_mul(out=row_c[:, 0:TH], in0=row_a[:, 0:TH], in1=row_a[:, 0:TH]))
            inst = e.tensor_sub(out=row_b[:, 0:TH], in0=row_b[:, 0:TH], in1=row_c[:, 0:TH])
            s_rows.inc(inst)

        @B.scalar
        def _(e):
            e.wait_ge(s_rows.h, s_rows.base + (1))
            e.wait_ge(s_cmem.h, s_cmem.base + (1))
            inst = e.activation(out=row_c[:, 0:TH], in_=row_b[:, 0:TH], func=AF.Sqrt, bias=EPS)
            s_rows.inc(inst)

        @B.vector
        def _(e):
            e.wait_ge(s_rows.h, s_rows.base + (2))
            inst = e.reciprocal(out=row_c[:, 0:TH], in_=row_c[:, 0:TH])
            s_rows.inc(inst)

        @B.sync
        def _(e):
            e.wait_ge(s_rows.h, s_rows.base + (3))
            s_bc.dma(e.dma_start(out=rowbc.ap()[1:2, 0:TH], in_=row_a[:, 0:TH]))
            s_bc.dma(e.dma_start(out=rowbc.ap()[2:3, 0:TH], in_=row_c[:, 0:TH]))
            e.wait_ge(s_bc.h, s_bc.base + (32))
            s_bc.dma(e.dma_start(out=mean_bc[:, :],
                                 in_=rowbc.ap()[1, 0:TH][None, :].partition_broadcast(128)))
            s_bc.dma(e.dma_start(out=rstd_bc[:, :],
                                 in_=rowbc.ap()[2, 0:TH][None, :].partition_broadcast(128)))

    with nc.psum_tensor("ps_rms", [1, TH], F32) as ps_rms:

        @B.sync
        def _(e):
            for c in range(32):
                if c >= 2:
                    e.wait_ge(s_xnv.h, s_xnv.base + (c - 1))
                xin2_st.put(c, e.dma_start(out=xin[:, (c % 2) * TH:(c % 2) * TH + TH],
                                           in_=xT[c * 128:(c + 1) * 128, :]))

        @B.vector
        def _(e):
            e.wait_ge(s_bc.h, s_bc.base + (64))
            for c in range(32):
                xin2_st.wait(e, c)
                if c >= 1:
                    e.wait_ge(s_xns.h, s_xns.base + (2 * c))
                xc = xin[:, (c % 2) * TH:(c % 2) * TH + TH]
                fa = f16a[:, (c % 2) * TH:(c % 2) * TH + TH]
                fence(e, "dve", e.tensor_sub(out=f32a[:], in0=xc, in1=mean_bc[:]))
                inst = e.tensor_mul(out=fa, in0=f32a[:], in1=rstd_bc[:])
                s_xnv.inc(inst)

        @B.scalar
        def _(e):
            for c in range(32):
                e.wait_ge(s_xnv.h, s_xnv.base + (c + 1))
                if c >= 2:
                    e.wait_ge(s_rmsp.h, s_rmsp.base + (c - 1))
                    xnh_st.wait(e, c - 2)
                fa = f16a[:, (c % 2) * TH:(c % 2) * TH + TH]
                fb = f16b[:, (c % 2) * TH:(c % 2) * TH + TH]
                s_xns.inc(e.activation(out=fb, in_=fa, func=AF.Square))
                fence(e, "act", e.activation(out=stg[:, (c % 2) * T:(c % 2) * T + T],
                                             in_=fa[:, 3:TH], func=AF.Copy))
                xnh_st.put(c, e.dma_start(out=xn_hbm.ap()[c * 128:(c + 1) * 128, :],
                                          in_=stg[:, (c % 2) * T:(c % 2) * T + T]))
                s_xns.inc(e.activation(out=xr[:, c * TH:(c + 1) * TH], in_=fa, func=AF.Copy))

        @B.tensor
        def _(e):
            e.wait_ge(s_cmem.h, s_cmem.base + (1))
            inst = None
            for c in range(32):
                e.wait_ge(s_xns.h, s_xns.base + (2 * c + 1))
                fb = f16b[:, (c % 2) * TH:(c % 2) * TH + TH]
                for (lo, n) in ((0, 512), (512, 512), (1024, 3)):
                    inst = e.matmul(ps_rms[:, lo:lo + n], ones16_col[:], fb[:, lo:lo + n],
                                    start=(c == 0), stop=(c == 31))
                s_rmsp.inc(inst)

        @B.scalar
        def _(e):
            e.wait_ge(s_rmsp.h, s_rmsp.base + (32))
            e.wait_ge(s_cmem.h, s_cmem.base + (1))
            inst = e.activation(out=row_a[:, 0:TH], in_=ps_rms[:, :], func=AF.Sqrt,
                                scale=1.0 / DS, bias=EPS)
            s_rms.inc(inst)

        @B.vector
        def _(e):
            e.wait_ge(s_rms.h, s_rms.base + (1))
            inst = e.reciprocal(out=row_a[:, 0:TH], in_=row_a[:, 0:TH])
            s_rms.inc(inst)

        @B.sync
        def _(e):
            e.wait_ge(s_rms.h, s_rms.base + (2))
            s_bc.dma(e.dma_start(out=rowbc.ap()[3:4, 0:TH], in_=row_a[:, 0:TH]))
            e.wait_ge(s_bc.h, s_bc.n)
            s_bc.dma(e.dma_start(out=rstd_bc[:, :],
                                 in_=rowbc.ap()[3, 0:TH][None, :].partition_broadcast(128)))

        @B.vector
        def _(e):
            e.wait_ge(s_bc.h, s_bc.n)
            e.wait_ge(s_xns.h, s_xns.base + (64))
            for c in range(32):
                if c >= 1:
                    e.wait_ge(s_xr.h, s_xr.base + (c))
                fence(e, "dve", e.tensor_mul(out=f32a[:], in0=xr[:, c * TH:(c + 1) * TH], in1=rstd_bc[:]))
                inst = e.tensor_scalar(out=xr[:, c * TH:(c + 1) * TH], in0=f32a[:],
                                       scalar1=lnw_sb[:, c:c + 1], scalar2=None, op0=ALU.mult)
                s_xr.inc(inst)


    wsc_st = Stream("wsc_st", 2)
    s_wred = S("s_wred")
    s_thr = S("s_thr")

    WTILES = []
    for pj in range(3):
        for t0 in range(11):
            lo = t0 * 4096
            wdt = min(4096, 43712 - lo)
            WTILES.append((pj, lo, wdt))

    s_scbc = S("s_scbc")

    @B.gpsimd
    def _(e):
        e.wait_ge(s_cmem.h, s_cmem.base + (1))
        srcs = [wg_t, wu_t, wd_t]
        for i, (pj, lo, wdt) in enumerate(WTILES):
            if i >= 4:
                e.wait_ge(s_wred.h, s_wred.base + (i - 3))
            flat = srcs[pj].rearrange("j a b -> (j a b)").rearrange("(p f) -> p f", p=128)
            wsc_st.put(i, e.dma_start(out=wstream[:, (i % 4) * 4096:(i % 4) * 4096 + wdt],
                                      in_=flat[:, lo:lo + wdt]))

    @B.scalar
    def _(e):
        e.wait_ge(s_cmem.h, s_cmem.base + (1))
        for i, (pj, lo, wdt) in enumerate(WTILES):
            wsc_st.wait(e, i)
            ws = wstream[:, (i % 4) * 4096:(i % 4) * 4096 + wdt]
            inst = e.activation(out=ws, in_=ws, func=AF.Abs, accum_out=tsums[:, i:i + 1])
            s_wred.inc(inst)
            e.wait_ge(s_wred.h, s_wred.n)

    @B.gpsimd
    def _(e):
        e.wait_ge(s_scbc.h, s_scbc.base + (32))
        inst = None
        for pj, (sc, th) in enumerate(((sc_g, thr_g), (sc_u, thr_u), (sc_d, thr_d))):
            e.tensor_copy(out=sc[:, 0:1], in_=scmeans[:, pj:pj + 1])
            e.tensor_scalar(out=th[:, 0:1], in0=scmeans[:, pj:pj + 1], scalar1=0.5,
                            scalar2=None, op0=ALU.mult)
            inst = e.tensor_scalar(out=th[:, 1:2], in0=scmeans[:, pj:pj + 1], scalar1=-0.5,
                                   scalar2=None, op0=ALU.mult)
        s_thr.inc(inst)

    with nc.psum_tensor("ps_wtot", [1, 33], F32) as ps_wtot:
        @B.tensor
        def _(e):
            e.wait_ge(s_wred.h, s_wred.base + (33))
            inst = None
            for pj in range(3):
                inst = e.matmul(ps_wtot[0:1, pj * 11:(pj + 1) * 11], ones32_col[:],
                                tsums[:, pj * 11:(pj + 1) * 11], start=True, stop=True)
            s_thr.inc(inst)

        @B.scalar
        def _(e):
            e.wait_ge(s_thr.h, s_thr.base + (1))
            inst = None
            for pj in range(3):
                inst = e.activation(out=row_c[0:1, pj * 16:pj * 16 + 11],
                                    in_=ps_wtot[0:1, pj * 11:(pj + 1) * 11],
                                    func=AF.Copy, scale=1.0 / N_WG,
                                    accum_out=row_b[0:1, pj:pj + 1])
            s_thr.inc(inst)

    @B.sync
    def _(e):
        e.wait_ge(s_thr.h, s_thr.base + (2))
        s_scbc.dma(e.dma_start(out=rowbc.ap()[5:6, 0:3], in_=row_b[0:1, 0:3]))
        e.wait_ge(s_scbc.h, s_scbc.base + (16))
        s_scbc.dma(e.dma_start(out=scmeans[:, :],
                               in_=rowbc.ap()[5, 0:3][None, :].partition_broadcast(128)))

    wst = Stream("wst", 3)
    zhst = Stream("zhst", 2)
    qld = Stream("qld", 2)
    qst = Stream("qst", 2)
    s_mmoc = S("s_mmoc")
    s_evoc = S("s_evoc")
    s_tail = S("s_tail")
    s_qrd = S("s_qrd")
    s_qv = S("s_qv")

    @B.gpsimd
    def _(e):
        e.wait_ge(s_xr.h, s_xr.base + (32))
        e.wait_ge(s_scbc.h, s_scbc.base + (16))
        for _s in xnh_st.sems:
            e.wait_ge(_s.h, _s.n)
        for oc in range(67):
            if oc >= 3:
                e.wait_ge(s_mmoc.h, s_mmoc.base + (oc - 2))
            w = 64 if oc == 66 else 128
            dst = wslab[:, (oc % 3) * 4096:(oc % 3) * 4096 + 32 * w].rearrange(
                "p (k c) -> p k c", c=w)
            srcv = w_in[:, oc * 128: oc * 128 + w].rearrange("(k p) c -> p k c", p=128)
            wst.put(oc, e.dma_start(out=dst, in_=srcv))

    with nc.psum_tensor("ps_ip0", [128, 1024], F32) as ps_ip0, \
         nc.psum_tensor("ps_ip1", [128, 1024], F32) as ps_ip1, \
         nc.psum_tensor("ps_ip2", [128, 1024], F32) as ps_ip2, \
         nc.psum_tensor("ps_tail", [128, 102], F32) as ps_tail:
        ps_ip = [ps_ip0, ps_ip1, ps_ip2]

        @B.tensor
        def _(e):
            e.wait_ge(s_xr.h, s_xr.base + (32))
            for oc in range(67):
                wst.wait(e, oc)
                if oc >= 3:
                    e.wait_ge(s_evoc.h, s_evoc.base + (oc - 2))
                w = 64 if oc == 66 else 128
                buf = (oc % 3) * 4096
                pst = ps_ip[oc % 3]
                is_xbc = 32 <= oc <= 65
                inst = None
                for k in range(32):
                    lhsT = wslab[:, buf + k * w: buf + (k + 1) * w]
                    e.matmul(pst[0:w, 0:512], lhsT, xr[:, k * TH + 3: k * TH + 515],
                             start=(k == 0), stop=(k == 31))
                    inst = e.matmul(pst[0:w, 512:1024], lhsT, xr[:, k * TH + 515: k * TH + 1027],
                                    start=(k == 0), stop=(k == 31))
                    if is_xbc:
                        inst = e.matmul(ps_tail[:, (oc - 32) * 3:(oc - 32) * 3 + 3], lhsT,
                                        xr[:, k * TH: k * TH + 3],
                                        start=(k == 0), stop=(k == 31))
                s_mmoc.inc(inst)

        @B.scalar
        def _(e):
            e.wait_ge(s_thr.h, s_thr.base + (3))
            for oc in range(67):
                e.wait_ge(s_mmoc.h, s_mmoc.base + (oc + 1))
                pst = ps_ip[oc % 3]
                if oc < 32:
                    if oc >= 2:
                        zhst.wait(e, oc - 2)
                    ist = e.activation(out=zstg1[:, (oc % 2) * T:(oc % 2) * T + T],
                                       in_=pst[:, 0:1024], func=AF.Copy,
                                       scale=mup_sb[:, oc:oc + 1])
                    s_evoc.inc(ist)
                    e.wait_ge(s_evoc.h, s_evoc.n)
                    zhst.put(oc, e.dma_start(out=z_hbm.ap()[oc * 128:(oc + 1) * 128, :],
                                             in_=zstg1[:, (oc % 2) * T:(oc % 2) * T + T]))
                elif oc < 66:
                    cb = oc - 32
                    s_evoc.inc(e.activation(out=xbc[:, cb * TH + 3: cb * TH + TH],
                                            in_=pst[:, 0:1024], func=AF.Copy,
                                            scale=mup_sb[:, oc:oc + 1]))
                else:
                    s_evoc.inc(e.activation(out=dt_tmp[:, :], in_=pst[0:64, 0:1024],
                                            func=AF.Copy, scale=mup_sb[0:64, 66:67]))
            for cb in range(34):
                s_tail.inc(e.activation(out=xbc[:, cb * TH: cb * TH + 3],
                                        in_=ps_tail[:, cb * 3: cb * 3 + 3], func=AF.Copy,
                                        scale=mup_sb[:, 32 + cb:33 + cb]))

    QT = []
    for pj in range(3):
        for t0 in range(22):
            lo = t0 * 2048
            wdt = min(2048, 43712 - lo)
            QT.append((pj, lo, wdt))
    qsrc = [wg_t, wu_t, wd_t]
    qdst = [qg_hbm, qu_hbm, qd_hbm]

    def _qflat(t):
        a = t.ap() if not isinstance(t, bass.AP) else t
        return a.rearrange("j a b -> (j a b)").rearrange("(p f) -> p f", p=128)

    @B.sync
    def _(e):
        e.wait_ge(s_thr.h, s_thr.base + (3))
        e.wait_ge(s_xr.h, s_xr.base + (32))
        for i, (pj, lo, wdt) in enumerate(QT):
            if i >= 2:
                e.wait_ge(s_qrd.h, s_qrd.base + (i - 1))
            qld.put(i, e.dma_start(out=qin[:, (i % 2) * 2048:(i % 2) * 2048 + wdt],
                                   in_=_qflat(qsrc[pj])[:, lo:lo + wdt]))
            if i >= 1:
                pjp, lop, wdtp = QT[i - 1]
                e.wait_ge(s_qv.h, s_qv.base + (i))
                qst.put(i - 1, e.dma_start(out=_qflat(qdst[pjp])[:, lop:lop + wdtp],
                                           in_=qout[:, ((i - 1) % 2) * 2048:((i - 1) % 2) * 2048 + wdtp]))
        pjp, lop, wdtp = QT[-1]
        e.wait_ge(s_qv.h, s_qv.base + (len(QT)))
        qst.put(len(QT) - 1, e.dma_start(out=_qflat(qdst[pjp])[:, lop:lop + wdtp],
                                         in_=qout[:, ((len(QT) - 1) % 2) * 2048:((len(QT) - 1) % 2) * 2048 + wdtp]))

    @B.vector
    def _(e):
        thr = [thr_g, thr_u, thr_d]
        for i, (pj, lo, wdt) in enumerate(QT):
            qld.wait(e, i)
            if i >= 1:
                e.wait_ge(s_qv.h, s_qv.base + (i))
            if i >= 2:
                qst.wait(e, i - 2)
            src = qin[:, (i % 2) * 2048:(i % 2) * 2048 + wdt]
            e.tensor_scalar(out=qpos[:, 0:wdt], in0=src, scalar1=thr[pj][:, 0:1],
                            scalar2=None, op0=ALU.is_gt)
            inst = e.tensor_scalar(out=qneg[:, 0:wdt], in0=src, scalar1=thr[pj][:, 1:2],
                                   scalar2=None, op0=ALU.is_lt)
            s_qrd.inc(inst)
            e.wait_ge(s_qrd.h, s_qrd.n)
            inst = e.tensor_sub(out=qout[:, (i % 2) * 2048:(i % 2) * 2048 + wdt],
                                in0=qpos[:, 0:wdt], in1=qneg[:, 0:wdt])
            s_qv.inc(inst)


    quiesce_and_free("xin_st_0", "xin_st_1", "s_xpre", "s_mmp", "s_rows", "s_bc",
                     "xin2_st_0", "xin2_st_1", "s_xnv", "s_xns", "xnh_st_0", "xnh_st_1",
                     "s_rmsp", "s_rms", "s_xr", "s_wred", "s_scbc")
    s_dtc = S("s_dtc")
    s_cnvd = S("s_cnvd")
    s_cnvp = S("s_cnvp")
    s_cnva = S("s_cnva")
    s_dec = S("s_dec")
    s_lcch = S("s_lcch")
    s_lcqa = S("s_lcqa")
    lcrow_st = Stream("lcrow_st", 8)
    cq_st = Stream("cq_st", 8)

    def tapwait(e):
        for nm, sm in list(sems.items()):
            if nm.startswith("s_dbg_"):
                e.wait_ge(sm.h, sm.n)

    def conv_body(e, ename, chunks, base, smine):
        e.wait_ge(s_evoc.h, s_evoc.base + (67))
        e.wait_ge(s_tail.h, s_tail.base + (34))
        e.wait_ge(s_const.h, s_const.n)
        for idx, cb in enumerate(chunks):
            if idx >= 2:
                e.wait_ge(s_cnva.h, s_cnva.base + (chunks[idx - 2] + 1))
            pair = idx % 2
            a = csc[base + 2 * pair]
            b = csc[base + 2 * pair + 1]
            fence(e, ename, e.tensor_scalar(out=a[:, 0:T], in0=xbc[:, cb * TH: cb * TH + T],
                                            scalar1=cw_sb[:, cb * 4: cb * 4 + 1], scalar2=None,
                                            op0=ALU.mult))
            fence(e, ename, e.scalar_tensor_tensor(out=b[:, 0:T], in0=xbc[:, cb * TH + 1: cb * TH + 1 + T],
                                                   scalar=cw_sb[:, cb * 4 + 1: cb * 4 + 2], in1=a[:, 0:T],
                                                   op0=ALU.mult, op1=ALU.add))
            fence(e, ename, e.scalar_tensor_tensor(out=a[:, 0:T], in0=xbc[:, cb * TH + 2: cb * TH + 2 + T],
                                                   scalar=cw_sb[:, cb * 4 + 2: cb * 4 + 3], in1=b[:, 0:T],
                                                   op0=ALU.mult, op1=ALU.add))
            inst = e.scalar_tensor_tensor(out=b[:, 0:T], in0=xbc[:, cb * TH + 3: cb * TH + 3 + T],
                                          scalar=cw_sb[:, cb * 4 + 3: cb * 4 + 4], in1=a[:, 0:T],
                                          op0=ALU.mult, op1=ALU.add)
            smine.inc(inst)

    CONV_D = list(range(34))
    CONV_P = []

    @B.vector
    def _(e):
        e.wait_ge(s_mmoc.h, s_mmoc.base + (67))
        e.wait_ge(s_evoc.h, s_evoc.base + (67))
        tapwait(e)
        inst = e.tensor_copy(out=dt_sb[:, :], in_=dt_tmp[:, :])
        s_dtc.inc(inst)
        conv_body(e, "dve", CONV_D, 0, s_cnvd)

    s_hini = S("s_hini")

    @B.gpsimd
    def _(e):
        e.wait_ge(s_mmoc.h, s_mmoc.base + (67))
        tapwait(e)
        inst = e.memset(h_all[:], 0.0)
        s_hini.inc(inst)

    @B.scalar
    def _(e):
        e.wait_ge(s_qv.h, s_qv.base + 66)
        for _s in qst.sems:
            e.wait_ge(_s.h, _s.n)
        for _s in zhst.sems:
            e.wait_ge(_s.h, _s.n)
        e.wait_ge(s_dtc.h, s_dtc.base + 1)
        e.wait_ge(s_mmoc.h, s_mmoc.base + 67)
        for cb in range(34):
            e.wait_ge(s_cnvd.h, s_cnvd.base + (cb + 1))
            b = csc[2 * (cb % 2) + 1]
            inst = e.activation(out=convo[:, cb * T:(cb + 1) * T], in_=b[:, 0:T], func=AF.Silu)
            s_cnva.inc(inst)

    @B.vector
    def _(e):
        e.wait_ge(s_dtc.h, s_dtc.base + (1))
        inst = e.tensor_scalar(out=dt_sb[:, :], in0=dt_sb[:, :], scalar1=dtb_col[:, 0:1],
                               scalar2=None, op0=ALU.add)
        s_dec.inc(inst)
        e.wait_ge(s_dec.h, s_dec.n)
        inst = e.tensor_scalar(out=lc_sb[:, :], in0=dt_sb[:, :], scalar1=0.0, scalar2=None,
                               op0=ALU.max)
        s_dec.inc(inst)

    @B.scalar
    def _(e):
        e.wait_ge(s_dec.h, s_dec.base + (2))
        fence(e, "act", e.activation(out=lt_sb[:, :], in_=dt_sb[:, :], func=AF.Abs))
        fence(e, "act", e.activation(out=lt_sb[:, :], in_=lt_sb[:, :], func=AF.Exp, scale=-1.0))
        inst = e.activation(out=lt_sb[:, :], in_=lt_sb[:, :], func=AF.Ln, bias=1.0)
        s_dec.inc(inst)

    @B.vector
    def _(e):
        e.wait_ge(s_dec.h, s_dec.base + (3))
        inst = e.tensor_add(out=dt_sb[:, :], in0=lt_sb[:, :], in1=lc_sb[:, :])
        s_dec.inc(inst)
        e.wait_ge(s_dec.h, s_dec.n)
        e.wait_ge(s_cder.h, s_cder.base + (1))
        inst = e.tensor_scalar(out=lt_sb[:, :], in0=dt_sb[:, :], scalar1=acol[:, 0:1],
                               scalar2=None, op0=ALU.mult)
        s_dec.inc(inst)
        e.wait_ge(s_dec.h, s_dec.n)
        for ch in range(NCH):
            inst = e.tensor_tensor_scan(out=lc_sb[:, ch * Q:(ch + 1) * Q], data0=onesQ[:, :],
                                        data1=lt_sb[:, ch * Q:(ch + 1) * Q], initial=0.0,
                                        op0=ALU.mult, op1=ALU.add)
            s_lcch.inc(inst)
            e.wait_ge(s_lcch.h, s_lcch.n)
            inst = e.tensor_add(out=lcq_acc[:, 0:1], in0=lcq_acc[:, 0:1],
                                in1=lc_sb[:, ch * Q + Q - 1: ch * Q + Q])
            s_lcqa.inc(inst)
            e.wait_ge(s_lcqa.h, s_lcqa.n)

    @B.sync
    def _(e):
        for ch in range(NCH):
            e.wait_ge(s_lcch.h, s_lcch.base + (ch + 1))
            lcrow_st.put(ch, e.dma_start(out=lcrow.ap()[ch:ch + 1, :],
                                         in_=lc_sb[:, ch * Q:(ch + 1) * Q]))
            lcrow_st.wait(e, ch)
            cq_st.put(ch, e.dma_start(out=cq_dram.ap()[ch:ch + 1, :],
                                      in_=lcrow.ap()[ch, Q - 1::Q][None, :],
                                      allow_slow_non_contiguous=True))

    with nc.psum_tensor("tp16a", [128, 128], F16) as tp16a, \
         nc.psum_tensor("tp16b", [128, 128], F16) as tp16b, \
         nc.psum_tensor("tp32", [128, 64], F32) as tp32, \
         nc.psum_tensor("sbct_ps", [128, 128], F32) as sbct_ps, \
         nc.psum_tensor("psY0", [128, 128], F32) as psY0, \
         nc.psum_tensor("psY1", [128, 128], F32) as psY1, \
         nc.psum_tensor("psH0", [128, 64], F32) as psH0, \
         nc.psum_tensor("psH1", [128, 64], F32) as psH1:
        tp16 = [tp16a, tp16b]
        psY = [psY0, psY1]
        psH = [psH0, psH1]

        def emit_scan_pass(pfx, full):
            tr16 = S(pfx + "tr16")
            t16 = S(pfx + "t16")
            tr32 = S(pfx + "tr32")
            t32 = S(pfx + "t32")
            wsc0 = S(pfx + "wsc0")
            wsc1 = S(pfx + "wsc1")
            wscv = S(pfx + "wscv")
            cq = S(pfx + "cq")
            xsw_s = S(pfx + "xsw")
            psh = S(pfx + "psh")
            hstt = S(pfx + "hstt")
            n32 = 3 if full else 1
            psHp = psH if full else [psH0, psH1, psY0, psY1]
            DPH = len(psHp)
            LAG = 0 if full else 3
            if full:
                sbctmm = S(pfx + "sbctmm")
                sbcte = S(pfx + "sbcte")
                cmbL = S(pfx + "cmbL")
                cmb = S(pfx + "cmb")
                cwx = S(pfx + "cwx")
                cwm = S(pfx + "cwm")
                bsub = S(pfx + "bsub")
                mbx = S(pfx + "mbx")
                msk = S(pfx + "msk")
                m_s = S(pfx + "m")
                y_s = S(pfx + "y")
                yev = S(pfx + "yev")
                lcb_st = Stream(pfx + "lcb", 2)
            else:
                cqb_st = Stream(pfx + "cqb", 2)

            @B.sync
            def _(e):
                if full:
                    e.wait_ge(s_gath.h, s_gath.base + (2))
                    for ch in range(NCH):
                        for hl in range(2):
                            g2 = 2 * ch + hl
                            if g2 >= 1:
                                e.wait_ge(y_s.h, y_s.base + (32 * g2))
                            lcb_st.put(g2, e.dma_start(
                                out=lcb[:, :],
                                in_=lcrow.ap()[ch, hl * 4096:(hl + 1) * 4096][None, :].partition_broadcast(128)))
                else:
                    for ch in range(NCH):
                        cq_st.wait(e, ch)
                        if ch >= 1:
                            e.wait_ge(hstt.h, hstt.base + (ch * 64))
                        cqb_st.put(ch, e.dma_start(
                            out=cumq[:, 0:64],
                            in_=cq_dram.ap()[ch, :][None, :].partition_broadcast(128)))

            @B.tensor
            def _(e):
                e.wait_ge(s_cnva.h, s_cnva.base + (34))
                if full:
                    e.wait_ge(s_gath.h, s_gath.base + (2))
                for ch in range(NCH):
                    for j in range(33):
                        gt = ch * 33 + j
                        if gt >= 2:
                            e.wait_ge(t16.h, t16.base + (gt - 1))
                        src_off = j * T + ch * 128
                        inst = e.transpose(tp16[gt % 2][:, :], convo[:, src_off:src_off + 128],
                                           ident16[:])
                        tr16.inc(inst)
                    f32srcs = []
                    if full:
                        f32srcs = [dt_sb[:, ch * Q:(ch + 1) * Q], lc_sb[:, ch * Q:(ch + 1) * Q]]
                    for jj, srcv in enumerate(f32srcs):
                        g32 = ch * n32 + jj
                        if g32 >= 1:
                            e.wait_ge(t32.h, t32.base + (g32))
                        inst = e.transpose(tp32[:, 0:64], srcv, ident32[0:64, 0:64])
                        tr32.inc(inst)
                    e.wait_ge(wscv.h, wscv.base + (ch + 1))
                    g32 = ch * n32 + n32 - 1
                    if g32 >= 1:
                        e.wait_ge(t32.h, t32.base + (g32))
                    inst = e.transpose(tp32[:, 0:64], wsc_t[:, :], ident32[0:64, 0:64])
                    tr32.inc(inst)
                    if full:
                        if ch >= 1:
                            e.wait_ge(sbcte.h, sbcte.base + (ch))
                        inst = e.matmul(sbct_ps[:, :], convo[:, 32 * T + ch * 128: 32 * T + ch * 128 + 128],
                                        convo[:, 33 * T + ch * 128: 33 * T + ch * 128 + 128],
                                        start=True, stop=True)
                        sbctmm.inc(inst)
                    for r in range(64):
                        g = ch * 64 + r
                        hl = r // 32
                        if full:
                            p = r // 2
                            gp = ch * 32 + p
                            e.wait_ge(m_s.h, m_s.base + (2 * ch + hl + 1))
                            e.wait_ge(t16.h, t16.base + (ch * 33 + 33))
                            if gp >= 2:
                                e.wait_ge(yev.h, yev.base + (gp - 1))
                            e.matmul(psY[gp % 2][64 * (r % 2):64 * (r % 2) + 64, :],
                                     xst[:, (r // 2) * 128 + 64 * (r % 2): (r // 2) * 128 + 64 * (r % 2) + 64],
                                     mfin[:, (r % 32) * 128:(r % 32) * 128 + 128],
                                     start=True, stop=False,
                                     tile_position=(0, 64 * (r % 2)))
                            e.wait_ge(cwm.h, cwm.base + (2 * ch + hl + 1))
                            e.wait_ge(hstt.h, hstt.base + (ch * 64))
                            inst = e.matmul(psY[gp % 2][64 * (r % 2):64 * (r % 2) + 64, :],
                                            h_all[:, r * 64:(r + 1) * 64],
                                            cwt[:, (r % 32) * 128:(r % 32) * 128 + 128],
                                            start=False, stop=True,
                                            tile_position=(0, 64 * (r % 2)))
                            y_s.inc(inst)
                        e.wait_ge(xsw_s.h, xsw_s.base + (g + 1))
                        if g >= DPH:
                            e.wait_ge(hstt.h, hstt.base + (g - (DPH - 1)))
                        inst = e.matmul(psHp[r % DPH][:, 0:64], btok[:, :], xsw[:, (r % 4) * 64:(r % 4) * 64 + 64],
                                        start=True, stop=True)
                        psh.inc(inst)

            @B.scalar
            def _(e):
                if full:
                    e.wait_ge(s_gath.h, s_gath.base + (2))
                for ch in range(NCH):
                    for j in range(33):
                        gt = ch * 33 + j
                        e.wait_ge(tr16.h, tr16.base + (gt + 1))
                        if j < 32:
                            dst = xst[:, j * 128:(j + 1) * 128]
                        else:
                            dst = btok[:, :]
                        if full and gt >= 1 and j == 0:
                            e.wait_ge(psh.h, psh.base + (ch * 64))
                            if full:
                                e.wait_ge(y_s.h, y_s.base + (ch * 64))
                        if (not full) and gt >= 1 and j == 0:
                            e.wait_ge(psh.h, psh.base + (ch * 64))
                        inst = e.activation(out=dst, in_=tp16[gt % 2][:, :], func=AF.Copy)
                        t16.inc(inst)
                    e.wait_ge(wsc0.h, wsc0.base + (ch + 1))
                    inst = e.activation(out=wsc_t[:, :], in_=wsc_t[:, :], func=AF.Exp, scale=-1.0)
                    wsc1.inc(inst)
                    dsts = ([dtT, lcTn] if full else []) + [wT]
                    if ch >= 1:
                        e.wait_ge(xsw_s.h, xsw_s.base + (ch * 64))
                        if full:
                            e.wait_ge(cmbL.h, cmbL.base + (ch))
                            e.wait_ge(cmb.h, cmb.base + (ch))
                    for jj, dstv in enumerate(dsts):
                        g32 = ch * n32 + jj
                        e.wait_ge(tr32.h, tr32.base + (g32 + 1))
                        if full and dstv is lcTn:
                            inst = e.activation(out=dstv[:, 0:64], in_=tp32[:, 0:64], func=AF.Copy,
                                                scale=-1.0)
                        else:
                            inst = e.activation(out=dstv[:, 0:64], in_=tp32[:, 0:64], func=AF.Copy)
                        t32.inc(inst)
                    if full:
                        e.wait_ge(t32.h, t32.base + (ch * 3 + 1))
                        if ch >= 1:
                            e.wait_ge(mbx.h, mbx.base + (64 * ch))
                        inst = e.activation(out=combT[:, 0:64], in_=dtT[:, 0:64], func=AF.Ln)
                        cmbL.inc(inst)
                        e.wait_ge(sbctmm.h, sbctmm.base + (ch + 1))
                        inst = e.activation(out=sbct[:, :], in_=sbct_ps[:, :], func=AF.Copy)
                        sbcte.inc(inst)
                        for hl in range(2):
                            g2 = 2 * ch + hl
                            lcb_st.wait(e, g2)
                            e.wait_ge(m_s.h, m_s.base + (g2))
                            inst = e.activation(out=mbig[:, :], in_=lcb[:, :], func=AF.Exp)
                            cwx.inc(inst)
                            e.wait_ge(cwx.h, cwx.n)
                            if ch >= 1:
                                e.wait_ge(hstt.h, hstt.base + ((ch - 1)) * 64 + hl * 32 + 32)
                            ecv = bass.AP(mbig.ap().tensor, 127,
                                          [[4096, 128], [128, 32]])
                            inst = e.activation(out=cumq[:, hl * 32:(hl + 1) * 32], in_=ecv,
                                                func=AF.Copy)
                            cq.inc(inst)
                            e.wait_ge(cq.h, cq.n)
                            e.wait_ge(cwm.h, cwm.base + (g2 + 1))
                            e.wait_ge(msk.h, msk.base + (g2 + 1))
                            e.wait_ge(cmb.h, cmb.base + (ch + 1))
                            for r in range(hl * 32, hl * 32 + 32):
                                inst = e.activation(out=mbig[:, (r % 32) * 128:(r % 32) * 128 + 128],
                                                    in_=lcb[:, (r % 32) * 128:(r % 32) * 128 + 128],
                                                    func=AF.Exp, bias=combT[:, r:r + 1])
                                mbx.inc(inst)
                    else:
                        cqb_st.wait(e, ch)
                        inst = e.activation(out=cumq[:, 0:64], in_=cumq[:, 0:64], func=AF.Exp)
                        cq.inc(inst)
                        for r in range(64):
                            g = ch * 64 + r
                            if g >= 4:
                                e.wait_ge(psh.h, psh.base + (g - 3))
                            inst = e.activation(
                                out=xsw[:, (r % 4) * 64:(r % 4) * 64 + 64],
                                in_=xst[:, (r // 2) * 128 + 64 * (r % 2): (r // 2) * 128 + 64 * (r % 2) + 64],
                                func=AF.Copy, scale=wT[:, r:r + 1])
                            xsw_s.inc(inst)

            if full:
                @B.gpsimd
                def _(e):
                    e.wait_ge(s_gath.h, s_gath.base + (2))
                    for ch in range(NCH):
                        for hl in range(2):
                            g2 = 2 * ch + hl
                            e.wait_ge(cwx.h, cwx.base + (g2 + 1))
                            inst = e.affine_select(out=lcb[:, :], in_=lcb[:, :],
                                                   pattern=[[0, 32], [1, 128]],
                                                   compare_op=ALU.is_ge,
                                                   fill=-1.0e30, base=0,
                                                   channel_multiplier=-1)
                            msk.inc(inst)

            @B.vector
            def _(e):
                if full:
                    e.wait_ge(s_gath.h, s_gath.base + (2))
                    e.wait_ge(s_cder2.h, s_cder2.base + (1))
                else:
                    e.wait_ge(s_hini.h, s_hini.base + (1))
                e.wait_ge(s_lcch.h, s_lcch.base + (8))
                for ch in range(NCH):
                    if ch >= 1:
                        e.wait_ge(tr32.h, tr32.base + (ch * n32))
                    fence(e, "dve", e.tensor_scalar(out=wsc_t[:, :], in0=lc_sb[:, ch * Q:(ch + 1) * Q],
                                                    scalar1=lc_sb[:, ch * Q + Q - 1: ch * Q + Q],
                                                    scalar2=None, op0=ALU.subtract))
                    inst = e.tensor_copy(out=tcol[0:64, 0:1], in_=wsc_t[:, 0:1])
                    wsc0.inc(inst)
                    e.wait_ge(wsc1.h, wsc1.base + (ch + 1))
                    inst = e.tensor_mul(out=wsc_t[:, :], in0=wsc_t[:, :], in1=dt_sb[:, ch * Q:(ch + 1) * Q])
                    wscv.inc(inst)
                    e.wait_ge(wscv.h, wscv.n)
                    if full:
                        e.wait_ge(cmbL.h, cmbL.base + (ch + 1))
                        e.wait_ge(t32.h, t32.base + (ch * 3 + 2))
                        inst = e.tensor_add(out=combT[:, 0:64], in0=combT[:, 0:64],
                                            in1=lcTn[:, 0:64])
                        cmb.inc(inst)
                    if full:
                        for hl in range(2):
                            g2 = 2 * ch + hl
                            e.wait_ge(cwx.h, cwx.base + (g2 + 1))
                            e.wait_ge(y_s.h, y_s.base + (32 * g2))
                            cbc = convo.ap()[:, 33 * T + ch * 128: 33 * T + ch * 128 + 128]
                            cbc = cbc.unsqueeze(1).broadcast_to([128, 32, 128])
                            inst = e.tensor_tensor(
                                out=cwt.ap().rearrange("p (a b) -> p a b", b=128),
                                in0=cbc,
                                in1=mbig.ap().rearrange("p (a b) -> p a b", b=128),
                                op=ALU.mult)
                            cwm.inc(inst)
                            e.wait_ge(mbx.h, mbx.base + (32 * (g2 + 1)))
                            e.wait_ge(sbcte.h, sbcte.base + (ch + 1))
                            inst = e.tensor_tensor(
                                out=mfin.ap().rearrange("p (a b) -> p a b", b=128),
                                in0=sbct.ap().unsqueeze(1).broadcast_to([128, 32, 128]),
                                in1=mbig.ap().rearrange("p (a b) -> p a b", b=128),
                                op=ALU.mult)
                            m_s.inc(inst)
                            e.wait_ge(m_s.h, m_s.n)
                            def h_y(rr):
                                gg = ch * 64 + rr
                                e.wait_ge(psh.h, psh.base + (gg + 1))
                                e.wait_ge(cq.h, cq.base + (2 * ch + hl + 1))
                                e.wait_ge(y_s.h, y_s.base + (gg + 1))
                                inst = e.scalar_tensor_tensor(out=h_all[:, rr * 64:(rr + 1) * 64],
                                                              in0=h_all[:, rr * 64:(rr + 1) * 64],
                                                              scalar=cumq[:, rr:rr + 1],
                                                              in1=psH[rr % 2][:, 0:64],
                                                              op0=ALU.mult, op1=ALU.add)
                                hstt.inc(inst)
                                if rr % 2 == 1:
                                    p = rr // 2
                                    gp = ch * 32 + p
                                    inst = e.scalar_tensor_tensor(
                                        out=yfeat[:, p * T + ch * 128: p * T + ch * 128 + 128],
                                        in0=convo[:, p * T + ch * 128: p * T + ch * 128 + 128],
                                        scalar=db_pair[:, p:p + 1],
                                        in1=psY[gp % 2][:, :],
                                        op0=ALU.mult, op1=ALU.add)
                                    yev.inc(inst)
                            for r in range(hl * 32, hl * 32 + 32):
                                g = ch * 64 + r
                                e.wait_ge(t16.h, t16.base + (ch * 33 + 33))
                                e.wait_ge(t32.h, t32.base + (ch * n32 + n32))
                                if g >= 4:
                                    e.wait_ge(psh.h, psh.base + (g - 3))
                                inst = e.tensor_scalar(
                                    out=xsw[:, (r % 4) * 64:(r % 4) * 64 + 64],
                                    in0=xst[:, (r // 2) * 128 + 64 * (r % 2): (r // 2) * 128 + 64 * (r % 2) + 64],
                                    scalar1=wT[:, r:r + 1], scalar2=None, op0=ALU.mult)
                                xsw_s.inc(inst)
                                if r - hl * 32 >= 2:
                                    h_y(r - 2)
                            for rr in range(hl * 32 + 30, hl * 32 + 32):
                                h_y(rr)
                    else:
                        def h_upd(rr):
                            gg = ch * 64 + rr
                            e.wait_ge(psh.h, psh.base + (gg + 1))
                            e.wait_ge(cq.h, cq.base + (ch + 1))
                            inst = e.scalar_tensor_tensor(out=h_all[:, rr * 64:(rr + 1) * 64],
                                                          in0=h_all[:, rr * 64:(rr + 1) * 64],
                                                          scalar=cumq[:, rr:rr + 1],
                                                          in1=psHp[rr % DPH][:, 0:64],
                                                          op0=ALU.mult, op1=ALU.add)
                            hstt.inc(inst)
                        for r in range(64):
                            h_upd(r)
            return locals()

        s_gath = S("s_gath")
        s_cc = S("s_cc")
        pub_st = Stream("pub_st", 2)
        bsl_st = Stream("bsl_st", 2)
        gw_st = Stream("gw_st", 4)
        s_gmm = S("s_gmm")
        s_gev = S("s_gev")
        s_hac = S("s_hac")

        quiesce_and_free("qld_0", "qld_1", "qst_0", "qst_1", "s_qrd", "s_qv", "zhst_0",
                         "zhst_1", "s_mmoc", "s_evoc", "s_tail", "s_dec", "s_dtc")
        z = emit_scan_pass("z_", False)

        @B.sync
        def _(e):
            e.wait_ge(sems["z_hstt"].h, sems["z_hstt"].base + 512)
            pub_st.put(0, e.dma_start(out=ccb_in.ap()[:, :], in_=h_all[:, :]))
            e.wait_ge(s_lcqa.h, s_lcqa.base + (8))
            pub_st.put(1, e.dma_start(out=cclc_in.ap().rearrange("a b -> (a b)")[:, None],
                                      in_=lcq_acc[:, 0:1]))

        @B.gpsimd
        def _(e):
            pub_st.wait(e, 0)
            pub_st.wait(e, 1)
            e.collective_compute("AllGather", ALU.bypass, replica_groups=[list(range(cc_group))],
                                 ins=[ccb_in.ap().opt()], outs=[ccb_out.ap()[0:cc_group * NST, :].opt()]).then_inc(s_cc.h, 1)
            s_cc.n += 1
            e.collective_compute("AllGather", ALU.bypass, replica_groups=[list(range(cc_group))],
                                 ins=[cclc_in.ap().opt()], outs=[cclc_out.ap()[0:cc_group, :].opt()]).then_inc(s_cc.h, 1)
            s_cc.n += 1

        @B.sync
        def _(e):
            e.wait_ge(s_cc.h, s_cc.base + (2))
            gw_st.put(0, e.dma_start(out=lcall[:, :], in_=cclc_out.ap()[:, :]))

        @B.tensor
        def _(e):
            gw_st.wait(e, 0)
            e.wait_ge(s_cmem.h, s_cmem.base + (1))
            inst = e.matmul(psY0[0:8, 0:64], tri8[:, :], lcall[:, :], start=True, stop=True)
            inst = e.matmul(psY1[0:1, 0:64], mask_sb[:, 0:1], lcall[:, :], start=True, stop=True)
            s_gmm.inc(inst)

        @B.scalar
        def _(e):
            e.wait_ge(s_gmm.h, s_gmm.base + (1))
            fence(e, "act", e.activation(out=lcq8[:, :], in_=psY0[0:8, 0:64], func=AF.Copy))
            inst = e.activation(out=lcp[0:1, :], in_=psY1[0:1, 0:64], func=AF.Copy)
            s_gev.inc(inst)

        @B.tensor
        def _(e):
            e.wait_ge(s_gev.h, s_gev.base + (1))
            inst = e.matmul(psH0[0:8, 0:64], ones8[:, :], lcp[0:1, :], start=True, stop=True)
            s_gmm.inc(inst)

        @B.scalar
        def _(e):
            e.wait_ge(s_gmm.h, s_gmm.base + (2))
            inst = e.activation(out=wcore[:, :], in_=psH0[0:8, 0:64], func=AF.Copy)
            s_gev.inc(inst)

        @B.vector
        def _(e):
            e.wait_ge(s_gev.h, s_gev.base + (2))
            fence(e, "dve", e.tensor_sub(out=wcore[:, :], in0=wcore[:, :], in1=lcq8[:, :]))
            inst = e.tensor_scalar(out=wcore[:, :], in0=wcore[:, :], scalar1=0.0, scalar2=None,
                                   op0=ALU.min)
            s_gev.inc(inst)

        @B.scalar
        def _(e):
            e.wait_ge(s_gev.h, s_gev.base + (3))
            inst = e.activation(out=wcore[:, :], in_=wcore[:, :], func=AF.Exp)
            s_gev.inc(inst)

        @B.vector
        def _(e):
            e.wait_ge(s_gev.h, s_gev.base + (4))
            inst = e.tensor_scalar(out=wcore[:, :], in0=wcore[:, :], scalar1=mask_sb[:, 0:1],
                                   scalar2=None, op0=ALU.mult)
            s_gev.inc(inst)

        @B.sync
        def _(e):
            e.wait_ge(s_gev.h, s_gev.base + (5))
            gw_st.put(1, e.dma_start(out=rowbc.ap()[6:7, 0:512], in_=wcore[:, :]))
            gw_st.wait(e, 1)
            gw_st.put(2, e.dma_start(out=wful[:, :],
                                     in_=rowbc.ap()[6, 0:512][None, :].partition_broadcast(128)))
            for cp in range(8):
                if cp >= 2:
                    e.wait_ge(s_hac.h, s_hac.base + (cp - 1))
                bsl_st.put(cp, e.dma_start(out=bslice[:, (cp % 2) * 4096:(cp % 2) * 4096 + 4096],
                                           in_=ccb_out.ap()[cp * 128:(cp + 1) * 128, :]))

        @B.vector
        def _(e):
            gw_st.wait(e, 2)
            for cp in range(8):
                bsl_st.wait(e, cp)
                if cp >= 1:
                    e.wait_ge(s_hac.h, s_hac.base + (cp))
                inst = None
                for r in range(64):
                    bsl = bslice[:, (cp % 2) * 4096 + r * 64:(cp % 2) * 4096 + r * 64 + 64]
                    if cp == 0:
                        inst = e.tensor_scalar(out=hacc32[:, r * 64:(r + 1) * 64], in0=bsl,
                                               scalar1=wful[:, cp * 64 + r:cp * 64 + r + 1],
                                               scalar2=None, op0=ALU.mult)
                    else:
                        inst = e.scalar_tensor_tensor(out=hacc32[:, r * 64:(r + 1) * 64], in0=bsl,
                                                      scalar=wful[:, cp * 64 + r:cp * 64 + r + 1],
                                                      in1=hacc32[:, r * 64:(r + 1) * 64],
                                                      op0=ALU.mult, op1=ALU.add)
                s_hac.inc(inst)

        @B.scalar
        def _(e):
            e.wait_ge(s_hac.h, s_hac.base + (8))
            inst = e.activation(out=h_all[:, :], in_=hacc32[:, :], func=AF.Copy)
            s_gath.inc(inst, 1)
            e.wait_ge(s_gath.h, s_gath.n)
            inst = e.activation(out=tcol[:, 0:1], in_=h_all[:, 0:1], func=AF.Copy)
            s_gath.inc(inst, 1)

        quiesce_and_free("z_tr16", "z_t16", "z_tr32", "z_t32", "z_wsc0", "z_wsc1", "z_wscv",
                         "z_cq", "z_xsw", "z_psh", "z_hstt", "z_cqb_0", "z_cqb_1",
                         "pub_st_0", "pub_st_1", "bsl_st_0", "bsl_st_1",
                         "gw_st_0", "gw_st_1", "gw_st_2", "gw_st_3", "s_gmm", "s_gev", "s_hac",
                         "s_cnvd", "s_cnvp", "s_lcqa", "lcrow_st_0", "lcrow_st_1", "lcrow_st_2",
                         "lcrow_st_3", "lcrow_st_4", "lcrow_st_5", "lcrow_st_6", "lcrow_st_7",
                         "cq_st_0", "cq_st_1", "cq_st_2", "cq_st_3", "cq_st_4", "cq_st_5",
                         "cq_st_6", "cq_st_7")
        u = emit_scan_pass("u_", True)

    tz = tap("zx", [DS, T], F16)
    if tz is not None:
        @B.gpsimd
        def _(e):
            for _s in zhst.sems:
                e.wait_ge(_s.h, _s.n)
            sd = S("s_dbg_zx")
            sd.dma(e.dma_start(out=tz[:, :], in_=z_hbm.ap()[:, :]))
            e.wait_ge(sd.h, sd.base + (16))

    txb = tap("xbc", [128, 34 * TH], F16)
    if txb is not None:
        @B.sync
        def _(e):
            e.wait_ge(s_evoc.h, s_evoc.base + (67))
            e.wait_ge(s_tail.h, s_tail.base + (34))
            sd = S("s_dbg_xbc")
            sd.dma(e.dma_start(out=txb[:, :], in_=xbc[:, :]))
            e.wait_ge(sd.h, sd.base + (16))

    tdt = tap("dtraw", [NH, T], F32)
    if tdt is not None:
        @B.sync
        def _(e):
            e.wait_ge(s_evoc.h, s_evoc.base + (67))
            sd = S("s_dbg_dtraw")
            sd.dma(e.dma_start(out=tdt[:, :], in_=dt_tmp[:, :]))
            e.wait_ge(sd.h, sd.base + (16))

    tq = tap("qg", [8, D8, H8], F16)
    if tq is not None:
        @B.gpsimd
        def _(e):
            for _s in qst.sems:
                e.wait_ge(_s.h, _s.n)
            sd = S("s_dbg_qg")
            sd.dma(e.dma_start(out=tq[:, :, :], in_=qg_hbm.ap()[:, :, :]))
            e.wait_ge(sd.h, sd.base + (16))

    t = tap("xr", [128, 32 * TH], F16)
    if t is not None:
        @B.sync
        def _(e):
            e.wait_ge(s_xr.h, s_xr.base + (32))
            sd = S("s_dbg_xr")
            sd.dma(e.dma_start(out=t[:, :], in_=xr[:, :]))
            e.wait_ge(sd.h, sd.base + (16))

    t2 = tap("xn", [H, T], F16)
    if t2 is not None:
        @B.gpsimd
        def _(e):
            for _s in xnh_st.sems:
                e.wait_ge(_s.h, _s.n)
            sd = S("s_dbg_xn")
            sd.dma(e.dma_start(out=t2[:, :], in_=xn_hbm.ap()[:, :]))
            e.wait_ge(sd.h, sd.base + (16))


    ty = tap("yfeat", [128, 32 * T], F16)
    if ty is not None:
        @B.sync
        def _(e):
            e.wait_ge(sems["u_yev"].h, sems["u_yev"].base + 256)
            sd = S("s_dbg_yfeat")
            sd.dma(e.dma_start(out=ty[:, :], in_=yfeat[:, :]))
            e.wait_ge(sd.h, sd.base + 16)


    quiesce_and_free("u_tr16", "u_t16", "u_tr32", "u_t32", "u_wsc0", "u_wsc1",
                     "u_wscv", "u_cq", "u_xsw", "u_psh", "u_hstt", "u_sbctmm",
                     "u_sbcte", "u_cmbL", "u_cmb", "u_cwx", "u_cwm", "u_bsub",
                     "u_mbx", "u_msk", "u_m",
                     "u_y", "u_yev", "u_lcb_0", "u_lcb_1", "s_gath", "s_hini",
                     "s_cnva", "s_lcch")
    zld = Stream("zld", 2)
    ynh = Stream("ynh", 2)
    s_sz = S("s_sz")
    s_ygv = S("s_ygv")
    s_sq = S("s_sq")
    s_sqm = S("s_sqm")
    s_rst = S("s_rst")
    s_ynv = S("s_ynv")

    with nc.psum_tensor("ps_ssq", [1, T], F32) as ps_ssq:

        @B.sync
        def _(e):
            for cc in range(32):
                if cc >= 2:
                    e.wait_ge(s_sz.h, s_sz.base + cc - 1)
                zld.put(cc, e.dma_start(out=zst[:, (cc % 2) * T:(cc % 2) * T + T],
                                        in_=z_hbm.ap()[cc * 128:(cc + 1) * 128, :]))

        @B.scalar
        def _(e):
            for cc in range(32):
                zld.wait(e, cc)
                if cc >= 2:
                    e.wait_ge(s_ygv.h, s_ygv.base + cc - 1)
                inst = e.activation(out=sgl[:, (cc % 2) * T:(cc % 2) * T + T],
                                    in_=zst[:, (cc % 2) * T:(cc % 2) * T + T], func=AF.Silu)
                s_sz.inc(inst)
                e.wait_ge(s_ygv.h, s_ygv.base + cc + 1)
                if cc >= 2:
                    e.wait_ge(s_sqm.h, s_sqm.base + cc - 1)
                inst = e.activation(out=sqb[:, (cc % 2) * T:(cc % 2) * T + T],
                                    in_=yg[:, cc * T:(cc + 1) * T], func=AF.Square)
                s_sq.inc(inst)

        @B.vector
        def _(e):
            for cc in range(32):
                e.wait_ge(s_sz.h, s_sz.base + cc + 1)
                inst = e.tensor_mul(out=yg[:, cc * T:(cc + 1) * T],
                                    in0=yfeat[:, cc * T:(cc + 1) * T],
                                    in1=sgl[:, (cc % 2) * T:(cc % 2) * T + T])
                s_ygv.inc(inst)
                e.wait_ge(s_ygv.h, s_ygv.n)

        @B.tensor
        def _(e):
            inst = None
            for cc in range(32):
                e.wait_ge(s_sq.h, s_sq.base + cc + 1)
                for (lo, n) in ((0, 512), (512, 512)):
                    inst = e.matmul(ps_ssq[:, lo:lo + n], ones16_col[:],
                                    sqb[:, (cc % 2) * T + lo:(cc % 2) * T + lo + n],
                                    start=(cc == 0), stop=(cc == 31))
                s_sqm.inc(inst)

        @B.scalar
        def _(e):
            e.wait_ge(s_sqm.h, s_sqm.base + 32)
            inst = e.activation(out=row_d[0:1, 0:T], in_=ps_ssq[:, :], func=AF.Sqrt,
                                scale=1.0 / DS, bias=EPS)
            s_rst.inc(inst)

        @B.vector
        def _(e):
            e.wait_ge(s_rst.h, s_rst.base + 1)
            inst = e.reciprocal(out=row_d[0:1, 0:T], in_=row_d[0:1, 0:T])
            s_rst.inc(inst)

        @B.sync
        def _(e):
            e.wait_ge(s_rst.h, s_rst.base + 2)
            s_rst.inc(e.dma_start(out=rowbc.ap()[7:8, 0:T], in_=row_d[0:1, 0:T]), 16)
            e.wait_ge(s_rst.h, s_rst.n)
            s_rst.inc(e.dma_start(out=rstd2_bc[:, :],
                                  in_=rowbc.ap()[7, 0:T][None, :].partition_broadcast(128)), 16)

        @B.vector
        def _(e):
            e.wait_ge(s_rst.h, s_rst.n)
            for cc in range(32):
                if cc >= 2:
                    ynh.wait(e, cc - 2)
                inst = e.scalar_tensor_tensor(out=ynstg[:, (cc % 2) * T:(cc % 2) * T + T],
                                              in0=yg[:, cc * T:(cc + 1) * T],
                                              scalar=nw_sb[:, cc:cc + 1],
                                              in1=rstd2_bc[:, :],
                                              op0=ALU.mult, op1=ALU.mult)
                s_ynv.inc(inst)
                e.wait_ge(s_ynv.h, s_ynv.n)

        @B.sync
        def _(e):
            for cc in range(32):
                e.wait_ge(s_ynv.h, s_ynv.base + cc + 1)
                ynh.put(cc, e.dma_start(out=yn_hbm.ap()[cc * 128:(cc + 1) * 128, :],
                                        in_=ynstg[:, (cc % 2) * T:(cc % 2) * T + T]))

    quiesce_and_free("zld_0", "zld_1", "s_sz", "s_ygv", "s_sq", "s_sqm", "s_rst",
                     "s_ynv", "ynh_0", "ynh_1")
    TQ = 256
    NMC = 4 * NOI * 8
    xnl = Stream("xnl", 1)
    s_win = [S("s_win0"), S("s_win1")]
    s_uwin = [S("s_uwin0", fresh=True), S("s_uwin1", fresh=True)]
    s_dwin = S("s_dwin")
    s_gu = S("s_gu")
    a_ev = S("a_ev")
    v_ev = S("v_ev")
    s_hq = S("s_hq")
    s_dmm = S("s_dmm")
    a_dv = S("a_dv")
    v_dv = S("v_dv")
    s_gd = S("s_gd")
    gst = Stream("gst", 2)

    def win_dma(e, st, dst, src_t, oi):
        ow, wd = OCH[oi]
        for j in range(8):
            for db in range(4):
                st.inc(e.dma_start(out=dst[:, (j * 4 + db) * wd:(j * 4 + db) * wd + wd],
                                   in_=src_t[j, db * 128:(db + 1) * 128, ow:ow + wd]), 16)

    def dwin_dma(e, st, ocw, par):
        for j in range(8):
            for dd in range(NOI):
                do, cnt = OCH[dd]
                st.inc(e.dma_start(out=qwin_d[par][0:cnt, (j * NOI + dd) * 128:(j * NOI + dd) * 128 + 128],
                                   in_=qd_hbm.ap()[j, do:do + cnt, ocw * 128:(ocw + 1) * 128]), 16)

    @B.sync
    def _(e):
        xnl.put(0, e.dma_start(
            out=xn_sb[:, :].rearrange("p (cc t) -> p cc t", t=T),
            in_=xn_hbm.ap().rearrange("(cc p) t -> p cc t", p=128)))
        for qt in range(4):
            for oi in range(NOI):
                gi = qt * NOI + oi
                if gi >= 2:
                    e.wait_ge(s_gu.h, s_gu.base + ((gi - 1) * 8))
                win_dma(e, s_win[gi % 2], qwin_g[gi % 2], qg_hbm.ap(), oi)

    @B.gpsimd
    def _(e):
        for qt in range(4):
            for oi in range(NOI):
                gi = qt * NOI + oi
                if gi >= 2:
                    e.wait_ge(s_gu.h, s_gu.base + ((gi - 1) * 8))
                win_dma(e, s_uwin[gi % 2], qwin_u[gi % 2], qu_hbm.ap(), oi)
            for ocw in range(4):
                di = qt * 4 + ocw
                if di >= 2:
                    e.wait_ge(s_dmm.h, s_dmm.base + ((di - 1) * 8))
                dwin_dma(e, s_dwin, ocw, di % 2)

    with nc.psum_tensor("ps_gp0", [128, TQ], F32) as ps_gp0, \
         nc.psum_tensor("ps_gp1", [128, TQ], F32) as ps_gp1, \
         nc.psum_tensor("ps_gn0", [128, TQ], F32) as ps_gn0, \
         nc.psum_tensor("ps_gn1", [128, TQ], F32) as ps_gn1, \
         nc.psum_tensor("ps_up0", [128, TQ], F32) as ps_up0, \
         nc.psum_tensor("ps_up1", [128, TQ], F32) as ps_up1, \
         nc.psum_tensor("ps_un0", [128, TQ], F32) as ps_un0, \
         nc.psum_tensor("ps_un1", [128, TQ], F32) as ps_un1:
        ps_gp = [ps_gp0, ps_gp1]
        ps_gn = [ps_gn0, ps_gn1]
        ps_up = [ps_up0, ps_up1]
        ps_un = [ps_un0, ps_un1]

        @B.tensor
        def _(e):
            xnl.wait(e, 0)
            for qt in range(4):
                if qt >= 1:
                    e.wait_ge(s_gd.h, s_gd.base + (qt * 32))
                for oi in range(NOI):
                    gi = qt * NOI + oi
                    ow, wd = OCH[oi]
                    e.wait_ge(s_win[gi % 2].h, s_win[gi % 2].base + 512 * (gi // 2 + 1))
                    e.wait_ge(s_uwin[gi % 2].h, s_uwin[gi % 2].base + 512 * (gi // 2 + 1))
                    for kg in range(8):
                        mi = gi * 8 + kg
                        par = mi % 2
                        if mi >= 2:
                            e.wait_ge(s_hq.h, s_hq.base + (mi - 1))
                        inst = None
                        for pj in range(2):
                            qs = (qwin_g, qwin_u)[pj]
                            pp = (ps_gp, ps_up)[pj][par]
                            pn = (ps_gn, ps_un)[pj][par]
                            for kc in range(32):
                                i = kc // 4
                                j = int(JTAB[i, kg])
                                db = kc % 4
                                pos = STAB[i, kg] > 0
                                lst = GU_POS[kg] if pos else GU_NEG[kg]
                                inst = e.matmul((pp if pos else pn)[0:wd, :],
                                                qs[gi % 2][:, (j * 4 + db) * wd:(j * 4 + db) * wd + wd],
                                                xn_sb[:, kc * T + qt * TQ: kc * T + qt * TQ + TQ],
                                                start=(kc == lst[0]), stop=(kc == lst[-1]))
                        s_gu.inc(inst)
                for ocw in range(4):
                    di = qt * 4 + ocw
                    e.wait_ge(s_dwin.h, s_dwin.base + 16 * 88 * (di + 1))
                    e.wait_ge(s_hq.h, s_hq.base + ((qt + 1) * NOI * 8))
                    for kg in range(8):
                        ni = di * 8 + kg
                        par = ni % 2
                        if ni >= 2:
                            e.wait_ge(s_gd.h, s_gd.base + (ni - 1))
                        inst = None
                        for i in range(8):
                            j = int(JTAB[i, kg])
                            pos = STAB[i, kg] > 0
                            lst = D_POS[kg] if pos else D_NEG[kg]
                            dst = (ps_gp if pos else ps_gn)[par]
                            for dd in range(NOI):
                                do, cnt = OCH[dd]
                                inst = e.matmul(dst[:, :],
                                                qwin_d[di % 2][0:cnt, (j * NOI + dd) * 128:(j * NOI + dd) * 128 + 128],
                                                h_q[0:cnt, (i * NOI + dd) * TQ:(i * NOI + dd) * TQ + TQ],
                                                start=(i == lst[0] and dd == 0),
                                                stop=(i == lst[-1] and dd == NOI - 1))
                        s_dmm.inc(inst)

        @B.scalar
        def _(e):
            e.wait_ge(s_cmem.h, s_cmem.base + 1)
            for qt in range(4):
                for oi in range(NOI):
                    ow, wd = OCH[oi]
                    for kg in range(8):
                        mi = (qt * NOI + oi) * 8 + kg
                        par = mi % 2
                        fb = par * 3 * TQ
                        e.wait_ge(s_gu.h, s_gu.base + (mi + 1))
                        if mi >= 2:
                            e.wait_ge(s_hq.h, s_hq.base + (mi - 1))
                        e.activation(out=fevac[0:wd, fb:fb + TQ], in_=ps_gn[par][0:wd, :],
                                     func=AF.Copy)
                        inst = e.activation(out=fevac[0:wd, fb + TQ:fb + 2 * TQ],
                                            in_=ps_un[par][0:wd, :], func=AF.Copy)
                        a_ev.inc(inst)
                        e.wait_ge(a_ev.h, a_ev.n)
                        e.wait_ge(v_ev.h, v_ev.base + (2 * mi + 1))
                        inst = e.activation(out=fevac[0:wd, fb + 2 * TQ:fb + 3 * TQ],
                                            in_=fevac[0:wd, fb:fb + TQ], func=AF.Silu,
                                            scale=sc_g[0:wd, 0:1])
                        a_ev.inc(inst)
                        e.wait_ge(a_ev.h, a_ev.n)
                for ocw in range(4):
                    for kg in range(8):
                        ni = (qt * 4 + ocw) * 8 + kg
                        par = ni % 2
                        fb = par * 3 * TQ
                        e.wait_ge(s_dmm.h, s_dmm.base + (ni + 1))
                        if ni >= 2:
                            e.wait_ge(s_gd.h, s_gd.base + (ni - 1))
                        inst = e.activation(out=fevac[:, fb:fb + TQ], in_=ps_gn[par][:, :],
                                            func=AF.Copy)
                        a_dv.inc(inst)
                        e.wait_ge(a_dv.h, a_dv.n)
                        e.wait_ge(v_dv.h, v_dv.base + (ni + 1))
                        if ni >= 2:
                            gst.wait(e, ni - 2)
                        inst = e.activation(out=gstg[:, par * TQ:par * TQ + TQ],
                                            in_=fevac[:, fb:fb + TQ], func=AF.Silu,
                                            scale=sc_d[:, 0:1])
                        s_gd.inc(inst)
                        e.wait_ge(s_gd.h, s_gd.n)
                        orow = kg * 512 + ocw * 128
                        gst.put(ni, e.dma_start(
                            out=g_hbm.ap()[orow:orow + 128, qt * TQ:(qt + 1) * TQ],
                            in_=gstg[:, par * TQ:par * TQ + TQ]))

        @B.vector
        def _(e):
            for qt in range(4):
                if qt >= 1:
                    e.wait_ge(s_dmm.h, s_dmm.base + (qt * 32))
                for oi in range(NOI):
                    ow, wd = OCH[oi]
                    for kg in range(8):
                        mi = (qt * NOI + oi) * 8 + kg
                        par = mi % 2
                        fb = par * 3 * TQ
                        e.wait_ge(a_ev.h, a_ev.base + (2 * mi + 1))
                        inst = e.tensor_sub(out=fevac[0:wd, fb:fb + TQ],
                                            in0=ps_gp[par][0:wd, :],
                                            in1=fevac[0:wd, fb:fb + TQ])
                        v_ev.inc(inst, 1)
                        e.wait_ge(v_ev.h, v_ev.n)
                        inst = e.tensor_sub(out=fevac[0:wd, fb + TQ:fb + 2 * TQ],
                                            in0=ps_up[par][0:wd, :],
                                            in1=fevac[0:wd, fb + TQ:fb + 2 * TQ])
                        v_ev.inc(inst, 1)
                        e.wait_ge(v_ev.h, v_ev.n)
                        e.wait_ge(a_ev.h, a_ev.base + (2 * mi + 2))
                        inst = e.scalar_tensor_tensor(
                            out=h_q[0:wd, ((kg * NOI + oi)) * TQ:((kg * NOI + oi)) * TQ + TQ],
                            in0=fevac[0:wd, fb + TQ:fb + 2 * TQ],
                            scalar=sc_u[0:wd, 0:1],
                            in1=fevac[0:wd, fb + 2 * TQ:fb + 3 * TQ],
                            op0=ALU.mult, op1=ALU.mult)
                        s_hq.inc(inst)
                        e.wait_ge(s_hq.h, s_hq.n)
                for ocw in range(4):
                    for kg in range(8):
                        ni = (qt * 4 + ocw) * 8 + kg
                        par = ni % 2
                        fb = par * 3 * TQ
                        e.wait_ge(a_dv.h, a_dv.base + (ni + 1))
                        inst = e.tensor_sub(out=fevac[:, fb:fb + TQ], in0=ps_gp[par][:, :],
                                            in1=fevac[:, fb:fb + TQ])
                        v_dv.inc(inst)
                        e.wait_ge(v_dv.h, v_dv.n)

    quiesce_and_free("xnl_0", "s_win0", "s_win1", "s_dwin",
                     "s_gu", "a_ev", "v_ev",
                     "s_hq", "s_dmm", "a_dv", "v_dv", "s_gd", "gst_0", "gst_1")
    ynl = Stream("ynl", 1)
    wo_st = Stream("wo_st", 3, 16, fresh=True)
    gld = Stream("gld", 2)
    xld = Stream("xld", 2)
    ost = Stream("ost", 2)
    s_omm = S("s_omm")
    s_ofin = S("s_ofin")

    @B.sync
    def _(e):
        ynl.put(0, e.dma_start(
            out=yn[:, :].rearrange("p (cc t) -> p cc t", t=T),
            in_=yn_hbm.ap().rearrange("(cc p) t -> p cc t", p=128)))
        for oc in range(32):
            if oc >= 2:
                e.wait_ge(s_ofin.h, s_ofin.base + (2 * (oc - 1)))
            gld.put(oc, e.dma_start(out=gbuf[:, (oc % 2) * T:(oc % 2) * T + T],
                                    in_=g_hbm.ap()[oc * 128:(oc + 1) * 128, :]))
            xld.put(oc, e.dma_start(out=xin6[:, (oc % 2) * T:(oc % 2) * T + T],
                                    in_=xT[oc * 128:(oc + 1) * 128, 3:TH]))

    @B.gpsimd
    def _(e):
        for oc in range(32):
            if oc >= 3:
                e.wait_ge(s_omm.h, s_omm.base + (oc - 2))
            dst = wslab6[:, (oc % 3) * 4096:(oc % 3) * 4096 + 4096].rearrange(
                "p (k c) -> p k c", c=128)
            srcv = w_out[:, oc * 128:(oc + 1) * 128].rearrange("(k p) c -> p k c", p=128)
            wo_st.put(oc, e.dma_start(out=dst, in_=srcv))

    with nc.psum_tensor("ps_o0", [128, T], F32) as ps_o0, \
         nc.psum_tensor("ps_o1", [128, T], F32) as ps_o1, \
         nc.psum_tensor("ps_o2", [128, T], F32) as ps_o2:
        ps_o = [ps_o0, ps_o1, ps_o2]

        @B.tensor
        def _(e):
            ynl.wait(e, 0)
            for oc in range(32):
                wo_st.wait(e, oc)
                if oc >= 3:
                    e.wait_ge(s_ofin.h, s_ofin.base + (2 * (oc - 2)))
                pst = ps_o[oc % 3]
                inst = None
                for k in range(32):
                    lhsT = wslab6[:, (oc % 3) * 4096 + k * 128:(oc % 3) * 4096 + (k + 1) * 128]
                    for (lo, n) in ((0, 512), (512, 512)):
                        inst = e.matmul(pst[:, lo:lo + n], lhsT,
                                        yn[:, k * T + lo:k * T + lo + n],
                                        start=(k == 0), stop=(k == 31))
                s_omm.inc(inst)

        @B.vector
        def _(e):
            for oc in range(32):
                e.wait_ge(s_omm.h, s_omm.base + (oc + 1))
                gld.wait(e, oc)
                xld.wait(e, oc)
                if oc >= 2:
                    ost.wait(e, oc - 2)
                pst = ps_o[oc % 3]
                stv = stg32[:, (oc % 2) * T:(oc % 2) * T + T]
                inst = e.tensor_mul(out=stv, in0=pst[:, :],
                                    in1=gbuf[:, (oc % 2) * T:(oc % 2) * T + T])
                s_ofin.inc(inst)
                e.wait_ge(s_ofin.h, s_ofin.n)
                inst = e.tensor_add(out=stv, in0=stv,
                                    in1=xin6[:, (oc % 2) * T:(oc % 2) * T + T])
                s_ofin.inc(inst)
                e.wait_ge(s_ofin.h, s_ofin.n)

        @B.scalar
        def _(e):
            for oc in range(32):
                e.wait_ge(s_ofin.h, s_ofin.base + (2 * oc + 2))
                ost.put(oc, e.dma_start(out=outT[oc * 128:(oc + 1) * 128, :],
                                        in_=stg32[:, (oc % 2) * T:(oc % 2) * T + T]))
            for _s in ost.sems:
                e.wait_ge(_s.h, _s.n)

    ctx.close()
    build._sems = dict(sems)
    build._pool = list(free_pool)
    return nc, dbg


_BUILT = {}

def _get_nc(debug_taps=(), cc_group=8):
    key = (tuple(sorted(debug_taps)), cc_group)
    if key not in _BUILT:
        _BUILT[key] = build(debug_taps, cc_group)
    return _BUILT[key]


def _prep_inputs(L0_attn_hidden_states, L0_attn_residual, L0_attn_ln_w, L0_attn_in_proj_w,
                 L0_attn_conv_w, L0_attn_A_log, L0_attn_dt_bias, L0_attn_D, L0_attn_norm_w,
                 L0_attn_out_proj_w, L0_attn_mup_vector, L0_ffn_w_gate, L0_ffn_w_up,
                 L0_ffn_w_down):
    x = np.ascontiguousarray(np.asarray(L0_attn_hidden_states, dtype=np.float32)[0])
    xT_full = np.ascontiguousarray(x.T)
    common = dict(
        w_in=np.ascontiguousarray(np.asarray(L0_attn_in_proj_w, np.float32)),
        w_out=np.ascontiguousarray(np.asarray(L0_attn_out_proj_w, np.float32)),
        wg_t=np.ascontiguousarray(np.asarray(L0_ffn_w_gate, np.float32).transpose(0, 2, 1)),
        wu_t=np.ascontiguousarray(np.asarray(L0_ffn_w_up, np.float32).transpose(0, 2, 1)),
        wd_t=np.ascontiguousarray(np.asarray(L0_ffn_w_down, np.float32).transpose(0, 2, 1)),
        convw=np.ascontiguousarray(np.asarray(L0_attn_conv_w, np.float32)),
        ln_w=np.ascontiguousarray(np.asarray(L0_attn_ln_w, np.float32)),
        norm_w=np.ascontiguousarray(np.asarray(L0_attn_norm_w, np.float32)),
        mup=np.ascontiguousarray(np.asarray(L0_attn_mup_vector, np.float32)),
        a_log=np.ascontiguousarray(np.asarray(L0_attn_A_log, np.float32)),
        dt_bias=np.ascontiguousarray(np.asarray(L0_attn_dt_bias, np.float32)),
        d_vec=np.ascontiguousarray(np.asarray(L0_attn_D, np.float32)),
    )
    in_maps = []
    for c in range(NCORE):
        xs = np.zeros((H, TH), np.float32)
        lo = c * T
        xs[:, 3:] = xT_full[:, lo:lo + T]
        if c > 0:
            xs[:, 0:3] = xT_full[:, lo - 3:lo]
        m = dict(common)
        m["xT"] = xs
        m["maskv"] = (np.arange(8) < c).astype(np.float32)
        in_maps.append(m)
    return in_maps


def kernel(**inputs):
    in_maps = _prep_inputs(**inputs)
    nc, _ = _get_nc()
    res = run_bass_kernel_spmd(nc, in_maps, core_ids=list(range(NCORE)))
    parts = [res.results[c]["outT"].T for c in range(NCORE)]
    out = np.concatenate(parts, axis=0)[None]
    return out.astype(np.float32)
